# version 10
# baseline (speedup 1.0000x reference)
import numpy as np

import concourse.bass as bass
import concourse.mybir as mybir
from concourse.bass_utils import run_bass_kernel_spmd

F32 = mybir.dt.float32
F32R = mybir.dt.float32r
BF16 = mybir.dt.bfloat16
AF = mybir.ActivationFunctionType
ALU = mybir.AluOpType
AX = mybir.AxisListType

D = 1024
H = 16
HD = 64
NA = 4
EPS_ATTN = 1e-6
EPS_LN = 1e-5
N_CORES = 8

WNAMES = ["a_wq", "a_wk", "a_wv", "a_wo", "a_bq", "a_bk", "a_bv", "a_bo",
          "f_wq", "f_wk", "f_wv", "f_wo", "f_bq", "f_bk", "f_bv", "f_bo",
          "f_wr", "f_br"]


def _ap(t, offset, dims):
    return bass.AP(tensor=t[:].tensor, offset=offset, ap=[list(d) for d in dims])


class Prog:
    def __init__(self, S=2048, NB=2, debug=(), stop_after=None, p2_level=9,
                 p2y_level=9, p45_level=9, p45_qc=99):
        assert S % 1024 == 0
        self.stop_after = stop_after
        self.p2_level = p2_level
        self.p2y_level = p2y_level
        self.p45_level = p45_level
        self.p45_qc = p45_qc
        self.S, self.NB = S, NB
        self.NT = S // 128
        self.NG = self.NT // 4
        self.debug = set(debug)
        nc = bass.Bass("TRN2", target_bir_lowering=False)
        self.nc = nc

        dt = nc.dram_tensor
        self.x_h = dt("x", [NB, S, D], F32, kind="ExternalInput")
        self.out_h = dt("out", [NB, S, D], F32, kind="ExternalOutput")
        shp = {"a_wq": [D, D], "a_wk": [D, D], "a_wv": [D, D], "a_wo": [D, D],
               "a_bq": [D], "a_bk": [D], "a_bv": [D], "a_bo": [D],
               "f_wq": [D, D], "f_wk": [D, D], "f_wv": [D, D],
               "f_wo": [NA * HD, D], "f_bq": [D], "f_bk": [D], "f_bv": [D],
               "f_bo": [D], "f_wr": [D, H], "f_br": [H]}
        self.w = {n: dt(n, shp[n], F32, kind="ExternalInput") for n in WNAMES}
        self.wo1r_h = dt("wo1r", [D, D], F32R, kind="Internal")
        self.dbg = {}
        for name, s in [("d_qm", [NB, 128, 8, 512]), ("d_ynt", [NB, 128, 8, 512]),
                        ("d_xg", [NB, 8, D]), ("d_km", [NB, 128, D]),
                        ("d_vaug", [NB, 128, H * 65]), ("d_xsum", [NB, D]),
                        ("d_kv", [NB, 128, 16 * 65]),
                        ("d_x1", [NB, S, D]), ("d_r", [NB, H]),
                        ("d_sel", [NB, H, NA]),
                        ("d_qs", [NB, NA, 65, S]), ("d_ks", [NB, NA, 65, S]),
                        ("d_vs", [NB, S, NA * 65]), ("d_ot", [NB, NA, 64, S])]:
            if name in self.debug:
                self.dbg[name] = dt(name, s, F32, kind="ExternalOutput")

        it = nc.inline_tensor
        self.c_id = it(np.eye(128, dtype=np.float32), "c_id")
        self.c_ones2 = it(np.ones((2, 128), np.float32), "c_ones2")
        self.c_ones64 = it(np.ones((64, 1), np.float32), "c_ones64")
        o65 = np.ones((65, 1), np.float32); o65[64, 0] = 1e-30
        self.c_ones65 = it(o65, "c_ones65")
        self.c_ones128 = it(np.ones((128, 1), np.float32), "c_ones128")
        self.c_onesr64 = it(np.ones((1, 64), np.float32), "c_onesr64")
        self.c_u16 = it(np.triu(np.ones((H, H), np.float32), 1).copy(), "c_u16")
        lpar = np.zeros((H, 128), np.float32)
        for h in range(H):
            lpar[h, (h % 2) * 64:(h % 2) * 64 + 64] = 1.0
        self.c_lpar = it(lpar, "c_lpar")
        selmask = np.zeros((H, 32), np.float32)
        for h in range(H):
            selmask[h, (h // 2) * 4:(h // 2) * 4 + 4] = 1.0
        self.c_selmask = it(selmask, "c_selmask")
        self.c_i2 = it(np.vstack([np.eye(64, dtype=np.float32)] * 2).copy(), "c_i2")

        self._tr_rot = 0
        self._sbuf()
        self._sems()
        self._emit()

    def _sbuf(self):
        nc = self.nc
        S, NT = self.S, self.NT
        a = nc.alloc_sbuf_tensor

        def at(name, shape, dtp, off):
            return nc.alloc_sbuf_tensor_at(name, shape, dtp, offset=off)

        self.id_f = a("id_f", [128, 128], F32)
        self.id_bf = a("id_bf", [128, 128], BF16)
        self.ones2_bf = a("ones2_bf", [2, 128], BF16)
        self.ones64_bf = a("ones64_bf", [64, 1], BF16)
        self.ones65_bf = a("ones65_bf", [65, 1], BF16)
        self.onesr64_r = a("onesr64_r", [1, 64], F32R)
        self.ones128_f = a("ones128_f", [128, 1], F32)
        self.lpar_f = a("lpar_f", [H, 128], F32)
        self.selmask_f = a("selmask_f", [H, 32], F32)
        self.selp = a("selp", [H, 32], F32)
        self.i2_f = a("i2_f", [128, 64], F32)
        self.u16_f = a("u16_f", [H, H], F32)

        self.x1 = a("x1", [128, NT, D], F32)
        self.Vs = a("Vs", [128, NT, NA * 65], BF16)
        self.kv_z = a("kv_z", [128, H, 65], BF16)
        self.esc = a("esc", [128, 512], F32)
        self.esc2 = a("esc2", [128, 512], F32)
        self.sums = a("sums", [128, 10], F32)
        self.stat = a("stat", [128, 6], F32)
        self.wr_f = a("wr_f", [128, 8, H], F32)
        self.xsum = a("xsum", [1, D], F32)
        self.xsum_fm = a("xsum_fm", [128, 8], F32)
        self.nrm_f = a("nrm_f", [1, D], F32)

        self.bq1_fm = a("bq1_fm", [128, 8], F32)
        self.bq2_fm = a("bq2_fm", [128, 8], F32)
        self.bk2_fm = a("bk2_fm", [128, 8], F32)
        self.bv2_fm = a("bv2_fm", [128, 8], F32)
        self.bk1_hl = a("bk1_hl", [2, D], BF16)
        self.bv1_hl = a("bv1_hl", [2, D], BF16)
        self.bo1_hl = a("bo1_hl", [2, D], BF16)
        self.bo2_hl = a("bo2_hl", [2, D], BF16)
        self.br_row = a("br_row", [1, H], F32)

        self.r_row = a("r_row", [1, H], F32)
        self.vmax8 = a("vmax8", [1, 8], F32)
        self.active = a("active", [1, H], F32)
        self.activeT = a("activeT", [H, 1], F32)
        self.sel = a("sel", [H, NA], F32)
        self.kmax_row = a("kmax_row", [1, NA], F32)

        rqk = nc.bump_sbuf(NA * 2 * S * 2)[0]
        self.Qs = [at(f"Qs{i}", [65, S], BF16, rqk + i * (S * 2)) for i in range(NA)]
        self.Ks = [at(f"Ks{i}", [65, S], BF16, rqk + (NA + i) * (S * 2))
                   for i in range(NA)]

        ph = nc.bump_sbuf(78336)[0]
        self.ph = ph
        self.wk1 = at("wk1", [128, 8, D], BF16, ph + 0)
        self.wv1 = at("wv1", [128, 8, D], BF16, ph + 16384)
        self.xin = [at(f"xin{i}", [128, D], F32, ph + 32768 + i * 4096)
                    for i in range(2)]
        self.xin_bf = [at(f"xinb{i}", [128, D], BF16, ph + 32768 + i * 2048)
                       for i in range(4)]
        self.km = [at(f"km{i}", [128, D], BF16,
                      ph + (40960 if i == 0 else 65664)) for i in range(2)]
        self.vaug = [at(f"vaug{i}", [128, H * 65], BF16,
                        ph + (43008 if i == 0 else 67712)) for i in range(2)]
        self.xt1 = [at(f"xt1_{i}", [128, 8, 128], BF16, ph + 45184 + i * 2048)
                    for i in range(2)]
        self.wq1 = at("wq1", [128, 8, D], BF16, ph + 49280)
        self.xTg = at("xTg", [128, 8, 512], BF16, ph + 0)
        self.qm = at("qm", [128, 8, 512], BF16, ph + 8192)
        self.ynTg = at("ynTg", [128, 8, 512], F32R, ph + 16384)
        self.wo1_ring = [at(f"wo1g{i}", [128, 512], F32R,
                            ph + (32768 + i * 2048 if i < 4 else
                                  69824 + (i - 4) * 2048)) for i in range(8)]
        self.ynorm = [at(f"ynorm{i}", [128, D], F32,
                         ph + (40960 if i == 0 else 45056)) for i in range(2)]
        self.wq2 = at("wq2", [128, 8, D], BF16, ph + 0)
        self.wk2 = at("wk2", [128, 8, D], BF16, ph + 16384)
        self.wv2 = at("wv2", [128, 8, D], BF16, ph + 32768)
        self.x1T = at("x1T", [128, 8, 512], BF16, ph + 49152)
        self.qkv_full = at("qkv_full", [128, 8, 512], BF16, ph + 57344)
        self.shiftb = at("shiftb", [128, 512], BF16, ph + 65536)
        self.vsfm = at("vsfm", [128, 512], BF16, ph + 66560)
        self.G = at("G", [128, 8, 256], BF16, ph + 67584)
        self.sqb = [at(f"sqb{i}", [65, S], BF16, ph + i * 4096)
                    for i in range(2)]
        self.nac2 = [at(f"nac2_{i}", [1, S], F32, ph + 8192 + i * 8192)
                     for i in range(2)]
        self.OTn = at("OTn", [64, NA, D], F32R, ph + 0)
        self.Pt = [at(f"Pt{i}", [128, D], BF16, ph + 16384 + i * 2048)
                   for i in range(2)]
        self.rep_sb = at("rep_sb", [64, D], F32, ph + 20480)
        self.rec_row = at("rec_row", [1, D], F32R, ph + 24576)
        self.wo2r = at("wo2r", [64, NA, D], F32R, ph + 28672)
        self.wk1n = at("wk1n", [128, 8, D], BF16, ph + 45056)
        self.wv1n = at("wv1n", [128, 8, D], BF16, ph + 61440)
        self.brow = at("brow", [1, D], F32, ph + 0)
        self.blo_f = at("blo_f", [1, D], F32, ph + 4096)

        p = nc.alloc_psum_tensor
        (self.psA, self.psB, self.psC, self.psD, self.psE, self.psF,
         self.psG, self.psH) = [p(f"ps{i}", [128, 512], F32) for i in range(8)]

    def _sems(self):
        self.sem, self.cnt = {}, {}
        for n in ["dW", "dX", "dX0", "dX1", "dX2", "dX3",
                  "dB0", "dB1", "dB2",
                  "dG0", "dG1", "dG2", "dG3", "dO0", "dO1",
                  "dR0", "dR1", "dR2", "dR3", "dR4", "dR5", "dR6", "dR7",
                  "T", "V", "Sc"]:
            self.sem[n] = self.nc.alloc_semaphore(n)
            self.cnt[n] = 0

    def inc(self, instr, name):
        amt = 16 if name.startswith("d") else 1
        instr.then_inc(self.sem[name], amt)
        self.cnt[name] += amt
        return self.cnt[name]

    def wait(self, eng, name, val=None):
        eng.wait_ge(self.sem[name], self.cnt[name] if val is None else val)

    def _emit(self):
        nc = self.nc
        sa = self.stop_after
        with nc.Block():
            self._startup()
            nc.all_engine_barrier()
            self._wload_b1(first=True)
            self._w1_done = self.cnt["dW"]
            nc.all_engine_barrier()
            for b in range(self.NB):
                if sa == "startup":
                    break
                self._phase1(b)
                nc.all_engine_barrier()
                if sa == "p1":
                    break
                self._phase2(b)
                nc.all_engine_barrier()
                if sa == "p2":
                    break
                self._router(b)
                self._wload_b2()
                nc.all_engine_barrier()
                if sa == "router":
                    break
                self._phase3(b)
                nc.all_engine_barrier()
                if sa == "p3":
                    break
                self._wload_wo2()
                self._bounds(b)
                nc.all_engine_barrier()
                if sa == "bounds":
                    break
                self._phase45(b, last=(b == self.NB - 1))
                nc.all_engine_barrier()
                if b + 1 < self.NB:
                    gp = nc.gpsimd
                    self.inc(gp.dma_start(out=self.wk1[:], in_=self.wk1n[:]), "dW")
                    self.inc(gp.dma_start(out=self.wv1[:], in_=self.wv1n[:]), "dW")
                    gp.wait_ge(self.sem["dW"], self.cnt["dW"])
                    self._wload_wq1()
                    self._w1_done = self.cnt["dW"]
                    nc.all_engine_barrier()
            for s in ["dX", "dX0", "dX1", "dX2", "dX3", "dO0", "dO1",
                      "dR0", "dR1", "dR2",
                      "dR3", "dR4", "dR5", "dR6", "dR7"]:
                self.wait(nc.sync, s)
            for s in ["dG0", "dG1", "dG2", "dG3", "dB0", "dB1", "dB2"]:
                self.wait(nc.sync, s)
            self.wait(nc.gpsimd, "dW")

    def _startup(self):
        nc = self.nc
        sy, gp, V = nc.sync, nc.gpsimd, nc.vector
        w = self.w
        for dst, src in [(self.id_f, self.c_id), (self.ones128_f, self.c_ones128),
                         (self.lpar_f, self.c_lpar),
                         (self.selmask_f, self.c_selmask),
                         (self.i2_f, self.c_i2), (self.u16_f, self.c_u16)]:
            self.inc(sy.dma_start(out=dst[:], in_=src.ap()), "dX")
        for dst, src in [(self.id_bf, self.c_id), (self.ones2_bf, self.c_ones2),
                         (self.ones64_bf, self.c_ones64),
                         (self.ones65_bf, self.c_ones65),
                         (self.onesr64_r, self.c_onesr64)]:
            self.inc(gp.dma_start(out=dst[:], in_=src.ap()), "dW")
        self.inc(gp.dma_start(out=self.wo1r_h.ap(), in_=w["a_wo"].ap()), "dW")
        self.inc(sy.dma_start(
            out=self.wr_f[:],
            in_=w["f_wr"].ap().rearrange("(p c) h -> p c h", p=128)), "dX")
        for src, dst in [("a_bq", self.bq1_fm), ("f_bq", self.bq2_fm),
                         ("f_bk", self.bk2_fm), ("f_bv", self.bv2_fm)]:
            self.inc(sy.dma_start(
                out=dst[:], in_=w[src].ap().rearrange("(c p) -> p c", p=128),
                allow_slow_non_contiguous=True), "dX")
        self.inc(sy.dma_start(out=self.br_row[:], in_=w["f_br"].ap()[None, :]),
                 "dX")
        for src, dst in [("a_bk", self.bk1_hl), ("a_bv", self.bv1_hl),
                         ("a_bo", self.bo1_hl), ("f_bo", self.bo2_hl)]:
            self.inc(sy.dma_start(out=self.brow[:], in_=w[src].ap()[None, :]),
                     "dX")
            self.wait(V, "dX")
            self.wait(V, "dW")
            V.tensor_copy(dst[0:1, :], self.brow[:])
            V.drain()
            iv = self.inc(V.tensor_sub(self.blo_f[:], self.brow[:], dst[0:1, :]),
                          "V")
            self.wait(gp, "V", iv)
            self.inc(gp.dma_start(out=dst[1:2, :], in_=self.blo_f[:]), "dW")
            sy.wait_ge(self.sem["V"], iv)
        self.wait(V, "dW")
        V.memset(_ap(self.Vs, 64, [[self.NT * NA * 65, 128], [65, self.NT * NA]]),
                 1.0)
        V.memset(self.sums[:], 0.0)
        self.inc(V.memset(self.stat[:], 0.0), "V")

    def _wload_b1(self, first=False):
        gp = self.nc.gpsimd
        for src, dst in [("a_wk", self.wk1 if first else self.wk1n),
                         ("a_wv", self.wv1 if first else self.wv1n)]:
            self.inc(gp.dma_start(
                out=dst[:],
                in_=self.w[src].ap().rearrange("(c p) f -> p c f", p=128)), "dW")
        if first:
            self._wload_wq1()

    def _wload_wq1(self):
        gp = self.nc.gpsimd
        self.inc(gp.dma_start(
            out=self.wq1[:],
            in_=self.w["a_wq"].ap().rearrange("(c p) f -> p c f", p=128)), "dW")

    def _wload_b2(self):
        gp = self.nc.gpsimd
        self._w2_done = {}
        for src, dst, kind, sname in [("f_wq", self.wq2, "q", "dB0"),
                                      ("f_wk", self.wk2, "k", "dB1"),
                                      ("f_wv", self.wv2, "v", "dB2")]:
            self.wait(gp, sname)
            cnt = self.inc(gp.dma_start(
                out=dst[:],
                in_=self.w[src].ap().rearrange("(c p) f -> p c f", p=128)), sname)
            self._w2_done[kind] = (sname, cnt)

    def _wload_wo2(self):
        gp = self.nc.gpsimd
        self.inc(gp.dma_start(
            out=self.wo2r[:],
            in_=self.w["f_wo"].ap().rearrange("(a d) f -> d a f", d=HD)), "dW")

    def _tr8(self, src_f32, evac_list, banks=None):
        nc = self.nc
        TE = nc.tensor
        if banks is None:
            banks = [self.psG, self.psH]
        if not hasattr(self, "_tr_evac"):
            self._tr_evac = {}
        for g in range(2):
            bi = (self._tr_rot + g) % len(banks)
            ps = banks[bi]
            self.wait(TE, "Sc", self._tr_evac.get(id(ps), 0))
            self.wait(TE, "V", getattr(self, "_psGH_vfree", 0))
            last = None
            for c4 in range(4):
                c = 4 * g + c4
                last = TE.transpose(ps[:, c4 * 128:(c4 + 1) * 128],
                                    src_f32[:, c * 128:(c + 1) * 128],
                                    self.id_f[:])
            it = self.inc(last, "T")
            self.nc.scalar.wait_ge(self.sem["T"], it)
            self._tr_evac[id(ps)] = evac_list(g, ps)
        self._tr_rot = (self._tr_rot + 2) % len(banks)

    def _phase1(self, b):
        nc = self.nc
        sy = nc.sync
        TE, SC, V = nc.tensor, nc.scalar, nc.vector
        NT = self.NT
        self.wait(TE, "dW", self._w1_done)
        V.memset(_ap(self.vaug[0], 64, [[H * 65, 128], [65, H]]), 1.0)
        V.memset(_ap(self.vaug[1], 64, [[H * 65, 128], [65, H]]), 1.0)
        self.inc(V.memset(self.sums[:, 7:8], 0.0), "V")
        v_done = self.cnt["V"]
        self._p1_vdone = {}
        gp = nc.gpsimd
        for t in range(NT):
            if t % 2 == 0:
                eng, sx = sy, "dX%d" % ((t // 2) % 4)
            else:
                eng, sx = gp, "dG%d" % ((t // 2) % 4)
            self.wait(eng, sx)
            self.wait(eng, "dO%d" % (t % 2))
            self.inc(eng.dma_start(
                out=self.x1[:, t, :],
                in_=self.x_h.ap()[b, t * 128:(t + 1) * 128, :]), sx)
            self.wait(TE, sx)
            self.wait(TE, "V", v_done)
            xt = self.xt1[t % 2]

            def evac(gg, ps, xt=xt):
                out = _ap(xt, 4 * gg * 128, [[1024, 128], [1, 512]])
                return self.inc(SC.activation(out, ps[:], AF.Copy), "Sc")

            self._tr8(self.x1[:, t, :], evac)
            self.wait(TE, "Sc")
            tmark = {}
            for ti, (wsb, bhl) in enumerate([(self.wk1, self.bk1_hl),
                                             (self.wv1, self.bv1_hl)]):
                for half in range(2):
                    ps = [self.psA, self.psB, self.psC, self.psD][2 * ti + half]
                    for c in range(8):
                        TE.matmul(ps[:], xt[:, c, :],
                                  wsb[:, c, half * 512:(half + 1) * 512],
                                  start=(c == 0), stop=False)
                    mm = TE.matmul(ps[:], self.ones2_bf[:],
                                   bhl[:, half * 512:(half + 1) * 512],
                                   start=False, stop=True)
                    tmark[(ti, half)] = self.inc(mm, "T")
            for half in range(2):
                ps = [self.psA, self.psB][half]
                self.wait(V, "T", tmark[(0, half)])
                V.tensor_scalar_min(self.esc[:], ps[:], 0.0)
                iv = self.inc(V.memset(self.sums[:, 7:8], 0.0), "V")
                self.wait(SC, "V", iv)
                isc = self.inc(SC.activation(self.esc2[:], self.esc[:], AF.Exp),
                               "Sc")
                self.wait(V, "Sc", isc)
                st = V.scalar_tensor_tensor(
                    self.km[t % 2][:, half * 512:(half + 1) * 512],
                    ps[:], 0.0, self.esc2[:], ALU.max, ALU.add)
                self.inc(st, "V")
            for half in range(2):
                ps = [self.psC, self.psD][half]
                self.wait(V, "T", tmark[(1, half)])
                out = _ap(self.vaug[t % 2], half * 8 * 65,
                          [[H * 65, 128], [65, 8], [1, 64]])
                self.inc(V.tensor_copy(out, ps[:]), "V")
            v_done = self.cnt["V"]
            self._p1_vdone[t] = v_done
            if t == 0 and "d_km" in self.dbg:
                gp = nc.gpsimd
                self.wait(gp, "V", v_done)
                self.inc(gp.dma_start(out=self.dbg["d_km"].ap()[b],
                                      in_=self.km[t % 2][:]), "dW")
                idm = self.inc(gp.dma_start(out=self.dbg["d_vaug"].ap()[b],
                                            in_=self.vaug[t % 2][:]), "dW")
                gp.wait_ge(self.sem["dW"], idm)
                self.wait(V, "dW", idm)
            if t > 0:
                self._p1_kv_accum(t - 1, NT)
        self._p1_kv_accum(NT - 1, NT)
        self.wait(V, "T")
        V.memset(self.kv_z[:], 0.0)
        V.drain()
        for bi, bank in enumerate([self.psE, self.psF]):
            oute = _ap(self.kv_z, (2 * 4 * bi) * 65,
                       [[H * 65, 64], [2 * 65, 4], [1, 65]])
            srce = _ap(bank, 0, [[512, 64], [128, 4], [1, 65]])
            V.tensor_copy(oute, srce)
            V.drain()
            oto = _ap(self.kv_z, (2 * 4 * bi + 1) * 65 + 64 * H * 65,
                      [[H * 65, 64], [2 * 65, 4], [1, 65]])
            srco = _ap(bank, 64 * 512, [[512, 64], [128, 4], [1, 65]])
            self.inc(V.tensor_copy(oto, srco), "V")
            V.drain()
        if "d_kv" in self.dbg:
            gp = nc.gpsimd
            self.wait(gp, "V")
            idm = self.inc(gp.dma_start(
                out=self.dbg["d_kv"].ap()[b],
                in_=self.kv_z[:].rearrange("p c k -> p (c k)")), "dW")
            gp.wait_ge(self.sem["dW"], idm)

    def _p1_kv_accum(self, t, NT):
        TE = self.nc.tensor
        self.wait(TE, "V", self._p1_vdone[t])
        last = None
        for h in range(H):
            bank = self.psE if h < 8 else self.psF
            slot = (h % 8) // 2
            bp = (h % 2) * 64
            last = TE.matmul(bank[bp:bp + 64, slot * 128:slot * 128 + 65],
                             self.km[t % 2][:, h * 64:(h + 1) * 64],
                             _ap(self.vaug[t % 2], h * 65,
                                 [[H * 65, 128], [1, 65]]),
                             start=(t == 0 and h % 8 < 2),
                             stop=(t == NT - 1), skip_group_check=True)
        self.inc(last, "T")

    def _phase2(self, b):
        nc = self.nc
        sy = nc.sync
        TE, SC, V = nc.tensor, nc.scalar, nc.vector
        NG = self.NG
        self.wait(TE, "dW", self._w1_done)
        self.qm_evac = {}
        self._ring_used = {}
        self._xsum_pending = None
        self._ynorm_v = {}
        self._ynT_done = {}
        V.memset(self.xsum[:], 0.0)
        self.inc(V.memset(self.sums[:], 0.0), "V")
        ring_cnt = 0
        for g in range(NG):
            for dt_ in range(4):
                t = g * 4 + dt_

                def evac(gg, ps, dt_=dt_):
                    out = _ap(self.xTg, (4 * gg) * 512 + dt_ * 128,
                              [[8 * 512, 128], [512, 4], [1, 128]])
                    return self.inc(SC.activation(out, ps[:], AF.Copy), "Sc")

                self._tr8(self.x1[:, t, :], evac)
            self.wait(TE, "Sc")
            self._emit_xsum(b)
            if self.p2_level < 2:
                continue
            for hc in range(8):
                ps = [self.psA, self.psB, self.psC, self.psD][hc % 4]
                if hc >= 4:
                    self.wait(TE, "V", self.qm_evac[hc - 4])
                mm = None
                for c in range(8):
                    mm = TE.matmul(ps[:], self.wq1[:, c, hc * 128:(hc + 1) * 128],
                                   self.xTg[:, c, :], start=(c == 0), stop=(c == 7))
                it = self.inc(mm, "T")
                self.wait(SC, "T", it)
                self.wait(SC, "V", self.qm_evac.get(hc - 1, 0))
                isc = self.inc(SC.activation(self.esc2[:], ps[:], AF.Relu,
                                             bias=self.bq1_fm[:, hc:hc + 1]), "Sc")
                self.wait(V, "Sc", isc)
                V.tensor_scalar(self.esc[:], ps[:], self.bq1_fm[:, hc:hc + 1],
                                None, ALU.add)
                V.drain()
                V.tensor_sub(self.esc[:], self.esc[:], self.esc2[:])
                iv = self.inc(V.memset(self.sums[:, 7:8], 0.0), "V")
                self.wait(SC, "V", iv)
                isc = self.inc(SC.activation(self.esc[:], self.esc[:], AF.Exp),
                               "Sc")
                self.wait(V, "Sc", isc)
                self.qm_evac[hc] = self.inc(
                    V.tensor_add(self.qm[:, hc, :], self.esc[:], self.esc2[:]),
                    "V")
            self.wait(TE, "V")
            if self.p2_level < 3:
                continue
            for dt_ in range(4):
                self._p2_y_tile(dt_)
                if dt_ > 0:
                    self._p2_ynt(dt_ - 1)
            self._p2_ynt(3)
            if self.p2_level < 4:
                continue
            if g == 0 and "d_qm" in self.dbg:
                gpq = nc.gpsimd
                self.wait(gpq, "V")
                self.wait(gpq, "Sc")
                self.inc(gpq.dma_start(out=self.dbg["d_qm"].ap()[b],
                                       in_=self.qm[:]), "dW")
                idm = self.inc(gpq.dma_start(out=self.dbg["d_ynt"].ap()[b],
                                             in_=self.ynTg[:]), "dW")
                gpq.wait_ge(self.sem["dW"], idm)
                self.wait(TE, "dW", idm)
            self.wait(TE, "Sc")
            for half in range(2):
                for c in range(8):
                    slot = ring_cnt % 8
                    sr = "dR%d" % slot
                    self.wait(sy, sr)
                    self.wait(sy, "T", self._ring_used.get(slot, 0))
                    self.inc(sy.dma_start(
                        out=self.wo1_ring[slot][:],
                        in_=self.wo1r_h.ap()[c * 128:(c + 1) * 128,
                                             half * 512:(half + 1) * 512]), sr)
                    ring_cnt += 1
                    self.wait(TE, sr)
                    if c == 0:
                        self.wait(TE, "V")
                    for dt_ in range(4):
                        psO = [self.psA, self.psB, self.psC, self.psD][dt_]
                        mm = TE.matmul(
                            psO[:],
                            _ap(self.ynTg, (dt_ % 2) * 64 + c * 512,
                                [[8 * 512, 128], [1, 64]]).bitcast(F32R)
                            if False else
                            self.ynTg[:, c, dt_ * 128:(dt_ + 1) * 128],
                            self.wo1_ring[slot][:],
                            start=(c == 0), stop=False)
                        if c == 7:
                            mm = TE.matmul(psO[:], self.ones2_bf[:],
                                           self.bo1_hl[:, half * 512:(half + 1) * 512],
                                           start=False, stop=True)
                    if c == 7:
                        self.inc(mm, "T")
                    else:
                        self.inc(mm, "T")
                    self._ring_used[slot] = self.cnt["T"]
                if self.p2_level < 5:
                    continue
                self.wait(V, "T")
                for dt_ in range(4):
                    t = g * 4 + dt_
                    psO = [self.psA, self.psB, self.psC, self.psD][dt_]
                    st = V.scalar_tensor_tensor(
                        self.x1[:, t, half * 512:(half + 1) * 512], psO[:], 0.0,
                        self.x1[:, t, half * 512:(half + 1) * 512],
                        ALU.add, ALU.add,
                        accum_out=self.sums[:, 2 * dt_ + half:2 * dt_ + half + 1])
                self.inc(V.memset(self.sums[:, 8:9], 0.0), "V")
                V.drain()
            if self.p2_level < 6:
                continue
            for dt_ in range(4):
                t = g * 4 + dt_
                self._ln_finalize(self.x1[:, t, :],
                                  self.sums[:, 2 * dt_:2 * dt_ + 1],
                                  self.sums[:, 2 * dt_ + 1:2 * dt_ + 2],
                                  self.x1[:, t, :])
            self._ln_done = self.cnt["V"]
            self._psGH_vfree = self.cnt["V"]
            self._xsum_pending = g
            if "d_xg" in self.dbg:
                gpx = nc.gpsimd
                self.wait(gpx, "V")
                idm = self.inc(gpx.dma_start(
                    out=self.dbg["d_xg"].ap()[b, g][None, :], in_=self.xsum[:]),
                    "dW")
                gpx.wait_ge(self.sem["dW"], idm)
                self.wait(V, "dW", idm)
        self._emit_xsum(b)
        if "d_x1" in self.dbg:
            self.wait(sy, "V")
            for t in range(self.NT):
                self.inc(sy.dma_start(
                    out=self.dbg["d_x1"].ap()[b, t * 128:(t + 1) * 128, :],
                    in_=self.x1[:, t, :]), "dX")
            self.wait(sy, "dX")

    def _emit_xsum(self, b):
        if getattr(self, "_xsum_pending", None) is None:
            return
        g = self._xsum_pending
        self._xsum_pending = None
        nc = self.nc
        TE, V = nc.tensor, nc.vector
        self.wait(TE, "V", self._ln_done)
        self.wait(TE, "V", getattr(self, "_xsum_vfree", 0))
        last = None
        for dt_ in range(4):
            t = g * 4 + dt_
            for half in range(2):
                ps = [self.psE, self.psF][half]
                last = TE.matmul(ps[0:1, :], self.ones128_f[:],
                                 self.x1[:, t, half * 512:(half + 1) * 512],
                                 start=(dt_ == 0), stop=(dt_ == 3),
                                 skip_group_check=True)
        it = self.inc(last, "T")
        self.wait(V, "T", it)
        for half in range(2):
            ps = [self.psE, self.psF][half]
            V.tensor_add(self.xsum[0:1, half * 512:(half + 1) * 512],
                         self.xsum[0:1, half * 512:(half + 1) * 512],
                         ps[0:1, :])
        V.drain()
        self._xsum_vfree = self.inc(V.memset(self.sums[:, 8:9], 0.0), "V")

    def _p2_y_tile(self, dt_):
        nc = self.nc
        TE, SC, V = nc.tensor, nc.scalar, nc.vector
        lvl = getattr(self, "p2y_level", 9)
        iv = None
        if not hasattr(self, "_y_pair_free"):
            self._y_pair_free = {}
        for hg in range(2):
            pid = (2 * dt_ + hg) % 2
            bpair = [[self.psC, self.psD], [self.psA, self.psB]][pid]
            self.wait(TE, "V", self._y_pair_free.get(pid, 0))
            last = None
            for hh in range(8):
                h = hg * 8 + hh
                bank = bpair[0] if hh < 4 else bpair[1]
                slot = hh % 4
                qm_lhs = self.qm[:, h // 2, dt_ * 128:(dt_ + 1) * 128]
                kv_rhs = self.kv_z[:, h, 0:65]
                last = TE.matmul(bank[:, slot * 128:slot * 128 + 65], qm_lhs,
                                 kv_rhs, start=True, stop=True)
            it = self.inc(last, "T")
            self.wait(V, "T", it)
            self.wait(V, "T", self._ynT_done.get(dt_ % 2, 0))
            for bi, bank in enumerate(bpair):
                h0 = hg * 8 + bi * 4
                if lvl < 2:
                    dst = _ap(self.ynorm, h0 * 64, [[1024, 128], [64, 4], [1, 64]])
                    src = _ap(bank, 0, [[512, 128], [128, 4], [1, 64]])
                    iv = self.inc(V.tensor_copy(dst, src), "V")
                    continue
                V.drain()
                V.tensor_scalar(self.sums[:, 4:8],
                                _ap(bank, 64, [[512, 128], [128, 4]]),
                                EPS_ATTN, None, ALU.add)
                V.drain()
                V.reciprocal(self.sums[:, 4:8], self.sums[:, 4:8])
                V.drain()
                dst = _ap(self.ynorm[dt_ % 2], h0 * 64,
                          [[1024, 128], [64, 4], [1, 64]])
                src = _ap(bank, 0, [[512, 128], [128, 4], [1, 64]])
                zb = _ap(self.sums, 4, [[10, 128], [1, 4], [0, 64]])
                iv = self.inc(V.tensor_tensor(dst, src, zb, ALU.mult), "V")
            self._y_pair_free[pid] = iv
        self._ynorm_v[dt_] = self.cnt["V"]

    def _p2_ynt(self, dt_):
        nc = self.nc
        TE, SC = nc.tensor, nc.scalar

        def evac(gg, ps, dt_=dt_):
            out = _ap(self.ynTg, (4 * gg) * 512 + dt_ * 128,
                      [[8 * 512, 128], [512, 4], [1, 128]])
            with nc.allow_low_precision(reason="f32r operand prep"):
                return self.inc(SC.activation(out, ps[:], AF.Copy), "Sc")

        self.wait(TE, "V", self._ynorm_v[dt_])
        self._tr8(self.ynorm[dt_ % 2][:], evac)
        self._ynT_done[dt_ % 2] = self.cnt["T"]

    def _ln_finalize(self, xpre, s0, s1, x_out):
        nc = self.nc
        SC, V = nc.scalar, nc.vector
        for half in range(2):
            V.scalar_tensor_tensor(
                [self.esc, self.esc2][half][:],
                xpre[:, half * 512:(half + 1) * 512], 1.0,
                xpre[:, half * 512:(half + 1) * 512], ALU.mult, ALU.mult,
                accum_out=self.sums[:, 8 + half:9 + half])
        V.drain()
        V.tensor_add(self.stat[:, 0:1], s0, s1)
        V.tensor_add(self.stat[:, 1:2], self.sums[:, 8:9], self.sums[:, 9:10])
        V.drain()
        V.tensor_scalar_mul(self.stat[:, 0:2], self.stat[:, 0:2], 1.0 / D)
        V.drain()
        V.scalar_tensor_tensor(self.stat[:, 2:3], self.stat[:, 0:1], 1.0,
                               self.stat[:, 0:1], ALU.mult, ALU.mult)
        V.drain()
        V.tensor_sub(self.stat[:, 1:2], self.stat[:, 1:2], self.stat[:, 2:3])
        V.drain()
        iv = self.inc(V.tensor_scalar_add(self.stat[:, 1:2], self.stat[:, 1:2],
                                          EPS_LN), "V")
        self.wait(SC, "V", iv)
        SC.activation(self.stat[:, 4:5], self.stat[:, 1:2], AF.Ln)
        SC.drain()
        isc = self.inc(SC.activation(self.stat[:, 2:3], self.stat[:, 4:5],
                                     AF.Exp, scale=-0.5), "Sc")
        self.wait(V, "Sc", isc)
        V.tensor_mul(self.stat[:, 3:4], self.stat[:, 0:1], self.stat[:, 2:3])
        V.drain()
        V.tensor_scalar_mul(self.stat[:, 3:4], self.stat[:, 3:4], -1.0)
        V.drain()
        self.inc(V.tensor_scalar(x_out, xpre, self.stat[:, 2:3],
                                 self.stat[:, 3:4], ALU.mult, ALU.add), "V")

    def _router(self, b):
        nc = self.nc
        sy, gp = nc.sync, nc.gpsimd
        TE, SC, V = nc.tensor, nc.scalar, nc.vector
        self.wait(gp, "V")
        if "d_xsum" in self.dbg:
            idm = self.inc(gp.dma_start(out=self.dbg["d_xsum"].ap()[b][None, :],
                                        in_=self.xsum[:]), "dW")
            gp.wait_ge(self.sem["dW"], idm)
        self.inc(gp.dma_start(
            out=self.xsum_fm[:],
            in_=_ap(self.xsum, 0, [[1024, 1], [8, 128], [1, 8]]),
            allow_slow_non_contiguous=True), "dW")
        gp.wait_ge(self.sem["dW"], self.cnt["dW"])
        self.wait(TE, "dW")
        last = None
        for c in range(8):
            last = TE.matmul(self.psA[0:1, 0:H], self.xsum_fm[:, c:c + 1],
                             self.wr_f[:, c, :], start=(c == 0), stop=(c == 7))
        it = self.inc(last, "T")
        self.wait(V, "T", it)
        V.scalar_tensor_tensor(self.r_row[:], self.br_row[:], float(self.S),
                               self.psA[0:1, 0:H], ALU.mult, ALU.add)
        V.drain()
        V.max(out=self.vmax8[:], in_=self.r_row[:])
        V.drain()
        iv = self.inc(V.tensor_scalar(self.active[:], self.r_row[:],
                                      self.vmax8[:, NA - 1:NA], None, ALU.is_ge),
                      "V")
        V.memset(self.xsum[:], 0.0)
        self.wait(gp, "V", iv)
        self.inc(gp.dma_start(out=self.activeT[:],
                              in_=_ap(self.active, 0, [[16, 1], [1, 16]]),
                              allow_slow_non_contiguous=True), "dW")
        gp.wait_ge(self.sem["dW"], self.cnt["dW"])
        self.wait(TE, "dW")
        mm = TE.matmul(self.psC[0:H, 0:1], self.u16_f[:], self.activeT[:],
                       start=True, stop=True)
        it = self.inc(mm, "T")
        self.wait(V, "T", it)
        last = None
        for a in range(NA):
            last = V.scalar_tensor_tensor(self.sel[:, a:a + 1], self.psC[0:H, 0:1],
                                          float(a), self.activeT[:],
                                          ALU.is_equal, ALU.mult)
        iv = self.inc(last, "V")
        V.drain()
        s_b = _ap(self.sel, 0, [[NA, H], [0, 8], [1, NA]])
        m_b = _ap(self.selmask_f, 0, [[32, H], [NA, 8], [1, NA]])
        o_b = _ap(self.selp, 0, [[32, H], [NA, 8], [1, NA]])
        iv = self.inc(V.tensor_tensor(o_b, s_b, m_b, ALU.mult), "V")
        self.wait(TE, "V", iv)
        mm = TE.matmul(self.psD[:, 0:32], self.lpar_f[:], self.selp[:],
                       start=True, stop=True)
        it = self.inc(mm, "T")
        self.wait(V, "T", it)
        last = None
        for c in range(8):
            src = _ap(self.psD, c * 4, [[512, 128], [1, NA], [0, 64]])
            i2b = _ap(self.i2_f, 0, [[64, 128], [0, NA], [1, 64]])
            dst = _ap(self.G, c * 256, [[8 * 256, 128], [64, NA], [1, 64]])
            last = V.tensor_tensor(dst, src, i2b, ALU.mult)
        self.inc(last, "V")
        self.wait(TE, "V")
        if "d_r" in self.dbg:
            self.wait(sy, "V")
            self.inc(sy.dma_start(out=self.dbg["d_r"].ap()[b][None, :],
                                  in_=self.r_row[:]), "dX")
        if "d_sel" in self.dbg:
            self.wait(sy, "V")
            self.inc(sy.dma_start(out=self.dbg["d_sel"].ap()[b], in_=self.sel[:]),
                     "dX")

    def _phase3(self, b):
        nc = self.nc
        sy, gp = nc.sync, nc.gpsimd
        TE, SC, V = nc.tensor, nc.scalar, nc.vector
        for g in range(self.NG):
            for dt_ in range(4):
                t = g * 4 + dt_

                def evac(gg, ps, dt_=dt_):
                    out = _ap(self.x1T, (4 * gg) * 512 + dt_ * 128,
                              [[8 * 512, 128], [512, 4], [1, 128]])
                    return self.inc(SC.activation(out, ps[:], AF.Copy), "Sc")

                self._tr8(self.x1[:, t, :], evac,
                          banks=[self.psA, self.psB, self.psG, self.psH])
            self.wait(TE, "Sc")
            for ti, (wsb, bias, kind) in enumerate([
                    (self.wq2, self.bq2_fm, "q"),
                    (self.wk2, self.bk2_fm, "k"),
                    (self.wv2, self.bv2_fm, "v")]):
                self.wait(TE, "V")
                sname, scnt = self._w2_done[kind]
                self.wait(TE, sname, scnt)
                evacd = {}
                for hc in range(9):
                    if hc < 8:
                        ps = [self.psA, self.psB][hc % 2]
                        if hc >= 2:
                            self.wait(TE, "V", evacd[hc - 2])
                        mm = None
                        for c in range(8):
                            mm = TE.matmul(ps[:],
                                           wsb[:, c, hc * 128:(hc + 1) * 128],
                                           self.x1T[:, c, :],
                                           start=(c == 0), stop=(c == 7))
                        it = self.inc(mm, "T")
                        self.wait(V, "T", it)
                        evacd[hc] = self.inc(
                            V.tensor_scalar(self.qkv_full[:, hc, :], ps[:],
                                            bias[:, hc:hc + 1], None, ALU.add),
                            "V")
                    cg = hc - 1
                    if cg >= 0:
                        self.wait(TE, "V", evacd[cg])
                        if cg == 0:
                            self.wait(TE, "Sc", getattr(self, "_gps_free", 0))
                        for j in range(2):
                            psj = [self.psC, self.psD][j]
                            mm = TE.matmul(psj[:],
                                           self.G[:, cg, j * 128:(j + 1) * 128],
                                           self.qkv_full[:, cg, :],
                                           start=(cg == 0), stop=(cg == 7),
                                           skip_group_check=True)
                            if cg == 7:
                                self.inc(mm, "T")
                for j in range(2):
                    ps = [self.psC, self.psD][j]
                    it = self.cnt["T"]
                    if kind == "v":
                        self.wait(SC, "T", it)
                        isc = self.inc(SC.activation(self.vsfm[:], ps[:],
                                                     AF.Copy), "Sc")
                        self.wait(TE, "Sc", isc)
                        tr = None
                        for dt_ in range(4):
                            out_bf = self.psE[:, dt_ * 64:(dt_ + 1) * 64].bitcast(
                                BF16)
                            tr = TE.transpose(out_bf,
                                              self.vsfm[:, dt_ * 128:(dt_ + 1) * 128],
                                              self.id_bf[:])
                        it2 = self.inc(tr, "T")
                        self.wait(SC, "T", it2)
                        isc = None
                        for dt_ in range(4):
                            t = g * 4 + dt_
                            dst = _ap(self.Vs, t * NA * 65 + (2 * j) * 65,
                                      [[self.NT * NA * 65, 128], [65, 2], [1, 64]])
                            srcb = self.psE[:, dt_ * 64:(dt_ + 1) * 64].bitcast(
                                BF16)
                            src2 = bass.AP(tensor=srcb.tensor, offset=srcb.offset,
                                           ap=[list(srcb.ap[0]), [64, 2], [1, 64]])
                            isc = self.inc(SC.activation(dst, src2, AF.Copy),
                                           "Sc")
                        self.wait(TE, "Sc", isc)
                        self._gps_free = self.cnt["Sc"]
                    else:
                        dsts = self.Qs if kind == "q" else self.Ks
                        scale = 0.125 if kind == "q" else 1.0
                        self.wait(SC, "T", it)
                        self.inc(SC.activation(
                            dsts[2 * j][0:64, g * 512:(g + 1) * 512],
                            ps[0:64, :], AF.Copy, scale=scale), "Sc")
                        self.wait(SC, "dW", getattr(self, "_shift_idm", 0))
                        isc = self.inc(SC.activation(
                            self.shiftb[64:128, :], ps[64:128, :], AF.Copy,
                            scale=scale), "Sc")
                        self.wait(gp, "Sc", isc)
                        idm = self.inc(gp.dma_start(
                            out=dsts[2 * j + 1][0:64, g * 512:(g + 1) * 512],
                            in_=self.shiftb[64:128, :]), "dW")
                        gp.wait_ge(self.sem["dW"], idm)
                        self._shift_idm = idm
                        self._gps_free = self.cnt["Sc"]

    def _bounds(self, b):
        nc = self.nc
        sy = nc.sync
        TE, SC, V = nc.tensor, nc.scalar, nc.vector
        S = self.S
        nch = S // 512
        V.memset(self.sqb[0][64:65, :], 1.0)
        V.memset(self.sqb[1][64:65, :], 1.0)
        units = []
        for a in range(NA):
            units.append((a, "k", self.Ks[a]))
            units.append((a, "q", self.Qs[a]))
        mm_done, ln_done, exp_done, fin_done = {}, {}, {}, {}
        for u, (a, which, ten) in enumerate(units):
            sq = self.sqb[u % 2]
            nac = self.nac2[u % 2]
            pb = [self.psA, self.psB] if u % 2 == 0 else [self.psC, self.psD]
            self.wait(V, "T", mm_done.get(u - 2, 0))
            iv = self.inc(V.tensor_tensor(sq[0:64, :], ten[0:64, :],
                                          ten[0:64, :], ALU.mult), "V")
            self.wait(TE, "V", iv)
            self.wait(TE, "Sc", ln_done.get(u - 2, 0))
            mm = None
            for ch in range(nch):
                bp = (ch // 2) * 32
                mm = TE.matmul(pb[ch % 2][bp:bp + 1, 0:512], self.ones65_bf[:],
                               sq[:, ch * 512:(ch + 1) * 512],
                               start=True, stop=True)
            mm_done[u] = self.inc(mm, "T")
            self.wait(SC, "T", mm_done[u])
            self.wait(SC, "V", fin_done.get(u - 2, 0))
            for ch in range(nch):
                bp = (ch // 2) * 32
                SC.activation(nac[0:1, ch * 512:(ch + 1) * 512],
                              pb[ch % 2][bp:bp + 1, 0:512], AF.Ln)
            SC.drain()
            exp_done[u] = self.inc(SC.activation(nac[:], nac[:], AF.Exp,
                                                 scale=0.5), "Sc")
            ln_done[u] = exp_done[u]
            self.wait(V, "Sc", exp_done[u])
            if which == "k":
                iv = self.inc(V.reduce_max(self.kmax_row[0:1, a:a + 1],
                                           nac[:], AX.X), "V")
                V.drain()
            else:
                iv = self.inc(V.tensor_scalar(
                    ten[64:65, :], nac[:],
                    self.kmax_row[0:1, a:a + 1], -1.0,
                    ALU.mult, ALU.mult), "V")
                V.drain()
                V.memset(self.Ks[a][64:65, :], 1.0)
                self.inc(V.memset(self.vmax8[0:1, 1:2], 0.0), "V")
            fin_done[u] = self.cnt["V"]
        self.wait(TE, "V")
        gp = self.nc.gpsimd
        if "d_qs" in self.dbg:
            self.wait(gp, "V")
            idm = None
            for a in range(NA):
                self.inc(gp.dma_start(out=self.dbg["d_qs"].ap()[b, a],
                                      in_=self.Qs[a][:]), "dW")
                idm = self.inc(gp.dma_start(out=self.dbg["d_ks"].ap()[b, a],
                                            in_=self.Ks[a][:]), "dW")
            gp.wait_ge(self.sem["dW"], idm)
        if "d_vs" in self.dbg:
            self.wait(gp, "V")
            idm = None
            for t in range(self.NT):
                idm = self.inc(gp.dma_start(
                    out=self.dbg["d_vs"].ap()[b, t * 128:(t + 1) * 128, :],
                    in_=self.Vs[:, t, :]), "dW")
            gp.wait_ge(self.sem["dW"], idm)

    def _phase45(self, b, last=False):
        nc = self.nc
        sy = nc.sync
        TE, SC, V = nc.tensor, nc.scalar, nc.vector
        S, NT = self.S, self.NT
        NQC = S // 1024
        dO_tile = {}
        self.wait(TE, "dW")
        for qc in range(min(NQC, self.p45_qc)):
            otn_done = {}
            for a in range(NA):
                exp_done = {}
                pv_done = {}
                psO_pair = [[self.psE, self.psF], [self.psG, self.psH]][a % 2]
                if a < 2:
                    self.wait(TE, "V")
                sbank = [self.psA, self.psB, self.psC, self.psD]
                for kt in range(NT + 1):
                    if kt < NT:
                        for hf in range(2):
                            ei = 2 * kt + hf
                            psS = sbank[ei % 4]
                            self.wait(TE, "Sc", exp_done.get(
                                ei - 4, getattr(self, "_rep_sc", 0)))
                            mm = TE.matmul(
                                psS[:],
                                self.Ks[a][:, kt * 128:(kt + 1) * 128],
                                self.Qs[a][:, qc * 1024 + hf * 512:
                                           qc * 1024 + (hf + 1) * 512],
                                start=True, stop=True)
                            it = self.inc(mm, "T")
                            self.wait(SC, "T", it)
                            self.wait(SC, "T", pv_done.get(kt - 2, 0))
                            isc = self.inc(SC.activation(
                                self.Pt[kt % 2][:, hf * 512:(hf + 1) * 512],
                                psS[:], AF.Exp), "Sc")
                            exp_done[ei] = isc
                    kp = kt - 1
                    if kp >= 0:
                        self.wait(TE, "Sc", exp_done[2 * kp + 1])
                        if kp == 0 and a >= 2:
                            self.wait(TE, "V", otn_done.get(a - 2, 0))
                        mm2 = None
                        for hf in range(2):
                            psO = psO_pair[hf]
                            mm2 = TE.matmul(
                                psO[0:65, :],
                                self.Vs[:, kp, a * 65:(a + 1) * 65],
                                self.Pt[kp % 2][:, hf * 512:(hf + 1) * 512],
                                start=(kp == 0), stop=(kp == NT - 1),
                                skip_group_check=True)
                        pv_done[kp] = self.inc(mm2, "T")
                self.wait(SC, "T")
                with nc.allow_low_precision(reason="f32r operand prep"):
                    for hf in range(2):
                        psO = psO_pair[hf]
                        SC.activation(self.nrm_f[0:1, hf * 512:(hf + 1) * 512],
                                      psO[64:65, :], AF.Ln)
                    SC.drain()
                    isc = self.inc(SC.activation(self.rec_row[:],
                                                 self.nrm_f[:], AF.Exp,
                                                 scale=-1.0), "Sc")
                self.wait(TE, "Sc", isc)
                psR_pair = [self.psC, self.psD]
                last_mm = None
                for hf in range(2):
                    last_mm = TE.matmul(psR_pair[hf][0:64, :], self.onesr64_r[:],
                                        self.rec_row[0:1, hf * 512:(hf + 1) * 512],
                                        start=True, stop=True)
                it = self.inc(last_mm, "T")
                self.wait(SC, "T", it)
                self.wait(SC, "V", otn_done.get(a - 1, 0))
                isc = None
                for hf in range(2):
                    isc = self.inc(SC.activation(
                        self.rep_sb[:, hf * 512:(hf + 1) * 512],
                        psR_pair[hf][0:64, :], AF.Copy), "Sc")
                self._rep_sc = isc
                self.wait(V, "Sc", isc)
                self.wait(V, "T", pv_done[NT - 1])
                last_v = None
                for hf in range(2):
                    psO = psO_pair[hf]
                    with nc.allow_low_precision(reason="f32r operand prep"):
                        last_v = V.tensor_tensor(
                            self.OTn[0:64, a, hf * 512:(hf + 1) * 512],
                            psO[0:64, :],
                            self.rep_sb[:, hf * 512:(hf + 1) * 512], ALU.mult)
                otn_done[a] = self.inc(last_v, "V")
                if "d_ot" in self.dbg:
                    gp = nc.gpsimd
                    self.wait(gp, "V")
                    idm = self.inc(gp.dma_start(
                        out=self.dbg["d_ot"].ap()[b, a, :,
                                                  qc * 1024:(qc + 1) * 1024],
                        in_=self.OTn[0:64, a, :]), "dW")
                    gp.wait_ge(self.sem["dW"], idm)
                    self.wait(V, "dW", idm)
            self.wait(TE, "V")

            if self.p45_level < 2:
                continue
            res_done = {}
            v_at_start = self.cnt["V"]
            for dt_ in range(8):
                t = qc * 8 + dt_
                opair = [[self.psA, self.psB], [self.psC, self.psD]][dt_ % 2]
                self.wait(TE, "V", res_done.get(dt_ - 2, v_at_start))
                for half in range(2):
                    ps = opair[half]
                    for a in range(NA):
                        TE.matmul(ps[:],
                                  self.OTn[0:64, a, dt_ * 128:(dt_ + 1) * 128],
                                  self.wo2r[:, a, half * 512:(half + 1) * 512],
                                  start=(a == 0), stop=False)
                    mm = TE.matmul(ps[:], self.ones2_bf[:],
                                   self.bo2_hl[:, half * 512:(half + 1) * 512],
                                   start=False, stop=True)
                    self.inc(mm, "T")
                so = "dO%d" % (t % 2)
                self.wait(V, so, dO_tile.get(t - 2, 0))
                self.wait(V, "T")
                for half in range(2):
                    psO = opair[half]
                    V.scalar_tensor_tensor(
                        self.x1[:, t, half * 512:(half + 1) * 512], psO[:], 0.0,
                        self.x1[:, t, half * 512:(half + 1) * 512],
                        ALU.add, ALU.add,
                        accum_out=self.sums[:, half:half + 1])
                self.inc(V.memset(self.sums[:, 8:9], 0.0), "V")
                V.drain()
                self._ln_finalize(self.x1[:, t, :], self.sums[:, 0:1],
                                  self.sums[:, 1:2], self.x1[:, t, :])
                res_done[dt_] = self.cnt["V"]
                self.wait(sy, "V")
                dO_tile[t] = self.inc(
                    sy.dma_start(out=self.out_h.ap()[b, t * 128:(t + 1) * 128, :],
                                 in_=self.x1[:, t, :]), so)
        if not last:
            self._wload_b1(first=False)


def _pack_inputs(inputs):
    g = lambda n: np.ascontiguousarray(np.asarray(inputs[n], np.float32))
    m = {}
    m["a_wq"] = g("L0_attn_wq").reshape(D, D).copy()
    m["a_wk"] = g("L0_attn_wk").reshape(D, D).copy()
    m["a_wv"] = g("L0_attn_wv").reshape(D, D).copy()
    m["a_wo"] = g("L0_attn_wo").reshape(D, D).copy()
    m["a_bq"] = g("L0_attn_bq").reshape(D).copy()
    m["a_bk"] = g("L0_attn_bk").reshape(D).copy()
    m["a_bv"] = g("L0_attn_bv").reshape(D).copy()
    m["a_bo"] = g("L0_attn_bo").reshape(D).copy()
    m["f_wq"] = g("L0_ffn_Wq")
    m["f_wk"] = g("L0_ffn_Wk")
    m["f_wv"] = g("L0_ffn_Wv")
    m["f_wo"] = g("L0_ffn_Wo")
    m["f_bq"] = g("L0_ffn_bq")
    m["f_bk"] = g("L0_ffn_bk")
    m["f_bv"] = g("L0_ffn_bv")
    m["f_bo"] = g("L0_ffn_bo")
    m["f_wr"] = g("L0_ffn_Wr")
    m["f_br"] = g("L0_ffn_br")
    return m


def kernel(**inputs):
    x = np.ascontiguousarray(np.asarray(inputs["L0_attn_x"], np.float32))
    B, S, _ = x.shape
    nb = B // N_CORES
    prog = Prog(S=S, NB=nb)
    wm = _pack_inputs(inputs)
    in_maps = []
    for core in range(N_CORES):
        im = dict(wm)
        im["x"] = np.ascontiguousarray(x[core * nb:(core + 1) * nb])
        in_maps.append(im)
    res = run_bass_kernel_spmd(prog.nc, in_maps, core_ids=list(range(N_CORES)))
    out = np.concatenate([r["out"] for r in res.results], axis=0)
    return out.astype(np.float32)


# version 11
# speedup vs baseline: 1.0072x; 1.0072x over previous
import numpy as np

import concourse.bass as bass
import concourse.mybir as mybir
from concourse.bass_utils import run_bass_kernel_spmd

F32 = mybir.dt.float32
F32R = mybir.dt.float32r
BF16 = mybir.dt.bfloat16
AF = mybir.ActivationFunctionType
ALU = mybir.AluOpType
AX = mybir.AxisListType

D = 1024
H = 16
HD = 64
NA = 4
EPS_ATTN = 1e-6
EPS_LN = 1e-5
N_CORES = 8

WNAMES = ["a_wq", "a_wk", "a_wv", "a_wo", "a_bq", "a_bk", "a_bv", "a_bo",
          "f_wq", "f_wk", "f_wv", "f_wo", "f_bq", "f_bk", "f_bv", "f_bo",
          "f_wr", "f_br"]


def _ap(t, offset, dims):
    return bass.AP(tensor=t[:].tensor, offset=offset, ap=[list(d) for d in dims])


class Prog:
    def __init__(self, S=2048, NB=2, debug=(), stop_after=None, p2_level=9,
                 p2y_level=9, p45_level=9, p45_qc=99):
        assert S % 1024 == 0
        self.stop_after = stop_after
        self.p2_level = p2_level
        self.p2y_level = p2y_level
        self.p45_level = p45_level
        self.p45_qc = p45_qc
        self.S, self.NB = S, NB
        self.NT = S // 128
        self.NG = self.NT // 4
        self.debug = set(debug)
        nc = bass.Bass("TRN2", target_bir_lowering=False)
        self.nc = nc

        dt = nc.dram_tensor
        self.x_h = dt("x", [NB, S, D], F32, kind="ExternalInput")
        self.out_h = dt("out", [NB, S, D], F32, kind="ExternalOutput")
        shp = {"a_wq": [D, D], "a_wk": [D, D], "a_wv": [D, D], "a_wo": [D, D],
               "a_bq": [D], "a_bk": [D], "a_bv": [D], "a_bo": [D],
               "f_wq": [D, D], "f_wk": [D, D], "f_wv": [D, D],
               "f_wo": [NA * HD, D], "f_bq": [D], "f_bk": [D], "f_bv": [D],
               "f_bo": [D], "f_wr": [D, H], "f_br": [H]}
        self.w = {n: dt(n, shp[n], F32, kind="ExternalInput") for n in WNAMES}
        self.wo1r_h = dt("wo1r", [D, D], F32R, kind="Internal")
        self.dbg = {}
        for name, s in [("d_qm", [NB, 128, 8, 512]), ("d_ynt", [NB, 128, 8, 512]),
                        ("d_xg", [NB, 8, D]), ("d_km", [NB, 128, D]),
                        ("d_vaug", [NB, 128, H * 65]), ("d_xsum", [NB, D]),
                        ("d_kv", [NB, 128, 16 * 65]),
                        ("d_x1", [NB, S, D]), ("d_r", [NB, H]),
                        ("d_sel", [NB, H, NA]),
                        ("d_qs", [NB, NA, 65, S]), ("d_ks", [NB, NA, 65, S]),
                        ("d_vs", [NB, S, NA * 65]), ("d_ot", [NB, NA, 64, S])]:
            if name in self.debug:
                self.dbg[name] = dt(name, s, F32, kind="ExternalOutput")

        it = nc.inline_tensor
        self.c_id = it(np.eye(128, dtype=np.float32), "c_id")
        self.c_ones2 = it(np.ones((2, 128), np.float32), "c_ones2")
        self.c_ones64 = it(np.ones((64, 1), np.float32), "c_ones64")
        o65 = np.ones((65, 1), np.float32); o65[64, 0] = 1e-30
        self.c_ones65 = it(o65, "c_ones65")
        self.c_ones128 = it(np.ones((128, 1), np.float32), "c_ones128")
        self.c_onesr64 = it(np.ones((1, 64), np.float32), "c_onesr64")
        self.c_u16 = it(np.triu(np.ones((H, H), np.float32), 1).copy(), "c_u16")
        lpar = np.zeros((H, 128), np.float32)
        for h in range(H):
            lpar[h, (h % 2) * 64:(h % 2) * 64 + 64] = 1.0
        self.c_lpar = it(lpar, "c_lpar")
        selmask = np.zeros((H, 32), np.float32)
        for h in range(H):
            selmask[h, (h // 2) * 4:(h // 2) * 4 + 4] = 1.0
        self.c_selmask = it(selmask, "c_selmask")
        self.c_i2 = it(np.vstack([np.eye(64, dtype=np.float32)] * 2).copy(), "c_i2")

        self._tr_rot = 0
        self._sbuf()
        self._sems()
        self._emit()

    def _sbuf(self):
        nc = self.nc
        S, NT = self.S, self.NT
        a = nc.alloc_sbuf_tensor

        def at(name, shape, dtp, off):
            return nc.alloc_sbuf_tensor_at(name, shape, dtp, offset=off)

        self.id_f = a("id_f", [128, 128], F32)
        self.id_bf = a("id_bf", [128, 128], BF16)
        self.ones2_bf = a("ones2_bf", [2, 128], BF16)
        self.ones64_bf = a("ones64_bf", [64, 1], BF16)
        self.ones65_bf = a("ones65_bf", [65, 1], BF16)
        self.onesr64_r = a("onesr64_r", [1, 64], F32R)
        self.ones128_f = a("ones128_f", [128, 1], F32)
        self.lpar_f = a("lpar_f", [H, 128], F32)
        self.selmask_f = a("selmask_f", [H, 32], F32)
        self.selp = a("selp", [H, 32], F32)
        self.i2_f = a("i2_f", [128, 64], F32)
        self.u16_f = a("u16_f", [H, H], F32)

        self.x1 = a("x1", [128, NT, D], F32)
        self.Vs = a("Vs", [128, NT, NA * 65], BF16)
        self.kv_z = a("kv_z", [128, H, 65], BF16)
        self.esc = a("esc", [128, 512], F32)
        self.esc2 = a("esc2", [128, 512], F32)
        self.sums = a("sums", [128, 10], F32)
        self.stat = a("stat", [128, 6], F32)
        self.wr_f = a("wr_f", [128, 8, H], F32)
        self.xsum = a("xsum", [1, D], F32)
        self.xsum_fm = a("xsum_fm", [128, 8], F32)
        self.nrm_f = a("nrm_f", [1, D], F32)

        self.bq1_fm = a("bq1_fm", [128, 8], F32)
        self.bq2_fm = a("bq2_fm", [128, 8], F32)
        self.bk2_fm = a("bk2_fm", [128, 8], F32)
        self.bv2_fm = a("bv2_fm", [128, 8], F32)
        self.bk1_hl = a("bk1_hl", [2, D], BF16)
        self.bv1_hl = a("bv1_hl", [2, D], BF16)
        self.bo1_hl = a("bo1_hl", [2, D], BF16)
        self.bo2_hl = a("bo2_hl", [2, D], BF16)
        self.br_row = a("br_row", [1, H], F32)

        self.r_row = a("r_row", [1, H], F32)
        self.vmax8 = a("vmax8", [1, 8], F32)
        self.active = a("active", [1, H], F32)
        self.activeT = a("activeT", [H, 1], F32)
        self.sel = a("sel", [H, NA], F32)
        self.kmax_row = a("kmax_row", [1, NA], F32)

        rqk = nc.bump_sbuf(NA * 2 * S * 2)[0]
        self.Qs = [at(f"Qs{i}", [65, S], BF16, rqk + i * (S * 2)) for i in range(NA)]
        self.Ks = [at(f"Ks{i}", [65, S], BF16, rqk + (NA + i) * (S * 2))
                   for i in range(NA)]

        ph = nc.bump_sbuf(78336)[0]
        self.ph = ph
        self.wk1 = at("wk1", [128, 8, D], BF16, ph + 0)
        self.wv1 = at("wv1", [128, 8, D], BF16, ph + 16384)
        self.xin = [at(f"xin{i}", [128, D], F32, ph + 32768 + i * 4096)
                    for i in range(2)]
        self.xin_bf = [at(f"xinb{i}", [128, D], BF16, ph + 32768 + i * 2048)
                       for i in range(4)]
        self.km = [at(f"km{i}", [128, D], BF16,
                      ph + (40960 if i == 0 else 65664)) for i in range(2)]
        self.vaug = [at(f"vaug{i}", [128, H * 65], BF16,
                        ph + (43008 if i == 0 else 67712)) for i in range(2)]
        self.xt1 = [at(f"xt1_{i}", [128, 8, 128], BF16, ph + 45184 + i * 2048)
                    for i in range(2)]
        self.wq1 = at("wq1", [128, 8, D], BF16, ph + 49280)
        self.xTg = at("xTg", [128, 8, 512], BF16, ph + 0)
        self.qm = at("qm", [128, 8, 512], BF16, ph + 8192)
        self.ynTg = at("ynTg", [128, 8, 512], F32R, ph + 16384)
        self.wo1_ring = [at(f"wo1g{i}", [128, 512], F32R,
                            ph + (32768 + i * 2048 if i < 4 else
                                  69824 + (i - 4) * 2048)) for i in range(8)]
        self.ynorm = [at(f"ynorm{i}", [128, D], F32,
                         ph + (40960 if i == 0 else 45056)) for i in range(2)]
        self.wq2 = at("wq2", [128, 8, D], BF16, ph + 0)
        self.wk2 = at("wk2", [128, 8, D], BF16, ph + 16384)
        self.wv2 = at("wv2", [128, 8, D], BF16, ph + 32768)
        self.x1T = at("x1T", [128, 8, 512], BF16, ph + 49152)
        self.qkv_full = at("qkv_full", [128, 8, 512], BF16, ph + 57344)
        self.shiftb = at("shiftb", [128, 512], BF16, ph + 65536)
        self.vsfm = at("vsfm", [128, 512], BF16, ph + 66560)
        self.G = at("G", [128, 8, 256], BF16, ph + 67584)
        self.sqb = [at(f"sqb{i}", [65, S], BF16, ph + i * 4096)
                    for i in range(2)]
        self.nac2 = [at(f"nac2_{i}", [1, S], F32, ph + 8192 + i * 8192)
                     for i in range(2)]
        self.OTn = at("OTn", [64, NA, D], F32R, ph + 0)
        self.Pt = [at(f"Pt{i}", [128, D], BF16, ph + 16384 + i * 2048)
                   for i in range(2)]
        self.rep_sb = at("rep_sb", [64, D], F32, ph + 20480)
        self.rec_row = at("rec_row", [1, D], F32R, ph + 24576)
        self.wo2r = at("wo2r", [64, NA, D], F32R, ph + 28672)
        self.wk1n = at("wk1n", [128, 8, D], BF16, ph + 45056)
        self.wv1n = at("wv1n", [128, 8, D], BF16, ph + 61440)
        self.brow = at("brow", [1, D], F32, ph + 0)
        self.blo_f = at("blo_f", [1, D], F32, ph + 4096)

        p = nc.alloc_psum_tensor
        (self.psA, self.psB, self.psC, self.psD, self.psE, self.psF,
         self.psG, self.psH) = [p(f"ps{i}", [128, 512], F32) for i in range(8)]

    def _sems(self):
        self.sem, self.cnt = {}, {}
        for n in ["dW", "dX", "dX0", "dX1", "dX2", "dX3",
                  "dB0", "dB1", "dB2",
                  "dG0", "dG1", "dG2", "dG3", "dO0", "dO1",
                  "dR0", "dR1", "dR2", "dR3", "dR4", "dR5", "dR6", "dR7",
                  "T", "V", "Sc"]:
            self.sem[n] = self.nc.alloc_semaphore(n)
            self.cnt[n] = 0

    def inc(self, instr, name):
        amt = 16 if name.startswith("d") else 1
        instr.then_inc(self.sem[name], amt)
        self.cnt[name] += amt
        return self.cnt[name]

    def wait(self, eng, name, val=None):
        eng.wait_ge(self.sem[name], self.cnt[name] if val is None else val)

    def _emit(self):
        nc = self.nc
        sa = self.stop_after
        with nc.Block():
            self._startup()
            nc.all_engine_barrier()
            self._wload_b1(first=True)
            self._w1_done = self.cnt["dW"]
            nc.all_engine_barrier()
            for b in range(self.NB):
                if sa == "startup":
                    break
                self._phase1(b)
                nc.all_engine_barrier()
                if sa == "p1":
                    break
                self._phase2(b)
                nc.all_engine_barrier()
                if sa == "p2":
                    break
                self._router(b)
                self._wload_b2()
                nc.all_engine_barrier()
                if sa == "router":
                    break
                self._phase3(b)
                nc.all_engine_barrier()
                if sa == "p3":
                    break
                self._wload_wo2()
                self._bounds(b)
                nc.all_engine_barrier()
                if sa == "bounds":
                    break
                self._phase45(b, last=(b == self.NB - 1))
                nc.all_engine_barrier()
                if b + 1 < self.NB:
                    gp = nc.gpsimd
                    self.inc(gp.dma_start(out=self.wk1[:], in_=self.wk1n[:]), "dW")
                    self.inc(gp.dma_start(out=self.wv1[:], in_=self.wv1n[:]), "dW")
                    gp.wait_ge(self.sem["dW"], self.cnt["dW"])
                    self._wload_wq1()
                    self._w1_done = self.cnt["dW"]
                    nc.all_engine_barrier()
            for s in ["dX", "dX0", "dX1", "dX2", "dX3", "dO0", "dO1",
                      "dR0", "dR1", "dR2",
                      "dR3", "dR4", "dR5", "dR6", "dR7"]:
                self.wait(nc.sync, s)
            for s in ["dG0", "dG1", "dG2", "dG3", "dB0", "dB1", "dB2"]:
                self.wait(nc.sync, s)
            self.wait(nc.gpsimd, "dW")

    def _startup(self):
        nc = self.nc
        sy, gp, V = nc.sync, nc.gpsimd, nc.vector
        w = self.w
        for dst, src in [(self.id_f, self.c_id), (self.ones128_f, self.c_ones128),
                         (self.lpar_f, self.c_lpar),
                         (self.selmask_f, self.c_selmask),
                         (self.i2_f, self.c_i2), (self.u16_f, self.c_u16)]:
            self.inc(sy.dma_start(out=dst[:], in_=src.ap()), "dX")
        for dst, src in [(self.id_bf, self.c_id), (self.ones2_bf, self.c_ones2),
                         (self.ones64_bf, self.c_ones64),
                         (self.ones65_bf, self.c_ones65),
                         (self.onesr64_r, self.c_onesr64)]:
            self.inc(gp.dma_start(out=dst[:], in_=src.ap()), "dW")
        self.inc(gp.dma_start(out=self.wo1r_h.ap(), in_=w["a_wo"].ap()), "dW")
        self.inc(sy.dma_start(
            out=self.wr_f[:],
            in_=w["f_wr"].ap().rearrange("(p c) h -> p c h", p=128)), "dX")
        for src, dst in [("a_bq", self.bq1_fm), ("f_bq", self.bq2_fm),
                         ("f_bk", self.bk2_fm), ("f_bv", self.bv2_fm)]:
            self.inc(sy.dma_start(
                out=dst[:], in_=w[src].ap().rearrange("(c p) -> p c", p=128),
                allow_slow_non_contiguous=True), "dX")
        self.inc(sy.dma_start(out=self.br_row[:], in_=w["f_br"].ap()[None, :]),
                 "dX")
        for src, dst in [("a_bk", self.bk1_hl), ("a_bv", self.bv1_hl),
                         ("a_bo", self.bo1_hl), ("f_bo", self.bo2_hl)]:
            self.inc(sy.dma_start(out=self.brow[:], in_=w[src].ap()[None, :]),
                     "dX")
            self.wait(V, "dX")
            self.wait(V, "dW")
            V.tensor_copy(dst[0:1, :], self.brow[:])
            V.drain()
            iv = self.inc(V.tensor_sub(self.blo_f[:], self.brow[:], dst[0:1, :]),
                          "V")
            self.wait(gp, "V", iv)
            self.inc(gp.dma_start(out=dst[1:2, :], in_=self.blo_f[:]), "dW")
            sy.wait_ge(self.sem["V"], iv)
        self.wait(V, "dW")
        V.memset(_ap(self.Vs, 64, [[self.NT * NA * 65, 128], [65, self.NT * NA]]),
                 1.0)
        V.memset(self.sums[:], 0.0)
        self.inc(V.memset(self.stat[:], 0.0), "V")

    def _wload_b1(self, first=False):
        gp = self.nc.gpsimd
        for src, dst in [("a_wk", self.wk1 if first else self.wk1n),
                         ("a_wv", self.wv1 if first else self.wv1n)]:
            self.inc(gp.dma_start(
                out=dst[:],
                in_=self.w[src].ap().rearrange("(c p) f -> p c f", p=128)), "dW")
        if first:
            self._wload_wq1()

    def _wload_wq1(self):
        gp = self.nc.gpsimd
        self.inc(gp.dma_start(
            out=self.wq1[:],
            in_=self.w["a_wq"].ap().rearrange("(c p) f -> p c f", p=128)), "dW")

    def _wload_b2(self):
        gp = self.nc.gpsimd
        self._w2_done = {}
        for src, dst, kind, sname in [("f_wq", self.wq2, "q", "dB0"),
                                      ("f_wk", self.wk2, "k", "dB1"),
                                      ("f_wv", self.wv2, "v", "dB2")]:
            self.wait(gp, sname)
            cnt = self.inc(gp.dma_start(
                out=dst[:],
                in_=self.w[src].ap().rearrange("(c p) f -> p c f", p=128)), sname)
            self._w2_done[kind] = (sname, cnt)

    def _wload_wo2(self):
        gp = self.nc.gpsimd
        self.inc(gp.dma_start(
            out=self.wo2r[:],
            in_=self.w["f_wo"].ap().rearrange("(a d) f -> d a f", d=HD)), "dW")

    def _tr8(self, src_f32, evac_list, banks=None):
        nc = self.nc
        TE = nc.tensor
        if banks is None:
            banks = [self.psG, self.psH]
        if not hasattr(self, "_tr_evac"):
            self._tr_evac = {}
        for g in range(2):
            bi = (self._tr_rot + g) % len(banks)
            ps = banks[bi]
            esem, ecnt = self._tr_evac.get(id(ps), ("Sc", 0))
            self.wait(TE, esem, ecnt)
            self.wait(TE, "V", getattr(self, "_psGH_vfree", 0))
            last = None
            for c4 in range(4):
                c = 4 * g + c4
                last = TE.transpose(ps[:, c4 * 128:(c4 + 1) * 128],
                                    src_f32[:, c * 128:(c + 1) * 128],
                                    self.id_f[:])
            it = self.inc(last, "T")
            self.nc.scalar.wait_ge(self.sem["T"], it)
            self._tr_evac[id(ps)] = evac_list(g, ps)
        self._tr_rot = (self._tr_rot + 2) % len(banks)

    def _phase1(self, b):
        nc = self.nc
        sy = nc.sync
        TE, SC, V = nc.tensor, nc.scalar, nc.vector
        NT = self.NT
        self.wait(TE, "dW", self._w1_done)
        V.memset(_ap(self.vaug[0], 64, [[H * 65, 128], [65, H]]), 1.0)
        V.memset(_ap(self.vaug[1], 64, [[H * 65, 128], [65, H]]), 1.0)
        self.inc(V.memset(self.sums[:, 7:8], 0.0), "V")
        v_done = self.cnt["V"]
        self._p1_vdone = {}
        gp = nc.gpsimd
        for t in range(NT):
            if t % 2 == 0:
                eng, sx = sy, "dX%d" % ((t // 2) % 4)
            else:
                eng, sx = gp, "dG%d" % ((t // 2) % 4)
            self.wait(eng, sx)
            self.wait(eng, "dO%d" % (t % 2))
            self.inc(eng.dma_start(
                out=self.x1[:, t, :],
                in_=self.x_h.ap()[b, t * 128:(t + 1) * 128, :]), sx)
            self.wait(TE, sx)
            self.wait(TE, "V", v_done)
            xt = self.xt1[t % 2]

            def evac(gg, ps, xt=xt):
                out = _ap(xt, 4 * gg * 128, [[1024, 128], [1, 512]])
                if gg % 2 == 0:
                    return ("Sc", self.inc(SC.activation(out, ps[:], AF.Copy),
                                           "Sc"))
                self.wait(V, "T", self.cnt["T"])
                return ("V", self.inc(V.tensor_copy(out, ps[:]), "V"))

            self._tr8(self.x1[:, t, :], evac)
            self.wait(TE, "Sc")
            self.wait(TE, "V")
            tmark = {}
            for ti, (wsb, bhl) in enumerate([(self.wk1, self.bk1_hl),
                                             (self.wv1, self.bv1_hl)]):
                for half in range(2):
                    ps = [self.psA, self.psB, self.psC, self.psD][2 * ti + half]
                    for c in range(8):
                        TE.matmul(ps[:], xt[:, c, :],
                                  wsb[:, c, half * 512:(half + 1) * 512],
                                  start=(c == 0), stop=False)
                    mm = TE.matmul(ps[:], self.ones2_bf[:],
                                   bhl[:, half * 512:(half + 1) * 512],
                                   start=False, stop=True)
                    tmark[(ti, half)] = self.inc(mm, "T")
            for half in range(2):
                ps = [self.psA, self.psB][half]
                self.wait(V, "T", tmark[(0, half)])
                V.tensor_scalar_min(self.esc[:], ps[:], 0.0)
                iv = self.inc(V.memset(self.sums[:, 7:8], 0.0), "V")
                self.wait(SC, "V", iv)
                isc = self.inc(SC.activation(self.esc2[:], self.esc[:], AF.Exp),
                               "Sc")
                self.wait(V, "Sc", isc)
                st = V.scalar_tensor_tensor(
                    self.km[t % 2][:, half * 512:(half + 1) * 512],
                    ps[:], 0.0, self.esc2[:], ALU.max, ALU.add)
                self.inc(st, "V")
            for half in range(2):
                ps = [self.psC, self.psD][half]
                self.wait(V, "T", tmark[(1, half)])
                out = _ap(self.vaug[t % 2], half * 8 * 65,
                          [[H * 65, 128], [65, 8], [1, 64]])
                self.inc(V.tensor_copy(out, ps[:]), "V")
            v_done = self.cnt["V"]
            self._p1_vdone[t] = v_done
            if t == 0 and "d_km" in self.dbg:
                gp = nc.gpsimd
                self.wait(gp, "V", v_done)
                self.inc(gp.dma_start(out=self.dbg["d_km"].ap()[b],
                                      in_=self.km[t % 2][:]), "dW")
                idm = self.inc(gp.dma_start(out=self.dbg["d_vaug"].ap()[b],
                                            in_=self.vaug[t % 2][:]), "dW")
                gp.wait_ge(self.sem["dW"], idm)
                self.wait(V, "dW", idm)
            if t > 0:
                self._p1_kv_accum(t - 1, NT)
        self._p1_kv_accum(NT - 1, NT)
        self.wait(V, "T")
        V.memset(self.kv_z[:], 0.0)
        V.drain()
        for bi, bank in enumerate([self.psE, self.psF]):
            oute = _ap(self.kv_z, (2 * 4 * bi) * 65,
                       [[H * 65, 64], [2 * 65, 4], [1, 65]])
            srce = _ap(bank, 0, [[512, 64], [128, 4], [1, 65]])
            V.tensor_copy(oute, srce)
            V.drain()
            oto = _ap(self.kv_z, (2 * 4 * bi + 1) * 65 + 64 * H * 65,
                      [[H * 65, 64], [2 * 65, 4], [1, 65]])
            srco = _ap(bank, 64 * 512, [[512, 64], [128, 4], [1, 65]])
            self.inc(V.tensor_copy(oto, srco), "V")
            V.drain()
        if "d_kv" in self.dbg:
            gp = nc.gpsimd
            self.wait(gp, "V")
            idm = self.inc(gp.dma_start(
                out=self.dbg["d_kv"].ap()[b],
                in_=self.kv_z[:].rearrange("p c k -> p (c k)")), "dW")
            gp.wait_ge(self.sem["dW"], idm)

    def _p1_kv_accum(self, t, NT):
        TE = self.nc.tensor
        self.wait(TE, "V", self._p1_vdone[t])
        last = None
        for h in range(H):
            bank = self.psE if h < 8 else self.psF
            slot = (h % 8) // 2
            bp = (h % 2) * 64
            last = TE.matmul(bank[bp:bp + 64, slot * 128:slot * 128 + 65],
                             self.km[t % 2][:, h * 64:(h + 1) * 64],
                             _ap(self.vaug[t % 2], h * 65,
                                 [[H * 65, 128], [1, 65]]),
                             start=(t == 0 and h % 8 < 2),
                             stop=(t == NT - 1), skip_group_check=True)
        self.inc(last, "T")

    def _phase2(self, b):
        nc = self.nc
        sy = nc.sync
        TE, SC, V = nc.tensor, nc.scalar, nc.vector
        NG = self.NG
        self.wait(TE, "dW", self._w1_done)
        self.qm_evac = {}
        self._ring_used = {}
        self._xsum_pending = None
        self._ynorm_v = {}
        self._ynT_done = {}
        V.memset(self.xsum[:], 0.0)
        self.inc(V.memset(self.sums[:], 0.0), "V")
        ring_cnt = 0
        for g in range(NG):
            for dt_ in range(4):
                t = g * 4 + dt_

                def evac(gg, ps, dt_=dt_):
                    out = _ap(self.xTg, (4 * gg) * 512 + dt_ * 128,
                              [[8 * 512, 128], [512, 4], [1, 128]])
                    if gg % 2 == 0:
                        return ("Sc", self.inc(
                            SC.activation(out, ps[:], AF.Copy), "Sc"))
                    self.wait(V, "T", self.cnt["T"])
                    return ("V", self.inc(V.tensor_copy(out, ps[:]), "V"))

                self._tr8(self.x1[:, t, :], evac)
            self.wait(TE, "Sc")
            self.wait(TE, "V")
            self._emit_xsum(b)
            if self.p2_level < 2:
                continue
            for hc in range(8):
                ps = [self.psA, self.psB, self.psC, self.psD][hc % 4]
                if hc >= 4:
                    self.wait(TE, "V", self.qm_evac[hc - 4])
                mm = None
                for c in range(8):
                    mm = TE.matmul(ps[:], self.wq1[:, c, hc * 128:(hc + 1) * 128],
                                   self.xTg[:, c, :], start=(c == 0), stop=(c == 7))
                it = self.inc(mm, "T")
                self.wait(SC, "T", it)
                self.wait(SC, "V", self.qm_evac.get(hc - 1, 0))
                isc = self.inc(SC.activation(self.esc2[:], ps[:], AF.Relu,
                                             bias=self.bq1_fm[:, hc:hc + 1]), "Sc")
                self.wait(V, "Sc", isc)
                V.tensor_scalar(self.esc[:], ps[:], self.bq1_fm[:, hc:hc + 1],
                                None, ALU.add)
                V.drain()
                V.tensor_sub(self.esc[:], self.esc[:], self.esc2[:])
                iv = self.inc(V.memset(self.sums[:, 7:8], 0.0), "V")
                self.wait(SC, "V", iv)
                isc = self.inc(SC.activation(self.esc[:], self.esc[:], AF.Exp),
                               "Sc")
                self.wait(V, "Sc", isc)
                self.qm_evac[hc] = self.inc(
                    V.tensor_add(self.qm[:, hc, :], self.esc[:], self.esc2[:]),
                    "V")
            self.wait(TE, "V")
            if self.p2_level < 3:
                continue
            for dt_ in range(4):
                self._p2_y_tile(dt_)
                if dt_ > 0:
                    self._p2_ynt(dt_ - 1)
            self._p2_ynt(3)
            if self.p2_level < 4:
                continue
            if g == 0 and "d_qm" in self.dbg:
                gpq = nc.gpsimd
                self.wait(gpq, "V")
                self.wait(gpq, "Sc")
                self.inc(gpq.dma_start(out=self.dbg["d_qm"].ap()[b],
                                       in_=self.qm[:]), "dW")
                idm = self.inc(gpq.dma_start(out=self.dbg["d_ynt"].ap()[b],
                                             in_=self.ynTg[:]), "dW")
                gpq.wait_ge(self.sem["dW"], idm)
                self.wait(TE, "dW", idm)
            self.wait(TE, "Sc")
            for half in range(2):
                for c in range(8):
                    slot = ring_cnt % 8
                    sr = "dR%d" % slot
                    self.wait(sy, sr)
                    self.wait(sy, "T", self._ring_used.get(slot, 0))
                    self.inc(sy.dma_start(
                        out=self.wo1_ring[slot][:],
                        in_=self.wo1r_h.ap()[c * 128:(c + 1) * 128,
                                             half * 512:(half + 1) * 512]), sr)
                    ring_cnt += 1
                    self.wait(TE, sr)
                    if c == 0:
                        self.wait(TE, "V")
                    for dt_ in range(4):
                        psO = [self.psA, self.psB, self.psC, self.psD][dt_]
                        mm = TE.matmul(
                            psO[:],
                            _ap(self.ynTg, (dt_ % 2) * 64 + c * 512,
                                [[8 * 512, 128], [1, 64]]).bitcast(F32R)
                            if False else
                            self.ynTg[:, c, dt_ * 128:(dt_ + 1) * 128],
                            self.wo1_ring[slot][:],
                            start=(c == 0), stop=False)
                        if c == 7:
                            mm = TE.matmul(psO[:], self.ones2_bf[:],
                                           self.bo1_hl[:, half * 512:(half + 1) * 512],
                                           start=False, stop=True)
                    if c == 7:
                        self.inc(mm, "T")
                    else:
                        self.inc(mm, "T")
                    self._ring_used[slot] = self.cnt["T"]
                if self.p2_level < 5:
                    continue
                self.wait(V, "T")
                for dt_ in range(4):
                    t = g * 4 + dt_
                    psO = [self.psA, self.psB, self.psC, self.psD][dt_]
                    st = V.scalar_tensor_tensor(
                        self.x1[:, t, half * 512:(half + 1) * 512], psO[:], 0.0,
                        self.x1[:, t, half * 512:(half + 1) * 512],
                        ALU.add, ALU.add,
                        accum_out=self.sums[:, 2 * dt_ + half:2 * dt_ + half + 1])
                self.inc(V.memset(self.sums[:, 8:9], 0.0), "V")
                V.drain()
            if self.p2_level < 6:
                continue
            for dt_ in range(4):
                t = g * 4 + dt_
                self._ln_finalize(self.x1[:, t, :],
                                  self.sums[:, 2 * dt_:2 * dt_ + 1],
                                  self.sums[:, 2 * dt_ + 1:2 * dt_ + 2],
                                  self.x1[:, t, :])
            self._ln_done = self.cnt["V"]
            self._psGH_vfree = self.cnt["V"]
            self._xsum_pending = g
            if "d_xg" in self.dbg:
                gpx = nc.gpsimd
                self.wait(gpx, "V")
                idm = self.inc(gpx.dma_start(
                    out=self.dbg["d_xg"].ap()[b, g][None, :], in_=self.xsum[:]),
                    "dW")
                gpx.wait_ge(self.sem["dW"], idm)
                self.wait(V, "dW", idm)
        self._emit_xsum(b)
        if "d_x1" in self.dbg:
            self.wait(sy, "V")
            for t in range(self.NT):
                self.inc(sy.dma_start(
                    out=self.dbg["d_x1"].ap()[b, t * 128:(t + 1) * 128, :],
                    in_=self.x1[:, t, :]), "dX")
            self.wait(sy, "dX")

    def _emit_xsum(self, b):
        if getattr(self, "_xsum_pending", None) is None:
            return
        g = self._xsum_pending
        self._xsum_pending = None
        nc = self.nc
        TE, V = nc.tensor, nc.vector
        self.wait(TE, "V", self._ln_done)
        self.wait(TE, "V", getattr(self, "_xsum_vfree", 0))
        last = None
        for dt_ in range(4):
            t = g * 4 + dt_
            for half in range(2):
                ps = [self.psE, self.psF][half]
                last = TE.matmul(ps[0:1, :], self.ones128_f[:],
                                 self.x1[:, t, half * 512:(half + 1) * 512],
                                 start=(dt_ == 0), stop=(dt_ == 3),
                                 skip_group_check=True)
        it = self.inc(last, "T")
        self.wait(V, "T", it)
        for half in range(2):
            ps = [self.psE, self.psF][half]
            V.tensor_add(self.xsum[0:1, half * 512:(half + 1) * 512],
                         self.xsum[0:1, half * 512:(half + 1) * 512],
                         ps[0:1, :])
        V.drain()
        self._xsum_vfree = self.inc(V.memset(self.sums[:, 8:9], 0.0), "V")

    def _p2_y_tile(self, dt_):
        nc = self.nc
        TE, SC, V = nc.tensor, nc.scalar, nc.vector
        lvl = getattr(self, "p2y_level", 9)
        iv = None
        if not hasattr(self, "_y_pair_free"):
            self._y_pair_free = {}
        for hg in range(2):
            pid = (2 * dt_ + hg) % 2
            bpair = [[self.psC, self.psD], [self.psA, self.psB]][pid]
            self.wait(TE, "V", self._y_pair_free.get(pid, 0))
            last = None
            for hh in range(8):
                h = hg * 8 + hh
                bank = bpair[0] if hh < 4 else bpair[1]
                slot = hh % 4
                qm_lhs = self.qm[:, h // 2, dt_ * 128:(dt_ + 1) * 128]
                kv_rhs = self.kv_z[:, h, 0:65]
                last = TE.matmul(bank[:, slot * 128:slot * 128 + 65], qm_lhs,
                                 kv_rhs, start=True, stop=True)
            it = self.inc(last, "T")
            self.wait(V, "T", it)
            self.wait(V, "T", self._ynT_done.get(dt_ % 2, 0))
            for bi, bank in enumerate(bpair):
                h0 = hg * 8 + bi * 4
                if lvl < 2:
                    dst = _ap(self.ynorm, h0 * 64, [[1024, 128], [64, 4], [1, 64]])
                    src = _ap(bank, 0, [[512, 128], [128, 4], [1, 64]])
                    iv = self.inc(V.tensor_copy(dst, src), "V")
                    continue
                V.drain()
                V.tensor_scalar(self.sums[:, 4:8],
                                _ap(bank, 64, [[512, 128], [128, 4]]),
                                EPS_ATTN, None, ALU.add)
                V.drain()
                V.reciprocal(self.sums[:, 4:8], self.sums[:, 4:8])
                V.drain()
                dst = _ap(self.ynorm[dt_ % 2], h0 * 64,
                          [[1024, 128], [64, 4], [1, 64]])
                src = _ap(bank, 0, [[512, 128], [128, 4], [1, 64]])
                zb = _ap(self.sums, 4, [[10, 128], [1, 4], [0, 64]])
                iv = self.inc(V.tensor_tensor(dst, src, zb, ALU.mult), "V")
            self._y_pair_free[pid] = iv
        self._ynorm_v[dt_] = self.cnt["V"]

    def _p2_ynt(self, dt_):
        nc = self.nc
        TE, SC = nc.tensor, nc.scalar

        def evac(gg, ps, dt_=dt_):
            out = _ap(self.ynTg, (4 * gg) * 512 + dt_ * 128,
                      [[8 * 512, 128], [512, 4], [1, 128]])
            with nc.allow_low_precision(reason="f32r operand prep"):
                return ("Sc", self.inc(SC.activation(out, ps[:], AF.Copy),
                                       "Sc"))

        self.wait(TE, "V", self._ynorm_v[dt_])
        self._tr8(self.ynorm[dt_ % 2][:], evac)
        self._ynT_done[dt_ % 2] = self.cnt["T"]

    def _ln_finalize(self, xpre, s0, s1, x_out):
        nc = self.nc
        SC, V = nc.scalar, nc.vector
        for half in range(2):
            V.scalar_tensor_tensor(
                [self.esc, self.esc2][half][:],
                xpre[:, half * 512:(half + 1) * 512], 1.0,
                xpre[:, half * 512:(half + 1) * 512], ALU.mult, ALU.mult,
                accum_out=self.sums[:, 8 + half:9 + half])
        V.drain()
        V.tensor_add(self.stat[:, 0:1], s0, s1)
        V.tensor_add(self.stat[:, 1:2], self.sums[:, 8:9], self.sums[:, 9:10])
        V.drain()
        V.tensor_scalar_mul(self.stat[:, 0:2], self.stat[:, 0:2], 1.0 / D)
        V.drain()
        V.scalar_tensor_tensor(self.stat[:, 2:3], self.stat[:, 0:1], 1.0,
                               self.stat[:, 0:1], ALU.mult, ALU.mult)
        V.drain()
        V.tensor_sub(self.stat[:, 1:2], self.stat[:, 1:2], self.stat[:, 2:3])
        V.drain()
        iv = self.inc(V.tensor_scalar_add(self.stat[:, 1:2], self.stat[:, 1:2],
                                          EPS_LN), "V")
        self.wait(SC, "V", iv)
        SC.activation(self.stat[:, 4:5], self.stat[:, 1:2], AF.Ln)
        SC.drain()
        isc = self.inc(SC.activation(self.stat[:, 2:3], self.stat[:, 4:5],
                                     AF.Exp, scale=-0.5), "Sc")
        self.wait(V, "Sc", isc)
        V.tensor_mul(self.stat[:, 3:4], self.stat[:, 0:1], self.stat[:, 2:3])
        V.drain()
        V.tensor_scalar_mul(self.stat[:, 3:4], self.stat[:, 3:4], -1.0)
        V.drain()
        self.inc(V.tensor_scalar(x_out, xpre, self.stat[:, 2:3],
                                 self.stat[:, 3:4], ALU.mult, ALU.add), "V")

    def _router(self, b):
        nc = self.nc
        sy, gp = nc.sync, nc.gpsimd
        TE, SC, V = nc.tensor, nc.scalar, nc.vector
        self.wait(gp, "V")
        if "d_xsum" in self.dbg:
            idm = self.inc(gp.dma_start(out=self.dbg["d_xsum"].ap()[b][None, :],
                                        in_=self.xsum[:]), "dW")
            gp.wait_ge(self.sem["dW"], idm)
        self.inc(gp.dma_start(
            out=self.xsum_fm[:],
            in_=_ap(self.xsum, 0, [[1024, 1], [8, 128], [1, 8]]),
            allow_slow_non_contiguous=True), "dW")
        gp.wait_ge(self.sem["dW"], self.cnt["dW"])
        self.wait(TE, "dW")
        last = None
        for c in range(8):
            last = TE.matmul(self.psA[0:1, 0:H], self.xsum_fm[:, c:c + 1],
                             self.wr_f[:, c, :], start=(c == 0), stop=(c == 7))
        it = self.inc(last, "T")
        self.wait(V, "T", it)
        V.scalar_tensor_tensor(self.r_row[:], self.br_row[:], float(self.S),
                               self.psA[0:1, 0:H], ALU.mult, ALU.add)
        V.drain()
        V.max(out=self.vmax8[:], in_=self.r_row[:])
        V.drain()
        iv = self.inc(V.tensor_scalar(self.active[:], self.r_row[:],
                                      self.vmax8[:, NA - 1:NA], None, ALU.is_ge),
                      "V")
        V.memset(self.xsum[:], 0.0)
        self.wait(gp, "V", iv)
        self.inc(gp.dma_start(out=self.activeT[:],
                              in_=_ap(self.active, 0, [[16, 1], [1, 16]]),
                              allow_slow_non_contiguous=True), "dW")
        gp.wait_ge(self.sem["dW"], self.cnt["dW"])
        self.wait(TE, "dW")
        mm = TE.matmul(self.psC[0:H, 0:1], self.u16_f[:], self.activeT[:],
                       start=True, stop=True)
        it = self.inc(mm, "T")
        self.wait(V, "T", it)
        last = None
        for a in range(NA):
            last = V.scalar_tensor_tensor(self.sel[:, a:a + 1], self.psC[0:H, 0:1],
                                          float(a), self.activeT[:],
                                          ALU.is_equal, ALU.mult)
        iv = self.inc(last, "V")
        V.drain()
        s_b = _ap(self.sel, 0, [[NA, H], [0, 8], [1, NA]])
        m_b = _ap(self.selmask_f, 0, [[32, H], [NA, 8], [1, NA]])
        o_b = _ap(self.selp, 0, [[32, H], [NA, 8], [1, NA]])
        iv = self.inc(V.tensor_tensor(o_b, s_b, m_b, ALU.mult), "V")
        self.wait(TE, "V", iv)
        mm = TE.matmul(self.psD[:, 0:32], self.lpar_f[:], self.selp[:],
                       start=True, stop=True)
        it = self.inc(mm, "T")
        self.wait(V, "T", it)
        last = None
        for c in range(8):
            src = _ap(self.psD, c * 4, [[512, 128], [1, NA], [0, 64]])
            i2b = _ap(self.i2_f, 0, [[64, 128], [0, NA], [1, 64]])
            dst = _ap(self.G, c * 256, [[8 * 256, 128], [64, NA], [1, 64]])
            last = V.tensor_tensor(dst, src, i2b, ALU.mult)
        self.inc(last, "V")
        self.wait(TE, "V")
        if "d_r" in self.dbg:
            self.wait(sy, "V")
            self.inc(sy.dma_start(out=self.dbg["d_r"].ap()[b][None, :],
                                  in_=self.r_row[:]), "dX")
        if "d_sel" in self.dbg:
            self.wait(sy, "V")
            self.inc(sy.dma_start(out=self.dbg["d_sel"].ap()[b], in_=self.sel[:]),
                     "dX")

    def _phase3(self, b):
        nc = self.nc
        sy, gp = nc.sync, nc.gpsimd
        TE, SC, V = nc.tensor, nc.scalar, nc.vector
        for g in range(self.NG):
            for dt_ in range(4):
                t = g * 4 + dt_

                def evac(gg, ps, dt_=dt_):
                    out = _ap(self.x1T, (4 * gg) * 512 + dt_ * 128,
                              [[8 * 512, 128], [512, 4], [1, 128]])
                    if gg % 2 == 0:
                        return ("Sc", self.inc(
                            SC.activation(out, ps[:], AF.Copy), "Sc"))
                    self.wait(V, "T", self.cnt["T"])
                    return ("V", self.inc(V.tensor_copy(out, ps[:]), "V"))

                self._tr8(self.x1[:, t, :], evac,
                          banks=[self.psA, self.psB, self.psG, self.psH])
            self.wait(TE, "Sc")
            self.wait(TE, "V")
            for ti, (wsb, bias, kind) in enumerate([
                    (self.wq2, self.bq2_fm, "q"),
                    (self.wk2, self.bk2_fm, "k"),
                    (self.wv2, self.bv2_fm, "v")]):
                self.wait(TE, "V")
                sname, scnt = self._w2_done[kind]
                self.wait(TE, sname, scnt)
                evacd = {}
                for hc in range(9):
                    if hc < 8:
                        ps = [self.psA, self.psB][hc % 2]
                        if hc >= 2:
                            self.wait(TE, "V", evacd[hc - 2])
                        mm = None
                        for c in range(8):
                            mm = TE.matmul(ps[:],
                                           wsb[:, c, hc * 128:(hc + 1) * 128],
                                           self.x1T[:, c, :],
                                           start=(c == 0), stop=(c == 7))
                        it = self.inc(mm, "T")
                        self.wait(V, "T", it)
                        evacd[hc] = self.inc(
                            V.tensor_scalar(self.qkv_full[:, hc, :], ps[:],
                                            bias[:, hc:hc + 1], None, ALU.add),
                            "V")
                    cg = hc - 1
                    if cg >= 0:
                        self.wait(TE, "V", evacd[cg])
                        if cg == 0:
                            self.wait(TE, "Sc", getattr(self, "_gps_free", 0))
                        for j in range(2):
                            psj = [self.psC, self.psD][j]
                            mm = TE.matmul(psj[:],
                                           self.G[:, cg, j * 128:(j + 1) * 128],
                                           self.qkv_full[:, cg, :],
                                           start=(cg == 0), stop=(cg == 7),
                                           skip_group_check=True)
                            if cg == 7:
                                self.inc(mm, "T")
                for j in range(2):
                    ps = [self.psC, self.psD][j]
                    it = self.cnt["T"]
                    if kind == "v":
                        self.wait(SC, "T", it)
                        isc = self.inc(SC.activation(self.vsfm[:], ps[:],
                                                     AF.Copy), "Sc")
                        self.wait(TE, "Sc", isc)
                        tr = None
                        for dt_ in range(4):
                            out_bf = self.psE[:, dt_ * 64:(dt_ + 1) * 64].bitcast(
                                BF16)
                            tr = TE.transpose(out_bf,
                                              self.vsfm[:, dt_ * 128:(dt_ + 1) * 128],
                                              self.id_bf[:])
                        it2 = self.inc(tr, "T")
                        self.wait(SC, "T", it2)
                        isc = None
                        for dt_ in range(4):
                            t = g * 4 + dt_
                            dst = _ap(self.Vs, t * NA * 65 + (2 * j) * 65,
                                      [[self.NT * NA * 65, 128], [65, 2], [1, 64]])
                            srcb = self.psE[:, dt_ * 64:(dt_ + 1) * 64].bitcast(
                                BF16)
                            src2 = bass.AP(tensor=srcb.tensor, offset=srcb.offset,
                                           ap=[list(srcb.ap[0]), [64, 2], [1, 64]])
                            isc = self.inc(SC.activation(dst, src2, AF.Copy),
                                           "Sc")
                        self.wait(TE, "Sc", isc)
                        self._gps_free = self.cnt["Sc"]
                    else:
                        dsts = self.Qs if kind == "q" else self.Ks
                        scale = 0.125 if kind == "q" else 1.0
                        self.wait(SC, "T", it)
                        self.inc(SC.activation(
                            dsts[2 * j][0:64, g * 512:(g + 1) * 512],
                            ps[0:64, :], AF.Copy, scale=scale), "Sc")
                        self.wait(SC, "dW", getattr(self, "_shift_idm", 0))
                        isc = self.inc(SC.activation(
                            self.shiftb[64:128, :], ps[64:128, :], AF.Copy,
                            scale=scale), "Sc")
                        self.wait(gp, "Sc", isc)
                        idm = self.inc(gp.dma_start(
                            out=dsts[2 * j + 1][0:64, g * 512:(g + 1) * 512],
                            in_=self.shiftb[64:128, :]), "dW")
                        gp.wait_ge(self.sem["dW"], idm)
                        self._shift_idm = idm
                        self._gps_free = self.cnt["Sc"]

    def _bounds(self, b):
        nc = self.nc
        sy = nc.sync
        TE, SC, V = nc.tensor, nc.scalar, nc.vector
        S = self.S
        nch = S // 512
        V.memset(self.sqb[0][64:65, :], 1.0)
        V.memset(self.sqb[1][64:65, :], 1.0)
        units = []
        for a in range(NA):
            units.append((a, "k", self.Ks[a]))
            units.append((a, "q", self.Qs[a]))
        mm_done, ln_done, exp_done, fin_done = {}, {}, {}, {}
        for u, (a, which, ten) in enumerate(units):
            sq = self.sqb[u % 2]
            nac = self.nac2[u % 2]
            pb = [self.psA, self.psB] if u % 2 == 0 else [self.psC, self.psD]
            self.wait(V, "T", mm_done.get(u - 2, 0))
            iv = self.inc(V.tensor_tensor(sq[0:64, :], ten[0:64, :],
                                          ten[0:64, :], ALU.mult), "V")
            self.wait(TE, "V", iv)
            self.wait(TE, "Sc", ln_done.get(u - 2, 0))
            mm = None
            for ch in range(nch):
                bp = (ch // 2) * 32
                mm = TE.matmul(pb[ch % 2][bp:bp + 1, 0:512], self.ones65_bf[:],
                               sq[:, ch * 512:(ch + 1) * 512],
                               start=True, stop=True)
            mm_done[u] = self.inc(mm, "T")
            self.wait(SC, "T", mm_done[u])
            self.wait(SC, "V", fin_done.get(u - 2, 0))
            for ch in range(nch):
                bp = (ch // 2) * 32
                SC.activation(nac[0:1, ch * 512:(ch + 1) * 512],
                              pb[ch % 2][bp:bp + 1, 0:512], AF.Ln)
            SC.drain()
            exp_done[u] = self.inc(SC.activation(nac[:], nac[:], AF.Exp,
                                                 scale=0.5), "Sc")
            ln_done[u] = exp_done[u]
            self.wait(V, "Sc", exp_done[u])
            if which == "k":
                iv = self.inc(V.reduce_max(self.kmax_row[0:1, a:a + 1],
                                           nac[:], AX.X), "V")
                V.drain()
            else:
                iv = self.inc(V.tensor_scalar(
                    ten[64:65, :], nac[:],
                    self.kmax_row[0:1, a:a + 1], -1.0,
                    ALU.mult, ALU.mult), "V")
                V.drain()
                V.memset(self.Ks[a][64:65, :], 1.0)
                self.inc(V.memset(self.vmax8[0:1, 1:2], 0.0), "V")
            fin_done[u] = self.cnt["V"]
        self.wait(TE, "V")
        gp = self.nc.gpsimd
        if "d_qs" in self.dbg:
            self.wait(gp, "V")
            idm = None
            for a in range(NA):
                self.inc(gp.dma_start(out=self.dbg["d_qs"].ap()[b, a],
                                      in_=self.Qs[a][:]), "dW")
                idm = self.inc(gp.dma_start(out=self.dbg["d_ks"].ap()[b, a],
                                            in_=self.Ks[a][:]), "dW")
            gp.wait_ge(self.sem["dW"], idm)
        if "d_vs" in self.dbg:
            self.wait(gp, "V")
            idm = None
            for t in range(self.NT):
                idm = self.inc(gp.dma_start(
                    out=self.dbg["d_vs"].ap()[b, t * 128:(t + 1) * 128, :],
                    in_=self.Vs[:, t, :]), "dW")
            gp.wait_ge(self.sem["dW"], idm)

    def _phase45(self, b, last=False):
        nc = self.nc
        sy = nc.sync
        TE, SC, V = nc.tensor, nc.scalar, nc.vector
        S, NT = self.S, self.NT
        NQC = S // 1024
        dO_tile = {}
        self.wait(TE, "dW")
        for qc in range(min(NQC, self.p45_qc)):
            otn_done = {}
            for a in range(NA):
                exp_done = {}
                pv_done = {}
                psO_pair = [[self.psE, self.psF], [self.psG, self.psH]][a % 2]
                if a < 2:
                    self.wait(TE, "V")
                sbank = [self.psA, self.psB, self.psC, self.psD]
                for kt in range(NT + 1):
                    if kt < NT:
                        for hf in range(2):
                            ei = 2 * kt + hf
                            psS = sbank[ei % 4]
                            self.wait(TE, "Sc", exp_done.get(
                                ei - 4, getattr(self, "_rep_sc", 0)))
                            mm = TE.matmul(
                                psS[:],
                                self.Ks[a][:, kt * 128:(kt + 1) * 128],
                                self.Qs[a][:, qc * 1024 + hf * 512:
                                           qc * 1024 + (hf + 1) * 512],
                                start=True, stop=True)
                            it = self.inc(mm, "T")
                            self.wait(SC, "T", it)
                            self.wait(SC, "T", pv_done.get(kt - 2, 0))
                            isc = self.inc(SC.activation(
                                self.Pt[kt % 2][:, hf * 512:(hf + 1) * 512],
                                psS[:], AF.Exp), "Sc")
                            exp_done[ei] = isc
                    kp = kt - 1
                    if kp >= 0:
                        self.wait(TE, "Sc", exp_done[2 * kp + 1])
                        if kp == 0 and a >= 2:
                            self.wait(TE, "V", otn_done.get(a - 2, 0))
                        mm2 = None
                        for hf in range(2):
                            psO = psO_pair[hf]
                            mm2 = TE.matmul(
                                psO[0:65, :],
                                self.Vs[:, kp, a * 65:(a + 1) * 65],
                                self.Pt[kp % 2][:, hf * 512:(hf + 1) * 512],
                                start=(kp == 0), stop=(kp == NT - 1),
                                skip_group_check=True)
                        pv_done[kp] = self.inc(mm2, "T")
                self.wait(SC, "T")
                with nc.allow_low_precision(reason="f32r operand prep"):
                    for hf in range(2):
                        psO = psO_pair[hf]
                        SC.activation(self.nrm_f[0:1, hf * 512:(hf + 1) * 512],
                                      psO[64:65, :], AF.Ln)
                    SC.drain()
                    isc = self.inc(SC.activation(self.rec_row[:],
                                                 self.nrm_f[:], AF.Exp,
                                                 scale=-1.0), "Sc")
                self.wait(TE, "Sc", isc)
                psR_pair = [self.psC, self.psD]
                last_mm = None
                for hf in range(2):
                    last_mm = TE.matmul(psR_pair[hf][0:64, :], self.onesr64_r[:],
                                        self.rec_row[0:1, hf * 512:(hf + 1) * 512],
                                        start=True, stop=True)
                it = self.inc(last_mm, "T")
                self.wait(SC, "T", it)
                self.wait(SC, "V", otn_done.get(a - 1, 0))
                isc = None
                for hf in range(2):
                    isc = self.inc(SC.activation(
                        self.rep_sb[:, hf * 512:(hf + 1) * 512],
                        psR_pair[hf][0:64, :], AF.Copy), "Sc")
                self._rep_sc = isc
                self.wait(V, "Sc", isc)
                self.wait(V, "T", pv_done[NT - 1])
                last_v = None
                for hf in range(2):
                    psO = psO_pair[hf]
                    with nc.allow_low_precision(reason="f32r operand prep"):
                        last_v = V.tensor_tensor(
                            self.OTn[0:64, a, hf * 512:(hf + 1) * 512],
                            psO[0:64, :],
                            self.rep_sb[:, hf * 512:(hf + 1) * 512], ALU.mult)
                otn_done[a] = self.inc(last_v, "V")
                if "d_ot" in self.dbg:
                    gp = nc.gpsimd
                    self.wait(gp, "V")
                    idm = self.inc(gp.dma_start(
                        out=self.dbg["d_ot"].ap()[b, a, :,
                                                  qc * 1024:(qc + 1) * 1024],
                        in_=self.OTn[0:64, a, :]), "dW")
                    gp.wait_ge(self.sem["dW"], idm)
                    self.wait(V, "dW", idm)
            self.wait(TE, "V")

            if self.p45_level < 2:
                continue
            res_done = {}
            v_at_start = self.cnt["V"]
            for dt_ in range(8):
                t = qc * 8 + dt_
                opair = [[self.psA, self.psB], [self.psC, self.psD]][dt_ % 2]
                self.wait(TE, "V", res_done.get(dt_ - 2, v_at_start))
                for half in range(2):
                    ps = opair[half]
                    for a in range(NA):
                        TE.matmul(ps[:],
                                  self.OTn[0:64, a, dt_ * 128:(dt_ + 1) * 128],
                                  self.wo2r[:, a, half * 512:(half + 1) * 512],
                                  start=(a == 0), stop=False)
                    mm = TE.matmul(ps[:], self.ones2_bf[:],
                                   self.bo2_hl[:, half * 512:(half + 1) * 512],
                                   start=False, stop=True)
                    self.inc(mm, "T")
                so = "dO%d" % (t % 2)
                self.wait(V, so, dO_tile.get(t - 2, 0))
                self.wait(V, "T")
                for half in range(2):
                    psO = opair[half]
                    V.scalar_tensor_tensor(
                        self.x1[:, t, half * 512:(half + 1) * 512], psO[:], 0.0,
                        self.x1[:, t, half * 512:(half + 1) * 512],
                        ALU.add, ALU.add,
                        accum_out=self.sums[:, half:half + 1])
                self.inc(V.memset(self.sums[:, 8:9], 0.0), "V")
                V.drain()
                self._ln_finalize(self.x1[:, t, :], self.sums[:, 0:1],
                                  self.sums[:, 1:2], self.x1[:, t, :])
                res_done[dt_] = self.cnt["V"]
                self.wait(sy, "V")
                dO_tile[t] = self.inc(
                    sy.dma_start(out=self.out_h.ap()[b, t * 128:(t + 1) * 128, :],
                                 in_=self.x1[:, t, :]), so)
        if not last:
            self._wload_b1(first=False)


def _pack_inputs(inputs):
    g = lambda n: np.ascontiguousarray(np.asarray(inputs[n], np.float32))
    m = {}
    m["a_wq"] = g("L0_attn_wq").reshape(D, D).copy()
    m["a_wk"] = g("L0_attn_wk").reshape(D, D).copy()
    m["a_wv"] = g("L0_attn_wv").reshape(D, D).copy()
    m["a_wo"] = g("L0_attn_wo").reshape(D, D).copy()
    m["a_bq"] = g("L0_attn_bq").reshape(D).copy()
    m["a_bk"] = g("L0_attn_bk").reshape(D).copy()
    m["a_bv"] = g("L0_attn_bv").reshape(D).copy()
    m["a_bo"] = g("L0_attn_bo").reshape(D).copy()
    m["f_wq"] = g("L0_ffn_Wq")
    m["f_wk"] = g("L0_ffn_Wk")
    m["f_wv"] = g("L0_ffn_Wv")
    m["f_wo"] = g("L0_ffn_Wo")
    m["f_bq"] = g("L0_ffn_bq")
    m["f_bk"] = g("L0_ffn_bk")
    m["f_bv"] = g("L0_ffn_bv")
    m["f_bo"] = g("L0_ffn_bo")
    m["f_wr"] = g("L0_ffn_Wr")
    m["f_br"] = g("L0_ffn_br")
    return m


def kernel(**inputs):
    x = np.ascontiguousarray(np.asarray(inputs["L0_attn_x"], np.float32))
    B, S, _ = x.shape
    nb = B // N_CORES
    prog = Prog(S=S, NB=nb)
    wm = _pack_inputs(inputs)
    in_maps = []
    for core in range(N_CORES):
        im = dict(wm)
        im["x"] = np.ascontiguousarray(x[core * nb:(core + 1) * nb])
        in_maps.append(im)
    res = run_bass_kernel_spmd(prog.nc, in_maps, core_ids=list(range(N_CORES)))
    out = np.concatenate([r["out"] for r in res.results], axis=0)
    return out.astype(np.float32)


# version 12
# speedup vs baseline: 1.0278x; 1.0204x over previous
import numpy as np

import concourse.bass as bass
import concourse.mybir as mybir
from concourse.bass_utils import run_bass_kernel_spmd

F32 = mybir.dt.float32
F32R = mybir.dt.float32r
BF16 = mybir.dt.bfloat16
AF = mybir.ActivationFunctionType
ALU = mybir.AluOpType
AX = mybir.AxisListType

D = 1024
H = 16
HD = 64
NA = 4
EPS_ATTN = 1e-6
EPS_LN = 1e-5
N_CORES = 8

WNAMES = ["a_wq", "a_wk", "a_wv", "a_wo", "a_bq", "a_bk", "a_bv", "a_bo",
          "f_wq", "f_wk", "f_wv", "f_wo", "f_bq", "f_bk", "f_bv", "f_bo",
          "f_wr", "f_br"]


def _ap(t, offset, dims):
    return bass.AP(tensor=t[:].tensor, offset=offset, ap=[list(d) for d in dims])


class Prog:
    def __init__(self, S=2048, NB=2, debug=(), stop_after=None, p2_level=9,
                 p2y_level=9, p45_level=9, p45_qc=99):
        assert S % 1024 == 0
        self.stop_after = stop_after
        self.p2_level = p2_level
        self.p2y_level = p2y_level
        self.p45_level = p45_level
        self.p45_qc = p45_qc
        self.S, self.NB = S, NB
        self.NT = S // 128
        self.NG = self.NT // 4
        self.debug = set(debug)
        nc = bass.Bass("TRN2", target_bir_lowering=False)
        self.nc = nc

        dt = nc.dram_tensor
        self.x_h = dt("x", [NB, S, D], F32, kind="ExternalInput")
        self.out_h = dt("out", [NB, S, D], F32, kind="ExternalOutput")
        shp = {"a_wq": [D, D], "a_wk": [D, D], "a_wv": [D, D], "a_wo": [D, D],
               "a_bq": [D], "a_bk": [D], "a_bv": [D], "a_bo": [D],
               "f_wq": [D, D], "f_wk": [D, D], "f_wv": [D, D],
               "f_wo": [NA * HD, D], "f_bq": [D], "f_bk": [D], "f_bv": [D],
               "f_bo": [D], "f_wr": [D, H], "f_br": [H]}
        self.w = {n: dt(n, shp[n], F32, kind="ExternalInput") for n in WNAMES}
        self.wo1r_h = dt("wo1r", [D, D], F32R, kind="Internal")
        self.dbg = {}
        for name, s in [("d_qm", [NB, 128, 8, 512]), ("d_ynt", [NB, 128, 8, 512]),
                        ("d_xg", [NB, 8, D]), ("d_km", [NB, 128, D]),
                        ("d_vaug", [NB, 128, H * 65]), ("d_xsum", [NB, D]),
                        ("d_kv", [NB, 128, 16 * 65]),
                        ("d_x1", [NB, S, D]), ("d_r", [NB, H]),
                        ("d_sel", [NB, H, NA]),
                        ("d_qs", [NB, NA, 65, S]), ("d_ks", [NB, NA, 65, S]),
                        ("d_vs", [NB, S, NA * 65]), ("d_ot", [NB, NA, 64, S])]:
            if name in self.debug:
                self.dbg[name] = dt(name, s, F32, kind="ExternalOutput")

        it = nc.inline_tensor
        self.c_id = it(np.eye(128, dtype=np.float32), "c_id")
        self.c_ones2 = it(np.ones((2, 128), np.float32), "c_ones2")
        self.c_ones64 = it(np.ones((64, 1), np.float32), "c_ones64")
        o65 = np.ones((65, 1), np.float32); o65[64, 0] = 1e-30
        self.c_ones65 = it(o65, "c_ones65")
        self.c_ones128 = it(np.ones((128, 1), np.float32), "c_ones128")
        self.c_onesr64 = it(np.ones((1, 64), np.float32), "c_onesr64")
        self.c_u16 = it(np.triu(np.ones((H, H), np.float32), 1).copy(), "c_u16")
        lpar = np.zeros((H, 128), np.float32)
        for h in range(H):
            lpar[h, (h % 2) * 64:(h % 2) * 64 + 64] = 1.0
        self.c_lpar = it(lpar, "c_lpar")
        selmask = np.zeros((H, 32), np.float32)
        for h in range(H):
            selmask[h, (h // 2) * 4:(h // 2) * 4 + 4] = 1.0
        self.c_selmask = it(selmask, "c_selmask")
        self.c_i2 = it(np.vstack([np.eye(64, dtype=np.float32)] * 2).copy(), "c_i2")

        self._tr_rot = 0
        self._sbuf()
        self._sems()
        self._emit()

    def _sbuf(self):
        nc = self.nc
        S, NT = self.S, self.NT
        a = nc.alloc_sbuf_tensor

        def at(name, shape, dtp, off):
            return nc.alloc_sbuf_tensor_at(name, shape, dtp, offset=off)

        self.id_f = a("id_f", [128, 128], F32)
        self.id_bf = a("id_bf", [128, 128], BF16)
        self.ones2_bf = a("ones2_bf", [2, 128], BF16)
        self.ones64_bf = a("ones64_bf", [64, 1], BF16)
        self.ones65_bf = a("ones65_bf", [65, 1], BF16)
        self.onesr64_r = a("onesr64_r", [1, 64], F32R)
        self.ones128_f = a("ones128_f", [128, 1], F32)
        self.lpar_f = a("lpar_f", [H, 128], F32)
        self.selmask_f = a("selmask_f", [H, 32], F32)
        self.selp = a("selp", [H, 32], F32)
        self.i2_f = a("i2_f", [128, 64], F32)
        self.u16_f = a("u16_f", [H, H], F32)

        self.x1 = a("x1", [128, NT, D], F32)
        self.Vs = a("Vs", [128, NT, NA * 65], BF16)
        self.kv_z = a("kv_z", [128, H, 65], BF16)
        self.esc = a("esc", [128, 512], F32)
        self.esc2 = a("esc2", [128, 512], F32)
        self.sums = a("sums", [128, 10], F32)
        self.stat = a("stat", [128, 6], F32)
        self.wr_f = a("wr_f", [128, 8, H], F32)
        self.xsum = a("xsum", [1, D], F32)
        self.xsum_fm = a("xsum_fm", [128, 8], F32)
        self.nrm_f = a("nrm_f", [1, D], F32)

        self.bq1_fm = a("bq1_fm", [128, 8], F32)
        self.bq2_fm = a("bq2_fm", [128, 8], F32)
        self.bk2_fm = a("bk2_fm", [128, 8], F32)
        self.bv2_fm = a("bv2_fm", [128, 8], F32)
        self.bk1_hl = a("bk1_hl", [2, D], BF16)
        self.bv1_hl = a("bv1_hl", [2, D], BF16)
        self.bo1_hl = a("bo1_hl", [2, D], BF16)
        self.bo2_hl = a("bo2_hl", [2, D], BF16)
        self.br_row = a("br_row", [1, H], F32)

        self.r_row = a("r_row", [1, H], F32)
        self.z16 = a("z16", [128, H], F32)
        self.vmax8 = a("vmax8", [1, 8], F32)
        self.active = a("active", [1, H], F32)
        self.activeT = a("activeT", [H, 1], F32)
        self.sel = a("sel", [H, NA], F32)
        self.kmax_row = a("kmax_row", [1, NA], F32)

        rqk = nc.bump_sbuf(NA * 2 * S * 2)[0]
        self.Qs = [at(f"Qs{i}", [65, S], BF16, rqk + i * (S * 2)) for i in range(NA)]
        self.Ks = [at(f"Ks{i}", [65, S], BF16, rqk + (NA + i) * (S * 2))
                   for i in range(NA)]

        ph = nc.bump_sbuf(78336)[0]
        self.ph = ph
        self.wk1 = at("wk1", [128, 8, D], BF16, ph + 0)
        self.wv1 = at("wv1", [128, 8, D], BF16, ph + 16384)
        self.xin = [at(f"xin{i}", [128, D], F32, ph + 32768 + i * 4096)
                    for i in range(2)]
        self.xin_bf = [at(f"xinb{i}", [128, D], BF16, ph + 32768 + i * 2048)
                       for i in range(4)]
        self.km = [at(f"km{i}", [128, D], BF16,
                      ph + (40960 if i == 0 else 65664)) for i in range(2)]
        self.vaug = [at(f"vaug{i}", [128, H * 65], BF16,
                        ph + (43008 if i == 0 else 67712)) for i in range(2)]
        self.xt1 = [at(f"xt1_{i}", [128, 8, 128], BF16, ph + 45184 + i * 2048)
                    for i in range(2)]
        self.wq1 = at("wq1", [128, 8, D], BF16, ph + 49280)
        self.xTg = at("xTg", [128, 8, 512], BF16, ph + 0)
        self.qm = at("qm", [128, 8, 512], BF16, ph + 8192)
        self.ynTg = at("ynTg", [128, 8, 512], F32R, ph + 16384)
        self.wo1_ring = [at(f"wo1g{i}", [128, 512], F32R,
                            ph + (32768 + i * 2048 if i < 4 else
                                  69824 + (i - 4) * 2048)) for i in range(8)]
        self.ynorm = [at(f"ynorm{i}", [128, D], F32,
                         ph + (40960 if i == 0 else 45056)) for i in range(2)]
        self.wq2 = at("wq2", [128, 8, D], BF16, ph + 0)
        self.wk2 = at("wk2", [128, 8, D], BF16, ph + 16384)
        self.wv2 = at("wv2", [128, 8, D], BF16, ph + 32768)
        self.x1T = at("x1T", [128, 8, 512], BF16, ph + 49152)
        self.qkv_full = at("qkv_full", [128, 8, 512], BF16, ph + 57344)
        self.shiftb = at("shiftb", [128, 512], BF16, ph + 65536)
        self.vsfm = at("vsfm", [128, 512], BF16, ph + 66560)
        self.G = at("G", [128, 8, 256], BF16, ph + 67584)
        self.sqb = [at(f"sqb{i}", [65, S], BF16, ph + i * 4096)
                    for i in range(2)]
        self.nac2 = [at(f"nac2_{i}", [1, S], F32, ph + 8192 + i * 8192)
                     for i in range(2)]
        self.OTn = at("OTn", [64, NA, D], F32R, ph + 0)
        self.Pt = [at(f"Pt{i}", [128, D], BF16, ph + 16384 + i * 2048)
                   for i in range(2)]
        self.rep_sb = at("rep_sb", [64, D], F32, ph + 20480)
        self.rec_row = at("rec_row", [1, D], F32R, ph + 24576)
        self.wo2r = at("wo2r", [64, NA, D], F32R, ph + 28672)
        self.wk1n = at("wk1n", [128, 8, D], BF16, ph + 45056)
        self.wv1n = at("wv1n", [128, 8, D], BF16, ph + 61440)
        self.brow = at("brow", [1, D], F32, ph + 0)
        self.blo_f = at("blo_f", [1, D], F32, ph + 4096)

        p = nc.alloc_psum_tensor
        (self.psA, self.psB, self.psC, self.psD, self.psE, self.psF,
         self.psG, self.psH) = [p(f"ps{i}", [128, 512], F32) for i in range(8)]

    def _sems(self):
        self.sem, self.cnt = {}, {}
        for n in ["dW", "dX", "dX0", "dX1", "dX2", "dX3",
                  "dB0", "dB1", "dB2",
                  "dG0", "dG1", "dG2", "dG3", "dO0", "dO1",
                  "dR0", "dR1", "dR2", "dR3", "dR4", "dR5", "dR6", "dR7",
                  "T", "V", "Sc"]:
            self.sem[n] = self.nc.alloc_semaphore(n)
            self.cnt[n] = 0

    def inc(self, instr, name):
        amt = 16 if name.startswith("d") else 1
        instr.then_inc(self.sem[name], amt)
        self.cnt[name] += amt
        return self.cnt[name]

    def wait(self, eng, name, val=None):
        eng.wait_ge(self.sem[name], self.cnt[name] if val is None else val)

    def _emit(self):
        nc = self.nc
        sa = self.stop_after
        with nc.Block():
            self._startup()
            nc.all_engine_barrier()
            self._wload_b1(first=True)
            self._w1_done = self.cnt["dW"]
            nc.all_engine_barrier()
            for b in range(self.NB):
                if sa == "startup":
                    break
                self._phase1(b)
                nc.all_engine_barrier()
                if sa == "p1":
                    break
                self._phase2(b)
                nc.all_engine_barrier()
                if sa == "p2":
                    break
                self._router(b)
                self._wload_b2()
                nc.all_engine_barrier()
                if sa == "router":
                    break
                self._phase3(b)
                nc.all_engine_barrier()
                if sa == "p3":
                    break
                self._wload_wo2()
                self._bounds(b)
                nc.all_engine_barrier()
                if sa == "bounds":
                    break
                self._phase45(b, last=(b == self.NB - 1))
                nc.all_engine_barrier()
                if b + 1 < self.NB:
                    gp = nc.gpsimd
                    self.inc(gp.dma_start(out=self.wk1[:], in_=self.wk1n[:]), "dW")
                    self.inc(gp.dma_start(out=self.wv1[:], in_=self.wv1n[:]), "dW")
                    gp.wait_ge(self.sem["dW"], self.cnt["dW"])
                    self._wload_wq1()
                    self._w1_done = self.cnt["dW"]
                    nc.all_engine_barrier()
            for s in ["dX", "dX0", "dX1", "dX2", "dX3", "dO0", "dO1",
                      "dR0", "dR1", "dR2",
                      "dR3", "dR4", "dR5", "dR6", "dR7"]:
                self.wait(nc.sync, s)
            for s in ["dG0", "dG1", "dG2", "dG3", "dB0", "dB1", "dB2"]:
                self.wait(nc.sync, s)
            self.wait(nc.gpsimd, "dW")

    def _startup(self):
        nc = self.nc
        sy, gp, V = nc.sync, nc.gpsimd, nc.vector
        w = self.w
        for dst, src in [(self.id_f, self.c_id), (self.ones128_f, self.c_ones128),
                         (self.lpar_f, self.c_lpar),
                         (self.selmask_f, self.c_selmask),
                         (self.i2_f, self.c_i2), (self.u16_f, self.c_u16)]:
            self.inc(sy.dma_start(out=dst[:], in_=src.ap()), "dX")
        for dst, src in [(self.id_bf, self.c_id), (self.ones2_bf, self.c_ones2),
                         (self.ones64_bf, self.c_ones64),
                         (self.ones65_bf, self.c_ones65),
                         (self.onesr64_r, self.c_onesr64)]:
            self.inc(gp.dma_start(out=dst[:], in_=src.ap()), "dW")
        self.inc(gp.dma_start(out=self.wo1r_h.ap(), in_=w["a_wo"].ap()), "dW")
        self.inc(sy.dma_start(
            out=self.wr_f[:],
            in_=w["f_wr"].ap().rearrange("(p c) h -> p c h", p=128)), "dX")
        for src, dst in [("a_bq", self.bq1_fm), ("f_bq", self.bq2_fm),
                         ("f_bk", self.bk2_fm), ("f_bv", self.bv2_fm)]:
            self.inc(sy.dma_start(
                out=dst[:], in_=w[src].ap().rearrange("(c p) -> p c", p=128),
                allow_slow_non_contiguous=True), "dX")
        self.inc(sy.dma_start(out=self.br_row[:], in_=w["f_br"].ap()[None, :]),
                 "dX")
        for src, dst in [("a_bk", self.bk1_hl), ("a_bv", self.bv1_hl),
                         ("a_bo", self.bo1_hl), ("f_bo", self.bo2_hl)]:
            self.inc(sy.dma_start(out=self.brow[:], in_=w[src].ap()[None, :]),
                     "dX")
            self.wait(V, "dX")
            self.wait(V, "dW")
            V.tensor_copy(dst[0:1, :], self.brow[:])
            V.drain()
            iv = self.inc(V.tensor_sub(self.blo_f[:], self.brow[:], dst[0:1, :]),
                          "V")
            self.wait(gp, "V", iv)
            self.inc(gp.dma_start(out=dst[1:2, :], in_=self.blo_f[:]), "dW")
            sy.wait_ge(self.sem["V"], iv)
        self.wait(V, "dW")
        V.memset(_ap(self.Vs, 64, [[self.NT * NA * 65, 128], [65, self.NT * NA]]),
                 1.0)
        V.memset(self.sums[:], 0.0)
        self.inc(V.memset(self.stat[:], 0.0), "V")

    def _wload_b1(self, first=False):
        gp = self.nc.gpsimd
        for src, dst in [("a_wk", self.wk1 if first else self.wk1n),
                         ("a_wv", self.wv1 if first else self.wv1n)]:
            self.inc(gp.dma_start(
                out=dst[:],
                in_=self.w[src].ap().rearrange("(c p) f -> p c f", p=128)), "dW")
        if first:
            self._wload_wq1()

    def _wload_wq1(self):
        gp = self.nc.gpsimd
        self.inc(gp.dma_start(
            out=self.wq1[:],
            in_=self.w["a_wq"].ap().rearrange("(c p) f -> p c f", p=128)), "dW")

    def _wload_b2(self):
        gp = self.nc.gpsimd
        self._w2_done = {}
        for src, dst, kind, sname in [("f_wq", self.wq2, "q", "dB0"),
                                      ("f_wk", self.wk2, "k", "dB1"),
                                      ("f_wv", self.wv2, "v", "dB2")]:
            self.wait(gp, sname)
            cnt = self.inc(gp.dma_start(
                out=dst[:],
                in_=self.w[src].ap().rearrange("(c p) f -> p c f", p=128)), sname)
            self._w2_done[kind] = (sname, cnt)

    def _wload_wo2(self):
        gp = self.nc.gpsimd
        self.inc(gp.dma_start(
            out=self.wo2r[:],
            in_=self.w["f_wo"].ap().rearrange("(a d) f -> d a f", d=HD)), "dW")

    def _tr8(self, src_f32, evac_list, banks=None):
        nc = self.nc
        TE = nc.tensor
        if banks is None:
            banks = [self.psG, self.psH]
        if not hasattr(self, "_tr_evac"):
            self._tr_evac = {}
        for g in range(2):
            bi = (self._tr_rot + g) % len(banks)
            ps = banks[bi]
            esem, ecnt = self._tr_evac.get(id(ps), ("Sc", 0))
            self.wait(TE, esem, ecnt)
            self.wait(TE, "V", getattr(self, "_psGH_vfree", 0))
            last = None
            for c4 in range(4):
                c = 4 * g + c4
                last = TE.transpose(ps[:, c4 * 128:(c4 + 1) * 128],
                                    src_f32[:, c * 128:(c + 1) * 128],
                                    self.id_f[:])
            it = self.inc(last, "T")
            self.nc.scalar.wait_ge(self.sem["T"], it)
            self._tr_evac[id(ps)] = evac_list(g, ps)
        self._tr_rot = (self._tr_rot + 2) % len(banks)

    def _phase1(self, b):
        nc = self.nc
        sy = nc.sync
        TE, SC, V = nc.tensor, nc.scalar, nc.vector
        NT = self.NT
        self.wait(TE, "dW", self._w1_done)
        V.memset(_ap(self.vaug[0], 64, [[H * 65, 128], [65, H]]), 1.0)
        V.memset(_ap(self.vaug[1], 64, [[H * 65, 128], [65, H]]), 1.0)
        self.inc(V.memset(self.sums[:, 7:8], 0.0), "V")
        v_done = self.cnt["V"]
        self._p1_vdone = {}
        gp = nc.gpsimd
        for t in range(NT):
            if t % 2 == 0:
                eng, sx = sy, "dX%d" % ((t // 2) % 4)
            else:
                eng, sx = gp, "dG%d" % ((t // 2) % 4)
            self.wait(eng, sx)
            self.wait(eng, "dO%d" % (t % 2))
            self.inc(eng.dma_start(
                out=self.x1[:, t, :],
                in_=self.x_h.ap()[b, t * 128:(t + 1) * 128, :]), sx)
            self.wait(TE, sx)
            self.wait(TE, "V", v_done)
            xt = self.xt1[t % 2]

            def evac(gg, ps, xt=xt):
                out = _ap(xt, 4 * gg * 128, [[1024, 128], [1, 512]])
                if gg % 2 == 0:
                    return ("Sc", self.inc(SC.activation(out, ps[:], AF.Copy),
                                           "Sc"))
                self.wait(V, "T", self.cnt["T"])
                return ("V", self.inc(V.tensor_copy(out, ps[:]), "V"))

            self._tr8(self.x1[:, t, :], evac)
            self.wait(TE, "Sc")
            self.wait(TE, "V")
            tmark = {}
            for ti, (wsb, bhl) in enumerate([(self.wk1, self.bk1_hl),
                                             (self.wv1, self.bv1_hl)]):
                for half in range(2):
                    ps = [self.psA, self.psB, self.psC, self.psD][2 * ti + half]
                    for c in range(8):
                        TE.matmul(ps[:], xt[:, c, :],
                                  wsb[:, c, half * 512:(half + 1) * 512],
                                  start=(c == 0), stop=False)
                    mm = TE.matmul(ps[:], self.ones2_bf[:],
                                   bhl[:, half * 512:(half + 1) * 512],
                                   start=False, stop=True)
                    tmark[(ti, half)] = self.inc(mm, "T")
            for half in range(2):
                ps = [self.psA, self.psB][half]
                self.wait(V, "T", tmark[(0, half)])
                V.tensor_scalar_min(self.esc[:], ps[:], 0.0)
                iv = self.inc(V.memset(self.sums[:, 7:8], 0.0), "V")
                self.wait(SC, "V", iv)
                isc = self.inc(SC.activation(self.esc2[:], self.esc[:], AF.Exp),
                               "Sc")
                self.wait(V, "Sc", isc)
                st = V.scalar_tensor_tensor(
                    self.km[t % 2][:, half * 512:(half + 1) * 512],
                    ps[:], 0.0, self.esc2[:], ALU.max, ALU.add)
                self.inc(st, "V")
            for half in range(2):
                ps = [self.psC, self.psD][half]
                self.wait(V, "T", tmark[(1, half)])
                out = _ap(self.vaug[t % 2], half * 8 * 65,
                          [[H * 65, 128], [65, 8], [1, 64]])
                self.inc(V.tensor_copy(out, ps[:]), "V")
            v_done = self.cnt["V"]
            self._p1_vdone[t] = v_done
            if t == 0 and "d_km" in self.dbg:
                gp = nc.gpsimd
                self.wait(gp, "V", v_done)
                self.inc(gp.dma_start(out=self.dbg["d_km"].ap()[b],
                                      in_=self.km[t % 2][:]), "dW")
                idm = self.inc(gp.dma_start(out=self.dbg["d_vaug"].ap()[b],
                                            in_=self.vaug[t % 2][:]), "dW")
                gp.wait_ge(self.sem["dW"], idm)
                self.wait(V, "dW", idm)
            if t > 0:
                self._p1_kv_accum(t - 1, NT)
        self._p1_kv_accum(NT - 1, NT)
        self.wait(V, "T")
        V.memset(self.kv_z[:], 0.0)
        V.drain()
        for bi, bank in enumerate([self.psE, self.psF]):
            oute = _ap(self.kv_z, (2 * 4 * bi) * 65,
                       [[H * 65, 64], [2 * 65, 4], [1, 65]])
            srce = _ap(bank, 0, [[512, 64], [128, 4], [1, 65]])
            V.tensor_copy(oute, srce)
            V.drain()
            oto = _ap(self.kv_z, (2 * 4 * bi + 1) * 65 + 64 * H * 65,
                      [[H * 65, 64], [2 * 65, 4], [1, 65]])
            srco = _ap(bank, 64 * 512, [[512, 64], [128, 4], [1, 65]])
            self.inc(V.tensor_copy(oto, srco), "V")
            V.drain()
        if "d_kv" in self.dbg:
            gp = nc.gpsimd
            self.wait(gp, "V")
            idm = self.inc(gp.dma_start(
                out=self.dbg["d_kv"].ap()[b],
                in_=self.kv_z[:].rearrange("p c k -> p (c k)")), "dW")
            gp.wait_ge(self.sem["dW"], idm)

    def _p1_kv_accum(self, t, NT):
        TE = self.nc.tensor
        self.wait(TE, "V", self._p1_vdone[t])
        last = None
        for h in range(H):
            bank = self.psE if h < 8 else self.psF
            slot = (h % 8) // 2
            bp = (h % 2) * 64
            last = TE.matmul(bank[bp:bp + 64, slot * 128:slot * 128 + 65],
                             self.km[t % 2][:, h * 64:(h + 1) * 64],
                             _ap(self.vaug[t % 2], h * 65,
                                 [[H * 65, 128], [1, 65]]),
                             start=(t == 0 and h % 8 < 2),
                             stop=(t == NT - 1), skip_group_check=True)
        self.inc(last, "T")

    def _phase2(self, b):
        nc = self.nc
        sy = nc.sync
        TE, SC, V = nc.tensor, nc.scalar, nc.vector
        NG = self.NG
        self.wait(TE, "dW", self._w1_done)
        self.qm_evac = {}
        self._ring_used = {}
        self._xsum_pending = None
        self._ynorm_v = {}
        self._ynT_done = {}
        V.memset(self.xsum[:], 0.0)
        self.inc(V.memset(self.sums[:], 0.0), "V")
        ring_cnt = 0
        for g in range(NG):
            for dt_ in range(4):
                t = g * 4 + dt_

                def evac(gg, ps, dt_=dt_):
                    out = _ap(self.xTg, (4 * gg) * 512 + dt_ * 128,
                              [[8 * 512, 128], [512, 4], [1, 128]])
                    if gg % 2 == 0:
                        return ("Sc", self.inc(
                            SC.activation(out, ps[:], AF.Copy), "Sc"))
                    self.wait(V, "T", self.cnt["T"])
                    return ("V", self.inc(V.tensor_copy(out, ps[:]), "V"))

                self._tr8(self.x1[:, t, :], evac)
            self.wait(TE, "Sc")
            self.wait(TE, "V")
            self._emit_xsum(b)
            if self.p2_level < 2:
                continue
            for hc in range(8):
                ps = [self.psA, self.psB, self.psC, self.psD][hc % 4]
                if hc >= 4:
                    self.wait(TE, "V", self.qm_evac[hc - 4])
                mm = None
                for c in range(8):
                    mm = TE.matmul(ps[:], self.wq1[:, c, hc * 128:(hc + 1) * 128],
                                   self.xTg[:, c, :], start=(c == 0), stop=(c == 7))
                it = self.inc(mm, "T")
                self.wait(SC, "T", it)
                self.wait(SC, "V", self.qm_evac.get(hc - 1, 0))
                isc = self.inc(SC.activation(self.esc2[:], ps[:], AF.Relu,
                                             bias=self.bq1_fm[:, hc:hc + 1]), "Sc")
                self.wait(V, "Sc", isc)
                V.tensor_scalar(self.esc[:], ps[:], self.bq1_fm[:, hc:hc + 1],
                                None, ALU.add)
                V.drain()
                V.tensor_sub(self.esc[:], self.esc[:], self.esc2[:])
                iv = self.inc(V.memset(self.sums[:, 7:8], 0.0), "V")
                self.wait(SC, "V", iv)
                isc = self.inc(SC.activation(self.esc[:], self.esc[:], AF.Exp),
                               "Sc")
                self.wait(V, "Sc", isc)
                self.qm_evac[hc] = self.inc(
                    V.tensor_add(self.qm[:, hc, :], self.esc[:], self.esc2[:]),
                    "V")
            self.wait(TE, "V")
            if self.p2_level < 3:
                continue
            for dt_ in range(4):
                self._p2_y_tile(dt_)
                if dt_ > 0:
                    self._p2_ynt(dt_ - 1)
            self._p2_ynt(3)
            if self.p2_level < 4:
                continue
            if g == 0 and "d_qm" in self.dbg:
                gpq = nc.gpsimd
                self.wait(gpq, "V")
                self.wait(gpq, "Sc")
                self.inc(gpq.dma_start(out=self.dbg["d_qm"].ap()[b],
                                       in_=self.qm[:]), "dW")
                idm = self.inc(gpq.dma_start(out=self.dbg["d_ynt"].ap()[b],
                                             in_=self.ynTg[:]), "dW")
                gpq.wait_ge(self.sem["dW"], idm)
                self.wait(TE, "dW", idm)
            self.wait(TE, "Sc")
            for half in range(2):
                for c in range(8):
                    slot = ring_cnt % 8
                    sr = "dR%d" % slot
                    self.wait(sy, sr)
                    self.wait(sy, "T", self._ring_used.get(slot, 0))
                    self.inc(sy.dma_start(
                        out=self.wo1_ring[slot][:],
                        in_=self.wo1r_h.ap()[c * 128:(c + 1) * 128,
                                             half * 512:(half + 1) * 512]), sr)
                    ring_cnt += 1
                    self.wait(TE, sr)
                    if c == 0:
                        self.wait(TE, "V")
                    for dt_ in range(4):
                        psO = [self.psA, self.psB, self.psC, self.psD][dt_]
                        mm = TE.matmul(
                            psO[:],
                            _ap(self.ynTg, (dt_ % 2) * 64 + c * 512,
                                [[8 * 512, 128], [1, 64]]).bitcast(F32R)
                            if False else
                            self.ynTg[:, c, dt_ * 128:(dt_ + 1) * 128],
                            self.wo1_ring[slot][:],
                            start=(c == 0), stop=False)
                        if c == 7:
                            mm = TE.matmul(psO[:], self.ones2_bf[:],
                                           self.bo1_hl[:, half * 512:(half + 1) * 512],
                                           start=False, stop=True)
                    if c == 7:
                        self.inc(mm, "T")
                    else:
                        self.inc(mm, "T")
                    self._ring_used[slot] = self.cnt["T"]
                if self.p2_level < 5:
                    continue
                self.wait(V, "T")
                for dt_ in range(4):
                    t = g * 4 + dt_
                    psO = [self.psA, self.psB, self.psC, self.psD][dt_]
                    st = V.scalar_tensor_tensor(
                        self.x1[:, t, half * 512:(half + 1) * 512], psO[:], 0.0,
                        self.x1[:, t, half * 512:(half + 1) * 512],
                        ALU.add, ALU.add,
                        accum_out=self.sums[:, 2 * dt_ + half:2 * dt_ + half + 1])
                self.inc(V.memset(self.sums[:, 8:9], 0.0), "V")
                V.drain()
            if self.p2_level < 6:
                continue
            for dt_ in range(4):
                t = g * 4 + dt_
                self._ln_finalize(self.x1[:, t, :],
                                  self.sums[:, 2 * dt_:2 * dt_ + 1],
                                  self.sums[:, 2 * dt_ + 1:2 * dt_ + 2],
                                  self.x1[:, t, :])
            self._ln_done = self.cnt["V"]
            self._psGH_vfree = self.cnt["V"]
            self._xsum_pending = g
            if "d_xg" in self.dbg:
                gpx = nc.gpsimd
                self.wait(gpx, "V")
                idm = self.inc(gpx.dma_start(
                    out=self.dbg["d_xg"].ap()[b, g][None, :], in_=self.xsum[:]),
                    "dW")
                gpx.wait_ge(self.sem["dW"], idm)
                self.wait(V, "dW", idm)
        self._emit_xsum(b)
        if "d_x1" in self.dbg:
            self.wait(sy, "V")
            for t in range(self.NT):
                self.inc(sy.dma_start(
                    out=self.dbg["d_x1"].ap()[b, t * 128:(t + 1) * 128, :],
                    in_=self.x1[:, t, :]), "dX")
            self.wait(sy, "dX")

    def _emit_xsum(self, b):
        if getattr(self, "_xsum_pending", None) is None:
            return
        g = self._xsum_pending
        self._xsum_pending = None
        nc = self.nc
        TE, V = nc.tensor, nc.vector
        self.wait(TE, "V", self._ln_done)
        self.wait(TE, "V", getattr(self, "_xsum_vfree", 0))
        last = None
        for dt_ in range(4):
            t = g * 4 + dt_
            for half in range(2):
                ps = [self.psE, self.psF][half]
                last = TE.matmul(ps[0:1, :], self.ones128_f[:],
                                 self.x1[:, t, half * 512:(half + 1) * 512],
                                 start=(dt_ == 0), stop=(dt_ == 3),
                                 skip_group_check=True)
        it = self.inc(last, "T")
        self.wait(V, "T", it)
        for half in range(2):
            ps = [self.psE, self.psF][half]
            V.tensor_add(self.xsum[0:1, half * 512:(half + 1) * 512],
                         self.xsum[0:1, half * 512:(half + 1) * 512],
                         ps[0:1, :])
        V.drain()
        self._xsum_vfree = self.inc(V.memset(self.sums[:, 8:9], 0.0), "V")

    def _p2_y_tile(self, dt_):
        nc = self.nc
        TE, SC, V = nc.tensor, nc.scalar, nc.vector
        lvl = getattr(self, "p2y_level", 9)
        if not hasattr(self, "_y_pair_free"):
            self._y_pair_free = {}
        pairs = [[self.psC, self.psD], [self.psA, self.psB]]
        marks = {}
        for hg in range(2):
            bpair = pairs[hg]
            self.wait(TE, "V", self._y_pair_free.get(hg, 0))
            last = None
            for hh in range(8):
                h = hg * 8 + hh
                bank = bpair[0] if hh < 4 else bpair[1]
                slot = hh % 4
                qm_lhs = self.qm[:, h // 2, dt_ * 128:(dt_ + 1) * 128]
                kv_rhs = self.kv_z[:, h, 0:65]
                last = TE.matmul(bank[:, slot * 128:slot * 128 + 65], qm_lhs,
                                 kv_rhs, start=True, stop=True)
            marks[hg] = self.inc(last, "T")
        banks4 = [(hg * 8 + bi * 4, pairs[hg][bi])
                  for hg in range(2) for bi in range(2)]
        self.wait(V, "T", marks[1])
        self.wait(V, "T", self._ynT_done.get(dt_ % 2, 0))
        if lvl >= 2:
            V.drain()
            for i, (h0, bank) in enumerate(banks4):
                V.tensor_scalar(self.z16[:, i * 4:(i + 1) * 4],
                                _ap(bank, 64, [[512, 128], [128, 4]]),
                                EPS_ATTN, None, ALU.add)
            V.drain()
            V.reciprocal(self.z16[:], self.z16[:])
            V.drain()
        iv = None
        for i, (h0, bank) in enumerate(banks4):
            dst = _ap(self.ynorm[dt_ % 2], h0 * 64,
                      [[1024, 128], [64, 4], [1, 64]])
            src = _ap(bank, 0, [[512, 128], [128, 4], [1, 64]])
            if lvl < 2:
                iv = self.inc(V.tensor_copy(dst, src), "V")
            else:
                zb = _ap(self.z16, i * 4, [[H, 128], [1, 4], [0, 64]])
                iv = self.inc(V.tensor_tensor(dst, src, zb, ALU.mult), "V")
            if i == 1:
                self._y_pair_free[0] = iv
        self._y_pair_free[1] = iv
        self._ynorm_v[dt_] = self.cnt["V"]

    def _p2_ynt(self, dt_):
        nc = self.nc
        TE, SC = nc.tensor, nc.scalar

        def evac(gg, ps, dt_=dt_):
            out = _ap(self.ynTg, (4 * gg) * 512 + dt_ * 128,
                      [[8 * 512, 128], [512, 4], [1, 128]])
            with nc.allow_low_precision(reason="f32r operand prep"):
                return ("Sc", self.inc(SC.activation(out, ps[:], AF.Copy),
                                       "Sc"))

        self.wait(TE, "V", self._ynorm_v[dt_])
        self._tr8(self.ynorm[dt_ % 2][:], evac)
        self._ynT_done[dt_ % 2] = self.cnt["T"]

    def _ln_finalize(self, xpre, s0, s1, x_out):
        nc = self.nc
        SC, V = nc.scalar, nc.vector
        for half in range(2):
            V.scalar_tensor_tensor(
                [self.esc, self.esc2][half][:],
                xpre[:, half * 512:(half + 1) * 512], 1.0,
                xpre[:, half * 512:(half + 1) * 512], ALU.mult, ALU.mult,
                accum_out=self.sums[:, 8 + half:9 + half])
        V.drain()
        V.tensor_add(self.stat[:, 0:1], s0, s1)
        V.tensor_add(self.stat[:, 1:2], self.sums[:, 8:9], self.sums[:, 9:10])
        V.drain()
        V.tensor_scalar_mul(self.stat[:, 0:2], self.stat[:, 0:2], 1.0 / D)
        V.drain()
        V.scalar_tensor_tensor(self.stat[:, 2:3], self.stat[:, 0:1], 1.0,
                               self.stat[:, 0:1], ALU.mult, ALU.mult)
        V.drain()
        V.tensor_sub(self.stat[:, 1:2], self.stat[:, 1:2], self.stat[:, 2:3])
        V.drain()
        iv = self.inc(V.tensor_scalar_add(self.stat[:, 1:2], self.stat[:, 1:2],
                                          EPS_LN), "V")
        self.wait(SC, "V", iv)
        SC.activation(self.stat[:, 4:5], self.stat[:, 1:2], AF.Ln)
        SC.drain()
        isc = self.inc(SC.activation(self.stat[:, 2:3], self.stat[:, 4:5],
                                     AF.Exp, scale=-0.5), "Sc")
        self.wait(V, "Sc", isc)
        V.tensor_mul(self.stat[:, 3:4], self.stat[:, 0:1], self.stat[:, 2:3])
        V.drain()
        V.tensor_scalar_mul(self.stat[:, 3:4], self.stat[:, 3:4], -1.0)
        V.drain()
        self.inc(V.tensor_scalar(x_out, xpre, self.stat[:, 2:3],
                                 self.stat[:, 3:4], ALU.mult, ALU.add), "V")

    def _router(self, b):
        nc = self.nc
        sy, gp = nc.sync, nc.gpsimd
        TE, SC, V = nc.tensor, nc.scalar, nc.vector
        self.wait(gp, "V")
        if "d_xsum" in self.dbg:
            idm = self.inc(gp.dma_start(out=self.dbg["d_xsum"].ap()[b][None, :],
                                        in_=self.xsum[:]), "dW")
            gp.wait_ge(self.sem["dW"], idm)
        self.inc(gp.dma_start(
            out=self.xsum_fm[:],
            in_=_ap(self.xsum, 0, [[1024, 1], [8, 128], [1, 8]]),
            allow_slow_non_contiguous=True), "dW")
        gp.wait_ge(self.sem["dW"], self.cnt["dW"])
        self.wait(TE, "dW")
        last = None
        for c in range(8):
            last = TE.matmul(self.psA[0:1, 0:H], self.xsum_fm[:, c:c + 1],
                             self.wr_f[:, c, :], start=(c == 0), stop=(c == 7))
        it = self.inc(last, "T")
        self.wait(V, "T", it)
        V.scalar_tensor_tensor(self.r_row[:], self.br_row[:], float(self.S),
                               self.psA[0:1, 0:H], ALU.mult, ALU.add)
        V.drain()
        V.max(out=self.vmax8[:], in_=self.r_row[:])
        V.drain()
        iv = self.inc(V.tensor_scalar(self.active[:], self.r_row[:],
                                      self.vmax8[:, NA - 1:NA], None, ALU.is_ge),
                      "V")
        V.memset(self.xsum[:], 0.0)
        self.wait(gp, "V", iv)
        self.inc(gp.dma_start(out=self.activeT[:],
                              in_=_ap(self.active, 0, [[16, 1], [1, 16]]),
                              allow_slow_non_contiguous=True), "dW")
        gp.wait_ge(self.sem["dW"], self.cnt["dW"])
        self.wait(TE, "dW")
        mm = TE.matmul(self.psC[0:H, 0:1], self.u16_f[:], self.activeT[:],
                       start=True, stop=True)
        it = self.inc(mm, "T")
        self.wait(V, "T", it)
        last = None
        for a in range(NA):
            last = V.scalar_tensor_tensor(self.sel[:, a:a + 1], self.psC[0:H, 0:1],
                                          float(a), self.activeT[:],
                                          ALU.is_equal, ALU.mult)
        iv = self.inc(last, "V")
        V.drain()
        s_b = _ap(self.sel, 0, [[NA, H], [0, 8], [1, NA]])
        m_b = _ap(self.selmask_f, 0, [[32, H], [NA, 8], [1, NA]])
        o_b = _ap(self.selp, 0, [[32, H], [NA, 8], [1, NA]])
        iv = self.inc(V.tensor_tensor(o_b, s_b, m_b, ALU.mult), "V")
        self.wait(TE, "V", iv)
        mm = TE.matmul(self.psD[:, 0:32], self.lpar_f[:], self.selp[:],
                       start=True, stop=True)
        it = self.inc(mm, "T")
        self.wait(V, "T", it)
        last = None
        for c in range(8):
            src = _ap(self.psD, c * 4, [[512, 128], [1, NA], [0, 64]])
            i2b = _ap(self.i2_f, 0, [[64, 128], [0, NA], [1, 64]])
            dst = _ap(self.G, c * 256, [[8 * 256, 128], [64, NA], [1, 64]])
            last = V.tensor_tensor(dst, src, i2b, ALU.mult)
        self.inc(last, "V")
        self.wait(TE, "V")
        if "d_r" in self.dbg:
            self.wait(sy, "V")
            self.inc(sy.dma_start(out=self.dbg["d_r"].ap()[b][None, :],
                                  in_=self.r_row[:]), "dX")
        if "d_sel" in self.dbg:
            self.wait(sy, "V")
            self.inc(sy.dma_start(out=self.dbg["d_sel"].ap()[b], in_=self.sel[:]),
                     "dX")

    def _phase3(self, b):
        nc = self.nc
        sy, gp = nc.sync, nc.gpsimd
        TE, SC, V = nc.tensor, nc.scalar, nc.vector
        for g in range(self.NG):
            for dt_ in range(4):
                t = g * 4 + dt_

                def evac(gg, ps, dt_=dt_):
                    out = _ap(self.x1T, (4 * gg) * 512 + dt_ * 128,
                              [[8 * 512, 128], [512, 4], [1, 128]])
                    if gg % 2 == 0:
                        return ("Sc", self.inc(
                            SC.activation(out, ps[:], AF.Copy), "Sc"))
                    self.wait(V, "T", self.cnt["T"])
                    return ("V", self.inc(V.tensor_copy(out, ps[:]), "V"))

                self._tr8(self.x1[:, t, :], evac,
                          banks=[self.psA, self.psB, self.psG, self.psH])
            self.wait(TE, "Sc")
            self.wait(TE, "V")
            for ti, (wsb, bias, kind) in enumerate([
                    (self.wq2, self.bq2_fm, "q"),
                    (self.wk2, self.bk2_fm, "k"),
                    (self.wv2, self.bv2_fm, "v")]):
                self.wait(TE, "V")
                sname, scnt = self._w2_done[kind]
                self.wait(TE, sname, scnt)
                evacd = {}
                for hc in range(9):
                    if hc < 8:
                        ps = [self.psA, self.psB][hc % 2]
                        if hc >= 2:
                            self.wait(TE, "V", evacd[hc - 2])
                        mm = None
                        for c in range(8):
                            mm = TE.matmul(ps[:],
                                           wsb[:, c, hc * 128:(hc + 1) * 128],
                                           self.x1T[:, c, :],
                                           start=(c == 0), stop=(c == 7))
                        it = self.inc(mm, "T")
                        self.wait(V, "T", it)
                        evacd[hc] = self.inc(
                            V.tensor_scalar(self.qkv_full[:, hc, :], ps[:],
                                            bias[:, hc:hc + 1], None, ALU.add),
                            "V")
                    cg = hc - 1
                    if cg >= 0:
                        self.wait(TE, "V", evacd[cg])
                        if cg == 0:
                            self.wait(TE, "Sc", getattr(self, "_gps_free", 0))
                        for j in range(2):
                            psj = [self.psC, self.psD][j]
                            mm = TE.matmul(psj[:],
                                           self.G[:, cg, j * 128:(j + 1) * 128],
                                           self.qkv_full[:, cg, :],
                                           start=(cg == 0), stop=(cg == 7),
                                           skip_group_check=True)
                            if cg == 7:
                                self.inc(mm, "T")
                for j in range(2):
                    ps = [self.psC, self.psD][j]
                    it = self.cnt["T"]
                    if kind == "v":
                        self.wait(SC, "T", it)
                        isc = self.inc(SC.activation(self.vsfm[:], ps[:],
                                                     AF.Copy), "Sc")
                        self.wait(TE, "Sc", isc)
                        tr = None
                        for dt_ in range(4):
                            out_bf = self.psE[:, dt_ * 64:(dt_ + 1) * 64].bitcast(
                                BF16)
                            tr = TE.transpose(out_bf,
                                              self.vsfm[:, dt_ * 128:(dt_ + 1) * 128],
                                              self.id_bf[:])
                        it2 = self.inc(tr, "T")
                        self.wait(SC, "T", it2)
                        isc = None
                        for dt_ in range(4):
                            t = g * 4 + dt_
                            dst = _ap(self.Vs, t * NA * 65 + (2 * j) * 65,
                                      [[self.NT * NA * 65, 128], [65, 2], [1, 64]])
                            srcb = self.psE[:, dt_ * 64:(dt_ + 1) * 64].bitcast(
                                BF16)
                            src2 = bass.AP(tensor=srcb.tensor, offset=srcb.offset,
                                           ap=[list(srcb.ap[0]), [64, 2], [1, 64]])
                            isc = self.inc(SC.activation(dst, src2, AF.Copy),
                                           "Sc")
                        self.wait(TE, "Sc", isc)
                        self._gps_free = self.cnt["Sc"]
                    else:
                        dsts = self.Qs if kind == "q" else self.Ks
                        scale = 0.125 if kind == "q" else 1.0
                        self.wait(SC, "T", it)
                        self.inc(SC.activation(
                            dsts[2 * j][0:64, g * 512:(g + 1) * 512],
                            ps[0:64, :], AF.Copy, scale=scale), "Sc")
                        self.wait(SC, "dW", getattr(self, "_shift_idm", 0))
                        isc = self.inc(SC.activation(
                            self.shiftb[64:128, :], ps[64:128, :], AF.Copy,
                            scale=scale), "Sc")
                        self.wait(gp, "Sc", isc)
                        idm = self.inc(gp.dma_start(
                            out=dsts[2 * j + 1][0:64, g * 512:(g + 1) * 512],
                            in_=self.shiftb[64:128, :]), "dW")
                        gp.wait_ge(self.sem["dW"], idm)
                        self._shift_idm = idm
                        self._gps_free = self.cnt["Sc"]

    def _bounds(self, b):
        nc = self.nc
        sy = nc.sync
        TE, SC, V = nc.tensor, nc.scalar, nc.vector
        S = self.S
        nch = S // 512
        V.memset(self.sqb[0][64:65, :], 1.0)
        V.memset(self.sqb[1][64:65, :], 1.0)
        units = []
        for a in range(NA):
            units.append((a, "k", self.Ks[a]))
            units.append((a, "q", self.Qs[a]))
        mm_done, ln_done, exp_done, fin_done = {}, {}, {}, {}
        for u, (a, which, ten) in enumerate(units):
            sq = self.sqb[u % 2]
            nac = self.nac2[u % 2]
            pb = [self.psA, self.psB] if u % 2 == 0 else [self.psC, self.psD]
            self.wait(V, "T", mm_done.get(u - 2, 0))
            iv = self.inc(V.tensor_tensor(sq[0:64, :], ten[0:64, :],
                                          ten[0:64, :], ALU.mult), "V")
            self.wait(TE, "V", iv)
            self.wait(TE, "Sc", ln_done.get(u - 2, 0))
            mm = None
            for ch in range(nch):
                bp = (ch // 2) * 32
                mm = TE.matmul(pb[ch % 2][bp:bp + 1, 0:512], self.ones65_bf[:],
                               sq[:, ch * 512:(ch + 1) * 512],
                               start=True, stop=True)
            mm_done[u] = self.inc(mm, "T")
            self.wait(SC, "T", mm_done[u])
            self.wait(SC, "V", fin_done.get(u - 2, 0))
            for ch in range(nch):
                bp = (ch // 2) * 32
                SC.activation(nac[0:1, ch * 512:(ch + 1) * 512],
                              pb[ch % 2][bp:bp + 1, 0:512], AF.Ln)
            SC.drain()
            exp_done[u] = self.inc(SC.activation(nac[:], nac[:], AF.Exp,
                                                 scale=0.5), "Sc")
            ln_done[u] = exp_done[u]
            self.wait(V, "Sc", exp_done[u])
            if which == "k":
                iv = self.inc(V.reduce_max(self.kmax_row[0:1, a:a + 1],
                                           nac[:], AX.X), "V")
                V.drain()
            else:
                iv = self.inc(V.tensor_scalar(
                    ten[64:65, :], nac[:],
                    self.kmax_row[0:1, a:a + 1], -1.0,
                    ALU.mult, ALU.mult), "V")
                V.drain()
                V.memset(self.Ks[a][64:65, :], 1.0)
                self.inc(V.memset(self.vmax8[0:1, 1:2], 0.0), "V")
            fin_done[u] = self.cnt["V"]
        self.wait(TE, "V")
        gp = self.nc.gpsimd
        if "d_qs" in self.dbg:
            self.wait(gp, "V")
            idm = None
            for a in range(NA):
                self.inc(gp.dma_start(out=self.dbg["d_qs"].ap()[b, a],
                                      in_=self.Qs[a][:]), "dW")
                idm = self.inc(gp.dma_start(out=self.dbg["d_ks"].ap()[b, a],
                                            in_=self.Ks[a][:]), "dW")
            gp.wait_ge(self.sem["dW"], idm)
        if "d_vs" in self.dbg:
            self.wait(gp, "V")
            idm = None
            for t in range(self.NT):
                idm = self.inc(gp.dma_start(
                    out=self.dbg["d_vs"].ap()[b, t * 128:(t + 1) * 128, :],
                    in_=self.Vs[:, t, :]), "dW")
            gp.wait_ge(self.sem["dW"], idm)

    def _phase45(self, b, last=False):
        nc = self.nc
        sy = nc.sync
        TE, SC, V = nc.tensor, nc.scalar, nc.vector
        S, NT = self.S, self.NT
        NQC = S // 1024
        dO_tile = {}
        self.wait(TE, "dW")
        for qc in range(min(NQC, self.p45_qc)):
            otn_done = {}
            for a in range(NA):
                exp_done = {}
                pv_done = {}
                psO_pair = [[self.psE, self.psF], [self.psG, self.psH]][a % 2]
                if a < 2:
                    self.wait(TE, "V")
                sbank = [self.psA, self.psB, self.psC, self.psD]
                for kt in range(NT + 1):
                    if kt < NT:
                        for hf in range(2):
                            ei = 2 * kt + hf
                            psS = sbank[ei % 4]
                            self.wait(TE, "Sc", exp_done.get(
                                ei - 4, getattr(self, "_rep_sc", 0)))
                            mm = TE.matmul(
                                psS[:],
                                self.Ks[a][:, kt * 128:(kt + 1) * 128],
                                self.Qs[a][:, qc * 1024 + hf * 512:
                                           qc * 1024 + (hf + 1) * 512],
                                start=True, stop=True)
                            it = self.inc(mm, "T")
                            self.wait(SC, "T", it)
                            self.wait(SC, "T", pv_done.get(kt - 2, 0))
                            isc = self.inc(SC.activation(
                                self.Pt[kt % 2][:, hf * 512:(hf + 1) * 512],
                                psS[:], AF.Exp), "Sc")
                            exp_done[ei] = isc
                    kp = kt - 1
                    if kp >= 0:
                        self.wait(TE, "Sc", exp_done[2 * kp + 1])
                        if kp == 0 and a >= 2:
                            self.wait(TE, "V", otn_done.get(a - 2, 0))
                        mm2 = None
                        for hf in range(2):
                            psO = psO_pair[hf]
                            mm2 = TE.matmul(
                                psO[0:65, :],
                                self.Vs[:, kp, a * 65:(a + 1) * 65],
                                self.Pt[kp % 2][:, hf * 512:(hf + 1) * 512],
                                start=(kp == 0), stop=(kp == NT - 1),
                                skip_group_check=True)
                        pv_done[kp] = self.inc(mm2, "T")
                self.wait(SC, "T")
                with nc.allow_low_precision(reason="f32r operand prep"):
                    for hf in range(2):
                        psO = psO_pair[hf]
                        SC.activation(self.nrm_f[0:1, hf * 512:(hf + 1) * 512],
                                      psO[64:65, :], AF.Ln)
                    SC.drain()
                    isc = self.inc(SC.activation(self.rec_row[:],
                                                 self.nrm_f[:], AF.Exp,
                                                 scale=-1.0), "Sc")
                self.wait(TE, "Sc", isc)
                psR_pair = [self.psC, self.psD]
                last_mm = None
                for hf in range(2):
                    last_mm = TE.matmul(psR_pair[hf][0:64, :], self.onesr64_r[:],
                                        self.rec_row[0:1, hf * 512:(hf + 1) * 512],
                                        start=True, stop=True)
                it = self.inc(last_mm, "T")
                self.wait(SC, "T", it)
                self.wait(SC, "V", otn_done.get(a - 1, 0))
                isc = None
                for hf in range(2):
                    isc = self.inc(SC.activation(
                        self.rep_sb[:, hf * 512:(hf + 1) * 512],
                        psR_pair[hf][0:64, :], AF.Copy), "Sc")
                self._rep_sc = isc
                self.wait(V, "Sc", isc)
                self.wait(V, "T", pv_done[NT - 1])
                last_v = None
                for hf in range(2):
                    psO = psO_pair[hf]
                    with nc.allow_low_precision(reason="f32r operand prep"):
                        last_v = V.tensor_tensor(
                            self.OTn[0:64, a, hf * 512:(hf + 1) * 512],
                            psO[0:64, :],
                            self.rep_sb[:, hf * 512:(hf + 1) * 512], ALU.mult)
                otn_done[a] = self.inc(last_v, "V")
                if "d_ot" in self.dbg:
                    gp = nc.gpsimd
                    self.wait(gp, "V")
                    idm = self.inc(gp.dma_start(
                        out=self.dbg["d_ot"].ap()[b, a, :,
                                                  qc * 1024:(qc + 1) * 1024],
                        in_=self.OTn[0:64, a, :]), "dW")
                    gp.wait_ge(self.sem["dW"], idm)
                    self.wait(V, "dW", idm)
            self.wait(TE, "V")

            if self.p45_level < 2:
                continue
            res_done = {}
            v_at_start = self.cnt["V"]
            for dt_ in range(8):
                t = qc * 8 + dt_
                opair = [[self.psA, self.psB], [self.psC, self.psD]][dt_ % 2]
                self.wait(TE, "V", res_done.get(dt_ - 2, v_at_start))
                for half in range(2):
                    ps = opair[half]
                    for a in range(NA):
                        TE.matmul(ps[:],
                                  self.OTn[0:64, a, dt_ * 128:(dt_ + 1) * 128],
                                  self.wo2r[:, a, half * 512:(half + 1) * 512],
                                  start=(a == 0), stop=False)
                    mm = TE.matmul(ps[:], self.ones2_bf[:],
                                   self.bo2_hl[:, half * 512:(half + 1) * 512],
                                   start=False, stop=True)
                    self.inc(mm, "T")
                so = "dO%d" % (t % 2)
                self.wait(V, so, dO_tile.get(t - 2, 0))
                self.wait(V, "T")
                for half in range(2):
                    psO = opair[half]
                    V.scalar_tensor_tensor(
                        self.x1[:, t, half * 512:(half + 1) * 512], psO[:], 0.0,
                        self.x1[:, t, half * 512:(half + 1) * 512],
                        ALU.add, ALU.add,
                        accum_out=self.sums[:, half:half + 1])
                self.inc(V.memset(self.sums[:, 8:9], 0.0), "V")
                V.drain()
                self._ln_finalize(self.x1[:, t, :], self.sums[:, 0:1],
                                  self.sums[:, 1:2], self.x1[:, t, :])
                res_done[dt_] = self.cnt["V"]
                self.wait(sy, "V")
                dO_tile[t] = self.inc(
                    sy.dma_start(out=self.out_h.ap()[b, t * 128:(t + 1) * 128, :],
                                 in_=self.x1[:, t, :]), so)
        if not last:
            self._wload_b1(first=False)


def _pack_inputs(inputs):
    g = lambda n: np.ascontiguousarray(np.asarray(inputs[n], np.float32))
    m = {}
    m["a_wq"] = g("L0_attn_wq").reshape(D, D).copy()
    m["a_wk"] = g("L0_attn_wk").reshape(D, D).copy()
    m["a_wv"] = g("L0_attn_wv").reshape(D, D).copy()
    m["a_wo"] = g("L0_attn_wo").reshape(D, D).copy()
    m["a_bq"] = g("L0_attn_bq").reshape(D).copy()
    m["a_bk"] = g("L0_attn_bk").reshape(D).copy()
    m["a_bv"] = g("L0_attn_bv").reshape(D).copy()
    m["a_bo"] = g("L0_attn_bo").reshape(D).copy()
    m["f_wq"] = g("L0_ffn_Wq")
    m["f_wk"] = g("L0_ffn_Wk")
    m["f_wv"] = g("L0_ffn_Wv")
    m["f_wo"] = g("L0_ffn_Wo")
    m["f_bq"] = g("L0_ffn_bq")
    m["f_bk"] = g("L0_ffn_bk")
    m["f_bv"] = g("L0_ffn_bv")
    m["f_bo"] = g("L0_ffn_bo")
    m["f_wr"] = g("L0_ffn_Wr")
    m["f_br"] = g("L0_ffn_br")
    return m


def kernel(**inputs):
    x = np.ascontiguousarray(np.asarray(inputs["L0_attn_x"], np.float32))
    B, S, _ = x.shape
    nb = B // N_CORES
    prog = Prog(S=S, NB=nb)
    wm = _pack_inputs(inputs)
    in_maps = []
    for core in range(N_CORES):
        im = dict(wm)
        im["x"] = np.ascontiguousarray(x[core * nb:(core + 1) * nb])
        in_maps.append(im)
    res = run_bass_kernel_spmd(prog.nc, in_maps, core_ids=list(range(N_CORES)))
    out = np.concatenate([r["out"] for r in res.results], axis=0)
    return out.astype(np.float32)


# version 13
# speedup vs baseline: 1.0441x; 1.0159x over previous
import numpy as np

import concourse.bass as bass
import concourse.mybir as mybir
from concourse.bass_utils import run_bass_kernel_spmd

F32 = mybir.dt.float32
F32R = mybir.dt.float32r
BF16 = mybir.dt.bfloat16
AF = mybir.ActivationFunctionType
ALU = mybir.AluOpType
AX = mybir.AxisListType

D = 1024
H = 16
HD = 64
NA = 4
EPS_ATTN = 1e-6
EPS_LN = 1e-5
N_CORES = 8

WNAMES = ["a_wq", "a_wk", "a_wv", "a_wo", "a_bq", "a_bk", "a_bv", "a_bo",
          "f_wq", "f_wk", "f_wv", "f_wo", "f_bq", "f_bk", "f_bv", "f_bo",
          "f_wr", "f_br"]


def _ap(t, offset, dims):
    return bass.AP(tensor=t[:].tensor, offset=offset, ap=[list(d) for d in dims])


class Prog:
    def __init__(self, S=2048, NB=2, debug=(), stop_after=None, p2_level=9,
                 p2y_level=9, p45_level=9, p45_qc=99):
        assert S % 1024 == 0
        self.stop_after = stop_after
        self.p2_level = p2_level
        self.p2y_level = p2y_level
        self.p45_level = p45_level
        self.p45_qc = p45_qc
        self.S, self.NB = S, NB
        self.NT = S // 128
        self.NG = self.NT // 4
        self.debug = set(debug)
        nc = bass.Bass("TRN2", target_bir_lowering=False)
        self.nc = nc

        dt = nc.dram_tensor
        self.x_h = dt("x", [NB, S, D], F32, kind="ExternalInput")
        self.out_h = dt("out", [NB, S, D], F32, kind="ExternalOutput")
        shp = {"a_wq": [D, D], "a_wk": [D, D], "a_wv": [D, D], "a_wo": [D, D],
               "a_bq": [D], "a_bk": [D], "a_bv": [D], "a_bo": [D],
               "f_wq": [D, D], "f_wk": [D, D], "f_wv": [D, D],
               "f_wo": [NA * HD, D], "f_bq": [D], "f_bk": [D], "f_bv": [D],
               "f_bo": [D], "f_wr": [D, H], "f_br": [H]}
        self.w = {n: dt(n, shp[n], F32, kind="ExternalInput") for n in WNAMES}
        self.wo1r_h = dt("wo1r", [D, D], F32R, kind="Internal")
        self.dbg = {}
        for name, s in [("d_qm", [NB, 128, 8, 512]), ("d_ynt", [NB, 128, 8, 512]),
                        ("d_xg", [NB, 8, D]), ("d_km", [NB, 128, D]),
                        ("d_vaug", [NB, 128, H * 65]), ("d_xsum", [NB, D]),
                        ("d_kv", [NB, 128, 16 * 65]),
                        ("d_x1", [NB, S, D]), ("d_r", [NB, H]),
                        ("d_sel", [NB, H, NA]),
                        ("d_qs", [NB, NA, 65, S]), ("d_ks", [NB, NA, 65, S]),
                        ("d_vs", [NB, S, NA * 65]), ("d_ot", [NB, NA, 64, S])]:
            if name in self.debug:
                self.dbg[name] = dt(name, s, F32, kind="ExternalOutput")

        it = nc.inline_tensor
        self.c_id = it(np.eye(128, dtype=np.float32), "c_id")
        self.c_ones2 = it(np.ones((2, 128), np.float32), "c_ones2")
        self.c_ones64 = it(np.ones((64, 1), np.float32), "c_ones64")
        o65 = np.ones((65, 1), np.float32); o65[64, 0] = 1e-30
        self.c_ones65 = it(o65, "c_ones65")
        self.c_ones128 = it(np.ones((128, 1), np.float32), "c_ones128")
        self.c_onesr64 = it(np.ones((1, 64), np.float32), "c_onesr64")
        self.c_u16 = it(np.triu(np.ones((H, H), np.float32), 1).copy(), "c_u16")
        lpar = np.zeros((H, 128), np.float32)
        for h in range(H):
            lpar[h, (h % 2) * 64:(h % 2) * 64 + 64] = 1.0
        self.c_lpar = it(lpar, "c_lpar")
        selmask = np.zeros((H, 32), np.float32)
        for h in range(H):
            selmask[h, (h // 2) * 4:(h // 2) * 4 + 4] = 1.0
        self.c_selmask = it(selmask, "c_selmask")
        self.c_i2 = it(np.vstack([np.eye(64, dtype=np.float32)] * 2).copy(), "c_i2")

        self._tr_rot = 0
        self._sbuf()
        self._sems()
        self._emit()

    def _sbuf(self):
        nc = self.nc
        S, NT = self.S, self.NT
        a = nc.alloc_sbuf_tensor

        def at(name, shape, dtp, off):
            return nc.alloc_sbuf_tensor_at(name, shape, dtp, offset=off)

        self.id_f = a("id_f", [128, 128], F32)
        self.id_bf = a("id_bf", [128, 128], BF16)
        self.ones2_bf = a("ones2_bf", [2, 128], BF16)
        self.ones64_bf = a("ones64_bf", [64, 1], BF16)
        self.ones65_bf = a("ones65_bf", [65, 1], BF16)
        self.onesr64_r = a("onesr64_r", [1, 64], F32R)
        self.ones128_f = a("ones128_f", [128, 1], F32)
        self.lpar_f = a("lpar_f", [H, 128], F32)
        self.selmask_f = a("selmask_f", [H, 32], F32)
        self.selp = a("selp", [H, 32], F32)
        self.i2_f = a("i2_f", [128, 64], F32)
        self.u16_f = a("u16_f", [H, H], F32)

        self.x1 = a("x1", [128, NT, D], F32)
        self.Vs = a("Vs", [128, NT, NA * 65], BF16)
        self.kv_z = a("kv_z", [128, H, 65], BF16)
        self.esc = a("esc", [128, 512], F32)
        self.esc2 = a("esc2", [128, 512], F32)
        self.sums = a("sums", [128, 10], F32)
        self.stat = a("stat", [128, 6], F32)
        self.wr_f = a("wr_f", [128, 8, H], F32)
        self.xsum = a("xsum", [1, D], F32)
        self.xsum_fm = a("xsum_fm", [128, 8], F32)
        self.nrm_f = a("nrm_f", [1, D], F32)

        self.bq1_fm = a("bq1_fm", [128, 8], F32)
        self.bq2_fm = a("bq2_fm", [128, 8], F32)
        self.bk2_fm = a("bk2_fm", [128, 8], F32)
        self.bv2_fm = a("bv2_fm", [128, 8], F32)
        self.bk1_hl = a("bk1_hl", [2, D], BF16)
        self.bv1_hl = a("bv1_hl", [2, D], BF16)
        self.bo1_hl = a("bo1_hl", [2, D], BF16)
        self.bo2_hl = a("bo2_hl", [2, D], BF16)
        self.br_row = a("br_row", [1, H], F32)

        self.r_row = a("r_row", [1, H], F32)
        self.z16 = a("z16", [128, H], F32)
        self.epsc = a("epsc", [128, 1], F32)
        self.vmax8 = a("vmax8", [1, 8], F32)
        self.active = a("active", [1, H], F32)
        self.activeT = a("activeT", [H, 1], F32)
        self.sel = a("sel", [H, NA], F32)
        self.kmax_row = a("kmax_row", [1, NA], F32)

        rqk = nc.bump_sbuf(NA * 2 * S * 2)[0]
        self.Qs = [at(f"Qs{i}", [65, S], BF16, rqk + i * (S * 2)) for i in range(NA)]
        self.Ks = [at(f"Ks{i}", [65, S], BF16, rqk + (NA + i) * (S * 2))
                   for i in range(NA)]

        ph = nc.bump_sbuf(78336)[0]
        self.ph = ph
        self.wk1 = at("wk1", [128, 8, D], BF16, ph + 0)
        self.wv1 = at("wv1", [128, 8, D], BF16, ph + 16384)
        self.xin = [at(f"xin{i}", [128, D], F32, ph + 32768 + i * 4096)
                    for i in range(2)]
        self.xin_bf = [at(f"xinb{i}", [128, D], BF16, ph + 32768 + i * 2048)
                       for i in range(4)]
        self.km = [at(f"km{i}", [128, D], BF16,
                      ph + (40960 if i == 0 else 65664)) for i in range(2)]
        self.vaug = [at(f"vaug{i}", [128, H * 65], BF16,
                        ph + (43008 if i == 0 else 67712)) for i in range(2)]
        self.xt1 = [at(f"xt1_{i}", [128, 8, 128], BF16, ph + 45184 + i * 2048)
                    for i in range(2)]
        self.wq1 = at("wq1", [128, 8, D], BF16, ph + 49280)
        self.xTg = at("xTg", [128, 8, 512], BF16, ph + 0)
        self.qm = at("qm", [128, 8, 512], BF16, ph + 8192)
        self.ynTg = at("ynTg", [128, 8, 512], F32R, ph + 16384)
        self.wo1_ring = [at(f"wo1g{i}", [128, 512], F32R,
                            ph + (32768 + i * 2048 if i < 4 else
                                  69824 + (i - 4) * 2048)) for i in range(8)]
        self.ynorm = [at(f"ynorm{i}", [128, D], F32,
                         ph + (40960 if i == 0 else 45056)) for i in range(2)]
        self.wq2 = at("wq2", [128, 8, D], BF16, ph + 0)
        self.wk2 = at("wk2", [128, 8, D], BF16, ph + 16384)
        self.wv2 = at("wv2", [128, 8, D], BF16, ph + 32768)
        self.x1T = at("x1T", [128, 8, 512], BF16, ph + 49152)
        self.qkv_full = at("qkv_full", [128, 8, 512], BF16, ph + 57344)
        self.shiftb = at("shiftb", [128, 512], BF16, ph + 65536)
        self.vsfm = at("vsfm", [128, 512], BF16, ph + 66560)
        self.G = at("G", [128, 8, 256], BF16, ph + 67584)
        self.sqb = [at(f"sqb{i}", [65, S], BF16, ph + i * 4096)
                    for i in range(2)]
        self.nac2 = [at(f"nac2_{i}", [1, S], F32, ph + 8192 + i * 8192)
                     for i in range(2)]
        self.OTn = at("OTn", [64, NA, D], F32R, ph + 0)
        self.Pt = [at(f"Pt{i}", [128, D], BF16, ph + 16384 + i * 2048)
                   for i in range(2)]
        self.rep_sb = at("rep_sb", [64, D], F32, ph + 20480)
        self.rec_row = at("rec_row", [1, D], F32R, ph + 24576)
        self.wo2r = at("wo2r", [64, NA, D], F32R, ph + 28672)
        self.wk1n = at("wk1n", [128, 8, D], BF16, ph + 45056)
        self.wv1n = at("wv1n", [128, 8, D], BF16, ph + 61440)
        self.brow = at("brow", [1, D], F32, ph + 0)
        self.blo_f = at("blo_f", [1, D], F32, ph + 4096)

        p = nc.alloc_psum_tensor
        (self.psA, self.psB, self.psC, self.psD, self.psE, self.psF,
         self.psG, self.psH) = [p(f"ps{i}", [128, 512], F32) for i in range(8)]

    def _sems(self):
        self.sem, self.cnt = {}, {}
        for n in ["dW", "dX", "dX0", "dX1", "dX2", "dX3",
                  "dB0", "dB1", "dB2",
                  "dG0", "dG1", "dG2", "dG3", "dO0", "dO1",
                  "dR0", "dR1", "dR2", "dR3", "dR4", "dR5", "dR6", "dR7",
                  "T", "V", "Sc"]:
            self.sem[n] = self.nc.alloc_semaphore(n)
            self.cnt[n] = 0

    def inc(self, instr, name):
        amt = 16 if name.startswith("d") else 1
        instr.then_inc(self.sem[name], amt)
        self.cnt[name] += amt
        return self.cnt[name]

    def wait(self, eng, name, val=None):
        eng.wait_ge(self.sem[name], self.cnt[name] if val is None else val)

    def _emit(self):
        nc = self.nc
        sa = self.stop_after
        with nc.Block():
            self._startup()
            nc.all_engine_barrier()
            self._wload_b1(first=True)
            self._w1_done = self.cnt["dW"]
            nc.all_engine_barrier()
            for b in range(self.NB):
                if sa == "startup":
                    break
                self._phase1(b)
                nc.all_engine_barrier()
                if sa == "p1":
                    break
                self._phase2(b)
                nc.all_engine_barrier()
                if sa == "p2":
                    break
                self._router(b)
                self._wload_b2()
                nc.all_engine_barrier()
                if sa == "router":
                    break
                self._phase3(b)
                nc.all_engine_barrier()
                if sa == "p3":
                    break
                self._wload_wo2()
                self._bounds(b)
                nc.all_engine_barrier()
                if sa == "bounds":
                    break
                self._phase45(b, last=(b == self.NB - 1))
                nc.all_engine_barrier()
                if b + 1 < self.NB:
                    gp = nc.gpsimd
                    self.inc(gp.dma_start(out=self.wk1[:], in_=self.wk1n[:]), "dW")
                    self.inc(gp.dma_start(out=self.wv1[:], in_=self.wv1n[:]), "dW")
                    gp.wait_ge(self.sem["dW"], self.cnt["dW"])
                    self._wload_wq1()
                    self._w1_done = self.cnt["dW"]
                    nc.all_engine_barrier()
            for s in ["dX", "dX0", "dX1", "dX2", "dX3", "dO0", "dO1",
                      "dR0", "dR1", "dR2",
                      "dR3", "dR4", "dR5", "dR6", "dR7"]:
                self.wait(nc.sync, s)
            for s in ["dG0", "dG1", "dG2", "dG3", "dB0", "dB1", "dB2"]:
                self.wait(nc.sync, s)
            self.wait(nc.gpsimd, "dW")

    def _startup(self):
        nc = self.nc
        sy, gp, V = nc.sync, nc.gpsimd, nc.vector
        w = self.w
        for dst, src in [(self.id_f, self.c_id), (self.ones128_f, self.c_ones128),
                         (self.lpar_f, self.c_lpar),
                         (self.selmask_f, self.c_selmask),
                         (self.i2_f, self.c_i2), (self.u16_f, self.c_u16)]:
            self.inc(sy.dma_start(out=dst[:], in_=src.ap()), "dX")
        for dst, src in [(self.id_bf, self.c_id), (self.ones2_bf, self.c_ones2),
                         (self.ones64_bf, self.c_ones64),
                         (self.ones65_bf, self.c_ones65),
                         (self.onesr64_r, self.c_onesr64)]:
            self.inc(gp.dma_start(out=dst[:], in_=src.ap()), "dW")
        self.inc(gp.dma_start(out=self.wo1r_h.ap(), in_=w["a_wo"].ap()), "dW")
        self.inc(sy.dma_start(
            out=self.wr_f[:],
            in_=w["f_wr"].ap().rearrange("(p c) h -> p c h", p=128)), "dX")
        for src, dst in [("a_bq", self.bq1_fm), ("f_bq", self.bq2_fm),
                         ("f_bk", self.bk2_fm), ("f_bv", self.bv2_fm)]:
            self.inc(sy.dma_start(
                out=dst[:], in_=w[src].ap().rearrange("(c p) -> p c", p=128),
                allow_slow_non_contiguous=True), "dX")
        self.inc(sy.dma_start(out=self.br_row[:], in_=w["f_br"].ap()[None, :]),
                 "dX")
        for src, dst in [("a_bk", self.bk1_hl), ("a_bv", self.bv1_hl),
                         ("a_bo", self.bo1_hl), ("f_bo", self.bo2_hl)]:
            self.inc(sy.dma_start(out=self.brow[:], in_=w[src].ap()[None, :]),
                     "dX")
            self.wait(V, "dX")
            self.wait(V, "dW")
            V.tensor_copy(dst[0:1, :], self.brow[:])
            V.drain()
            iv = self.inc(V.tensor_sub(self.blo_f[:], self.brow[:], dst[0:1, :]),
                          "V")
            self.wait(gp, "V", iv)
            self.inc(gp.dma_start(out=dst[1:2, :], in_=self.blo_f[:]), "dW")
            sy.wait_ge(self.sem["V"], iv)
        self.wait(V, "dW")
        V.memset(_ap(self.Vs, 64, [[self.NT * NA * 65, 128], [65, self.NT * NA]]),
                 1.0)
        V.memset(self.epsc[:], EPS_LN)
        V.memset(self.sums[:], 0.0)
        self.inc(V.memset(self.stat[:], 0.0), "V")

    def _wload_b1(self, first=False):
        gp = self.nc.gpsimd
        for src, dst in [("a_wk", self.wk1 if first else self.wk1n),
                         ("a_wv", self.wv1 if first else self.wv1n)]:
            self.inc(gp.dma_start(
                out=dst[:],
                in_=self.w[src].ap().rearrange("(c p) f -> p c f", p=128)), "dW")
        if first:
            self._wload_wq1()

    def _wload_wq1(self):
        gp = self.nc.gpsimd
        self.inc(gp.dma_start(
            out=self.wq1[:],
            in_=self.w["a_wq"].ap().rearrange("(c p) f -> p c f", p=128)), "dW")

    def _wload_b2(self):
        gp = self.nc.gpsimd
        self._w2_done = {}
        for src, dst, kind, sname in [("f_wq", self.wq2, "q", "dB0"),
                                      ("f_wk", self.wk2, "k", "dB1"),
                                      ("f_wv", self.wv2, "v", "dB2")]:
            self.wait(gp, sname)
            cnt = self.inc(gp.dma_start(
                out=dst[:],
                in_=self.w[src].ap().rearrange("(c p) f -> p c f", p=128)), sname)
            self._w2_done[kind] = (sname, cnt)

    def _wload_wo2(self):
        gp = self.nc.gpsimd
        self.inc(gp.dma_start(
            out=self.wo2r[:],
            in_=self.w["f_wo"].ap().rearrange("(a d) f -> d a f", d=HD)), "dW")

    def _tr8(self, src_f32, evac_list, banks=None):
        nc = self.nc
        TE = nc.tensor
        if banks is None:
            banks = [self.psG, self.psH]
        if not hasattr(self, "_tr_evac"):
            self._tr_evac = {}
        for g in range(2):
            bi = (self._tr_rot + g) % len(banks)
            ps = banks[bi]
            esem, ecnt = self._tr_evac.get(id(ps), ("Sc", 0))
            self.wait(TE, esem, ecnt)
            self.wait(TE, "V", getattr(self, "_psGH_vfree", 0))
            last = None
            for c4 in range(4):
                c = 4 * g + c4
                last = TE.transpose(ps[:, c4 * 128:(c4 + 1) * 128],
                                    src_f32[:, c * 128:(c + 1) * 128],
                                    self.id_f[:])
            it = self.inc(last, "T")
            self.nc.scalar.wait_ge(self.sem["T"], it)
            self._tr_evac[id(ps)] = evac_list(g, ps)
        self._tr_rot = (self._tr_rot + 2) % len(banks)

    def _phase1(self, b):
        nc = self.nc
        sy = nc.sync
        TE, SC, V = nc.tensor, nc.scalar, nc.vector
        NT = self.NT
        self.wait(TE, "dW", self._w1_done)
        V.memset(_ap(self.vaug[0], 64, [[H * 65, 128], [65, H]]), 1.0)
        V.memset(_ap(self.vaug[1], 64, [[H * 65, 128], [65, H]]), 1.0)
        self.inc(V.memset(self.sums[:, 7:8], 0.0), "V")
        v_done = self.cnt["V"]
        self._p1_vdone = {}
        gp = nc.gpsimd
        for t in range(NT):
            if t % 2 == 0:
                eng, sx = sy, "dX%d" % ((t // 2) % 4)
            else:
                eng, sx = gp, "dG%d" % ((t // 2) % 4)
            self.wait(eng, sx)
            self.wait(eng, "dO%d" % (t % 2))
            self.inc(eng.dma_start(
                out=self.x1[:, t, :],
                in_=self.x_h.ap()[b, t * 128:(t + 1) * 128, :]), sx)
            self.wait(TE, sx)
            self.wait(TE, "V", v_done)
            xt = self.xt1[t % 2]

            def evac(gg, ps, xt=xt):
                out = _ap(xt, 4 * gg * 128, [[1024, 128], [1, 512]])
                if gg % 2 == 0:
                    return ("Sc", self.inc(SC.activation(out, ps[:], AF.Copy),
                                           "Sc"))
                self.wait(V, "T", self.cnt["T"])
                return ("V", self.inc(V.tensor_copy(out, ps[:]), "V"))

            self._tr8(self.x1[:, t, :], evac)
            self.wait(TE, "Sc")
            self.wait(TE, "V")
            tmark = {}
            for ti, (wsb, bhl) in enumerate([(self.wk1, self.bk1_hl),
                                             (self.wv1, self.bv1_hl)]):
                for half in range(2):
                    ps = [self.psA, self.psB, self.psC, self.psD][2 * ti + half]
                    for c in range(8):
                        TE.matmul(ps[:], xt[:, c, :],
                                  wsb[:, c, half * 512:(half + 1) * 512],
                                  start=(c == 0), stop=False)
                    mm = TE.matmul(ps[:], self.ones2_bf[:],
                                   bhl[:, half * 512:(half + 1) * 512],
                                   start=False, stop=True)
                    tmark[(ti, half)] = self.inc(mm, "T")
            for half in range(2):
                ps = [self.psA, self.psB][half]
                self.wait(V, "T", tmark[(0, half)])
                V.tensor_scalar_min(self.esc[:], ps[:], 0.0)
                iv = self.inc(V.memset(self.sums[:, 7:8], 0.0), "V")
                self.wait(SC, "V", iv)
                isc = self.inc(SC.activation(self.esc2[:], self.esc[:], AF.Exp),
                               "Sc")
                self.wait(V, "Sc", isc)
                st = V.scalar_tensor_tensor(
                    self.km[t % 2][:, half * 512:(half + 1) * 512],
                    ps[:], 0.0, self.esc2[:], ALU.max, ALU.add)
                self.inc(st, "V")
            for half in range(2):
                ps = [self.psC, self.psD][half]
                self.wait(V, "T", tmark[(1, half)])
                out = _ap(self.vaug[t % 2], half * 8 * 65,
                          [[H * 65, 128], [65, 8], [1, 64]])
                self.inc(V.tensor_copy(out, ps[:]), "V")
            v_done = self.cnt["V"]
            self._p1_vdone[t] = v_done
            if t == 0 and "d_km" in self.dbg:
                gp = nc.gpsimd
                self.wait(gp, "V", v_done)
                self.inc(gp.dma_start(out=self.dbg["d_km"].ap()[b],
                                      in_=self.km[t % 2][:]), "dW")
                idm = self.inc(gp.dma_start(out=self.dbg["d_vaug"].ap()[b],
                                            in_=self.vaug[t % 2][:]), "dW")
                gp.wait_ge(self.sem["dW"], idm)
                self.wait(V, "dW", idm)
            if t > 0:
                self._p1_kv_accum(t - 1, NT)
        self._p1_kv_accum(NT - 1, NT)
        self.wait(V, "T")
        V.memset(self.kv_z[:], 0.0)
        V.drain()
        for bi, bank in enumerate([self.psE, self.psF]):
            oute = _ap(self.kv_z, (2 * 4 * bi) * 65,
                       [[H * 65, 64], [2 * 65, 4], [1, 65]])
            srce = _ap(bank, 0, [[512, 64], [128, 4], [1, 65]])
            V.tensor_copy(oute, srce)
            V.drain()
            oto = _ap(self.kv_z, (2 * 4 * bi + 1) * 65 + 64 * H * 65,
                      [[H * 65, 64], [2 * 65, 4], [1, 65]])
            srco = _ap(bank, 64 * 512, [[512, 64], [128, 4], [1, 65]])
            self.inc(V.tensor_copy(oto, srco), "V")
            V.drain()
        if "d_kv" in self.dbg:
            gp = nc.gpsimd
            self.wait(gp, "V")
            idm = self.inc(gp.dma_start(
                out=self.dbg["d_kv"].ap()[b],
                in_=self.kv_z[:].rearrange("p c k -> p (c k)")), "dW")
            gp.wait_ge(self.sem["dW"], idm)

    def _p1_kv_accum(self, t, NT):
        TE = self.nc.tensor
        self.wait(TE, "V", self._p1_vdone[t])
        last = None
        for h in range(H):
            bank = self.psE if h < 8 else self.psF
            slot = (h % 8) // 2
            bp = (h % 2) * 64
            last = TE.matmul(bank[bp:bp + 64, slot * 128:slot * 128 + 65],
                             self.km[t % 2][:, h * 64:(h + 1) * 64],
                             _ap(self.vaug[t % 2], h * 65,
                                 [[H * 65, 128], [1, 65]]),
                             start=(t == 0 and h % 8 < 2),
                             stop=(t == NT - 1), skip_group_check=True)
        self.inc(last, "T")

    def _phase2(self, b):
        nc = self.nc
        sy = nc.sync
        TE, SC, V = nc.tensor, nc.scalar, nc.vector
        NG = self.NG
        self.wait(TE, "dW", self._w1_done)
        self.qm_evac = {}
        self._ring_used = {}
        self._xsum_pending = None
        self._ynorm_v = {}
        self._ynT_done = {}
        V.memset(self.xsum[:], 0.0)
        self.inc(V.memset(self.sums[:], 0.0), "V")
        ring_cnt = 0
        for g in range(NG):
            for dt_ in range(4):
                t = g * 4 + dt_

                def evac(gg, ps, dt_=dt_):
                    out = _ap(self.xTg, (4 * gg) * 512 + dt_ * 128,
                              [[8 * 512, 128], [512, 4], [1, 128]])
                    if gg % 2 == 0:
                        return ("Sc", self.inc(
                            SC.activation(out, ps[:], AF.Copy), "Sc"))
                    self.wait(V, "T", self.cnt["T"])
                    return ("V", self.inc(V.tensor_copy(out, ps[:]), "V"))

                self._tr8(self.x1[:, t, :], evac)
            self.wait(TE, "Sc")
            self.wait(TE, "V")
            self._emit_xsum(b)
            if self.p2_level < 2:
                continue
            for hc in range(8):
                ps = [self.psA, self.psB, self.psC, self.psD][hc % 4]
                if hc >= 4:
                    self.wait(TE, "V", self.qm_evac[hc - 4])
                mm = None
                for c in range(8):
                    mm = TE.matmul(ps[:], self.wq1[:, c, hc * 128:(hc + 1) * 128],
                                   self.xTg[:, c, :], start=(c == 0), stop=(c == 7))
                it = self.inc(mm, "T")
                self.wait(SC, "T", it)
                self.wait(SC, "V", self.qm_evac.get(hc - 1, 0))
                isc = self.inc(SC.activation(self.esc2[:], ps[:], AF.Relu,
                                             bias=self.bq1_fm[:, hc:hc + 1]), "Sc")
                self.wait(V, "Sc", isc)
                V.tensor_scalar(self.esc[:], ps[:], self.bq1_fm[:, hc:hc + 1],
                                None, ALU.add)
                V.drain()
                V.tensor_sub(self.esc[:], self.esc[:], self.esc2[:])
                iv = self.inc(V.memset(self.sums[:, 7:8], 0.0), "V")
                self.wait(SC, "V", iv)
                isc = self.inc(SC.activation(self.esc[:], self.esc[:], AF.Exp),
                               "Sc")
                self.wait(V, "Sc", isc)
                self.qm_evac[hc] = self.inc(
                    V.tensor_add(self.qm[:, hc, :], self.esc[:], self.esc2[:]),
                    "V")
            self.wait(TE, "V")
            if self.p2_level < 3:
                continue
            for dt_ in range(4):
                self._p2_y_tile(dt_)
                if dt_ > 0:
                    self._p2_ynt(dt_ - 1)
            self._p2_ynt(3)
            if self.p2_level < 4:
                continue
            if g == 0 and "d_qm" in self.dbg:
                gpq = nc.gpsimd
                self.wait(gpq, "V")
                self.wait(gpq, "Sc")
                self.inc(gpq.dma_start(out=self.dbg["d_qm"].ap()[b],
                                       in_=self.qm[:]), "dW")
                idm = self.inc(gpq.dma_start(out=self.dbg["d_ynt"].ap()[b],
                                             in_=self.ynTg[:]), "dW")
                gpq.wait_ge(self.sem["dW"], idm)
                self.wait(TE, "dW", idm)
            self.wait(TE, "Sc")
            for half in range(2):
                for c in range(8):
                    slot = ring_cnt % 8
                    sr = "dR%d" % slot
                    self.wait(sy, sr)
                    self.wait(sy, "T", self._ring_used.get(slot, 0))
                    self.inc(sy.dma_start(
                        out=self.wo1_ring[slot][:],
                        in_=self.wo1r_h.ap()[c * 128:(c + 1) * 128,
                                             half * 512:(half + 1) * 512]), sr)
                    ring_cnt += 1
                    self.wait(TE, sr)
                    if c == 0:
                        self.wait(TE, "V")
                    for dt_ in range(4):
                        psO = [self.psA, self.psB, self.psC, self.psD][dt_]
                        mm = TE.matmul(
                            psO[:],
                            _ap(self.ynTg, (dt_ % 2) * 64 + c * 512,
                                [[8 * 512, 128], [1, 64]]).bitcast(F32R)
                            if False else
                            self.ynTg[:, c, dt_ * 128:(dt_ + 1) * 128],
                            self.wo1_ring[slot][:],
                            start=(c == 0), stop=False)
                        if c == 7:
                            mm = TE.matmul(psO[:], self.ones2_bf[:],
                                           self.bo1_hl[:, half * 512:(half + 1) * 512],
                                           start=False, stop=True)
                    if c == 7:
                        self.inc(mm, "T")
                    else:
                        self.inc(mm, "T")
                    self._ring_used[slot] = self.cnt["T"]
                if self.p2_level < 5:
                    continue
                self.wait(V, "T")
                for dt_ in range(4):
                    t = g * 4 + dt_
                    psO = [self.psA, self.psB, self.psC, self.psD][dt_]
                    st = V.scalar_tensor_tensor(
                        self.x1[:, t, half * 512:(half + 1) * 512], psO[:], 0.0,
                        self.x1[:, t, half * 512:(half + 1) * 512],
                        ALU.add, ALU.add,
                        accum_out=self.sums[:, 2 * dt_ + half:2 * dt_ + half + 1])
                self.inc(V.memset(self.sums[:, 8:9], 0.0), "V")
                V.drain()
            if self.p2_level < 6:
                continue
            for dt_ in range(4):
                t = g * 4 + dt_
                self._ln_finalize(self.x1[:, t, :],
                                  self.sums[:, 2 * dt_:2 * dt_ + 1],
                                  self.sums[:, 2 * dt_ + 1:2 * dt_ + 2],
                                  self.x1[:, t, :])
            self._ln_done = self.cnt["V"]
            self._psGH_vfree = self.cnt["V"]
            self._xsum_pending = g
            if "d_xg" in self.dbg:
                gpx = nc.gpsimd
                self.wait(gpx, "V")
                idm = self.inc(gpx.dma_start(
                    out=self.dbg["d_xg"].ap()[b, g][None, :], in_=self.xsum[:]),
                    "dW")
                gpx.wait_ge(self.sem["dW"], idm)
                self.wait(V, "dW", idm)
        self._emit_xsum(b)
        if "d_x1" in self.dbg:
            self.wait(sy, "V")
            for t in range(self.NT):
                self.inc(sy.dma_start(
                    out=self.dbg["d_x1"].ap()[b, t * 128:(t + 1) * 128, :],
                    in_=self.x1[:, t, :]), "dX")
            self.wait(sy, "dX")

    def _emit_xsum(self, b):
        if getattr(self, "_xsum_pending", None) is None:
            return
        g = self._xsum_pending
        self._xsum_pending = None
        nc = self.nc
        TE, V = nc.tensor, nc.vector
        self.wait(TE, "V", self._ln_done)
        self.wait(TE, "V", getattr(self, "_xsum_vfree", 0))
        last = None
        for dt_ in range(4):
            t = g * 4 + dt_
            for half in range(2):
                ps = [self.psE, self.psF][half]
                last = TE.matmul(ps[0:1, :], self.ones128_f[:],
                                 self.x1[:, t, half * 512:(half + 1) * 512],
                                 start=(dt_ == 0), stop=(dt_ == 3),
                                 skip_group_check=True)
        it = self.inc(last, "T")
        self.wait(V, "T", it)
        for half in range(2):
            ps = [self.psE, self.psF][half]
            V.tensor_add(self.xsum[0:1, half * 512:(half + 1) * 512],
                         self.xsum[0:1, half * 512:(half + 1) * 512],
                         ps[0:1, :])
        V.drain()
        self._xsum_vfree = self.inc(V.memset(self.sums[:, 8:9], 0.0), "V")

    def _p2_y_tile(self, dt_):
        nc = self.nc
        TE, SC, V = nc.tensor, nc.scalar, nc.vector
        lvl = getattr(self, "p2y_level", 9)
        if not hasattr(self, "_y_pair_free"):
            self._y_pair_free = {}
        pairs = [[self.psC, self.psD], [self.psA, self.psB]]
        marks = {}
        for hg in range(2):
            bpair = pairs[hg]
            self.wait(TE, "V", self._y_pair_free.get(hg, 0))
            last = None
            for hh in range(8):
                h = hg * 8 + hh
                bank = bpair[0] if hh < 4 else bpair[1]
                slot = hh % 4
                qm_lhs = self.qm[:, h // 2, dt_ * 128:(dt_ + 1) * 128]
                kv_rhs = self.kv_z[:, h, 0:65]
                last = TE.matmul(bank[:, slot * 128:slot * 128 + 65], qm_lhs,
                                 kv_rhs, start=True, stop=True)
            marks[hg] = self.inc(last, "T")
        banks4 = [(hg * 8 + bi * 4, pairs[hg][bi])
                  for hg in range(2) for bi in range(2)]
        self.wait(V, "T", marks[1])
        self.wait(V, "T", self._ynT_done.get(dt_ % 2, 0))
        if lvl >= 2:
            V.drain()
            for i, (h0, bank) in enumerate(banks4):
                V.tensor_scalar(self.z16[:, i * 4:(i + 1) * 4],
                                _ap(bank, 64, [[512, 128], [128, 4]]),
                                EPS_ATTN, None, ALU.add)
            V.drain()
            V.reciprocal(self.z16[:], self.z16[:])
            V.drain()
        iv = None
        for i, (h0, bank) in enumerate(banks4):
            dst = _ap(self.ynorm[dt_ % 2], h0 * 64,
                      [[1024, 128], [64, 4], [1, 64]])
            src = _ap(bank, 0, [[512, 128], [128, 4], [1, 64]])
            if lvl < 2:
                iv = self.inc(V.tensor_copy(dst, src), "V")
            else:
                zb = _ap(self.z16, i * 4, [[H, 128], [1, 4], [0, 64]])
                iv = self.inc(V.tensor_tensor(dst, src, zb, ALU.mult), "V")
            if i == 1:
                self._y_pair_free[0] = iv
        self._y_pair_free[1] = iv
        self._ynorm_v[dt_] = self.cnt["V"]

    def _p2_ynt(self, dt_):
        nc = self.nc
        TE, SC = nc.tensor, nc.scalar

        def evac(gg, ps, dt_=dt_):
            out = _ap(self.ynTg, (4 * gg) * 512 + dt_ * 128,
                      [[8 * 512, 128], [512, 4], [1, 128]])
            with nc.allow_low_precision(reason="f32r operand prep"):
                return ("Sc", self.inc(SC.activation(out, ps[:], AF.Copy),
                                       "Sc"))

        self.wait(TE, "V", self._ynorm_v[dt_])
        self._tr8(self.ynorm[dt_ % 2][:], evac)
        self._ynT_done[dt_ % 2] = self.cnt["T"]

    def _ln_finalize(self, xpre, s0, s1, x_out):
        nc = self.nc
        SC, V = nc.scalar, nc.vector
        for half in range(2):
            V.scalar_tensor_tensor(
                [self.esc, self.esc2][half][:],
                xpre[:, half * 512:(half + 1) * 512], 1.0,
                xpre[:, half * 512:(half + 1) * 512], ALU.mult, ALU.mult,
                accum_out=self.sums[:, 8 + half:9 + half])
        V.drain()
        V.tensor_add(self.stat[:, 0:1], s0, s1)
        V.tensor_add(self.stat[:, 1:2], self.sums[:, 8:9], self.sums[:, 9:10])
        V.drain()
        V.scalar_tensor_tensor(self.stat[:, 2:3], self.stat[:, 0:1], 1.0 / D,
                               self.stat[:, 0:1], ALU.mult, ALU.mult)
        V.drain()
        iv = self.inc(V.tensor_sub(self.stat[:, 1:2], self.stat[:, 1:2],
                                   self.stat[:, 2:3]), "V")
        self.wait(SC, "V", iv)
        SC.activation(self.stat[:, 4:5], self.stat[:, 1:2], AF.Ln,
                      scale=1.0 / D, bias=self.epsc[:, 0:1])
        SC.drain()
        isc = self.inc(SC.activation(self.stat[:, 2:3], self.stat[:, 4:5],
                                     AF.Exp, scale=-0.5), "Sc")
        self.wait(V, "Sc", isc)
        V.scalar_tensor_tensor(self.stat[:, 3:4], self.stat[:, 0:1], -1.0 / D,
                               self.stat[:, 2:3], ALU.mult, ALU.mult)
        V.drain()
        self.inc(V.tensor_scalar(x_out, xpre, self.stat[:, 2:3],
                                 self.stat[:, 3:4], ALU.mult, ALU.add), "V")

    def _router(self, b):
        nc = self.nc
        sy, gp = nc.sync, nc.gpsimd
        TE, SC, V = nc.tensor, nc.scalar, nc.vector
        self.wait(gp, "V")
        if "d_xsum" in self.dbg:
            idm = self.inc(gp.dma_start(out=self.dbg["d_xsum"].ap()[b][None, :],
                                        in_=self.xsum[:]), "dW")
            gp.wait_ge(self.sem["dW"], idm)
        self.inc(gp.dma_start(
            out=self.xsum_fm[:],
            in_=_ap(self.xsum, 0, [[1024, 1], [8, 128], [1, 8]]),
            allow_slow_non_contiguous=True), "dW")
        gp.wait_ge(self.sem["dW"], self.cnt["dW"])
        self.wait(TE, "dW")
        last = None
        for c in range(8):
            last = TE.matmul(self.psA[0:1, 0:H], self.xsum_fm[:, c:c + 1],
                             self.wr_f[:, c, :], start=(c == 0), stop=(c == 7))
        it = self.inc(last, "T")
        self.wait(V, "T", it)
        V.scalar_tensor_tensor(self.r_row[:], self.br_row[:], float(self.S),
                               self.psA[0:1, 0:H], ALU.mult, ALU.add)
        V.drain()
        V.max(out=self.vmax8[:], in_=self.r_row[:])
        V.drain()
        iv = self.inc(V.tensor_scalar(self.active[:], self.r_row[:],
                                      self.vmax8[:, NA - 1:NA], None, ALU.is_ge),
                      "V")
        V.memset(self.xsum[:], 0.0)
        self.wait(gp, "V", iv)
        self.inc(gp.dma_start(out=self.activeT[:],
                              in_=_ap(self.active, 0, [[16, 1], [1, 16]]),
                              allow_slow_non_contiguous=True), "dW")
        gp.wait_ge(self.sem["dW"], self.cnt["dW"])
        self.wait(TE, "dW")
        mm = TE.matmul(self.psC[0:H, 0:1], self.u16_f[:], self.activeT[:],
                       start=True, stop=True)
        it = self.inc(mm, "T")
        self.wait(V, "T", it)
        last = None
        for a in range(NA):
            last = V.scalar_tensor_tensor(self.sel[:, a:a + 1], self.psC[0:H, 0:1],
                                          float(a), self.activeT[:],
                                          ALU.is_equal, ALU.mult)
        iv = self.inc(last, "V")
        V.drain()
        s_b = _ap(self.sel, 0, [[NA, H], [0, 8], [1, NA]])
        m_b = _ap(self.selmask_f, 0, [[32, H], [NA, 8], [1, NA]])
        o_b = _ap(self.selp, 0, [[32, H], [NA, 8], [1, NA]])
        iv = self.inc(V.tensor_tensor(o_b, s_b, m_b, ALU.mult), "V")
        self.wait(TE, "V", iv)
        mm = TE.matmul(self.psD[:, 0:32], self.lpar_f[:], self.selp[:],
                       start=True, stop=True)
        it = self.inc(mm, "T")
        self.wait(V, "T", it)
        last = None
        for c in range(8):
            src = _ap(self.psD, c * 4, [[512, 128], [1, NA], [0, 64]])
            i2b = _ap(self.i2_f, 0, [[64, 128], [0, NA], [1, 64]])
            dst = _ap(self.G, c * 256, [[8 * 256, 128], [64, NA], [1, 64]])
            last = V.tensor_tensor(dst, src, i2b, ALU.mult)
        self.inc(last, "V")
        self.wait(TE, "V")
        if "d_r" in self.dbg:
            self.wait(sy, "V")
            self.inc(sy.dma_start(out=self.dbg["d_r"].ap()[b][None, :],
                                  in_=self.r_row[:]), "dX")
        if "d_sel" in self.dbg:
            self.wait(sy, "V")
            self.inc(sy.dma_start(out=self.dbg["d_sel"].ap()[b], in_=self.sel[:]),
                     "dX")

    def _phase3(self, b):
        nc = self.nc
        sy, gp = nc.sync, nc.gpsimd
        TE, SC, V = nc.tensor, nc.scalar, nc.vector
        for g in range(self.NG):
            for dt_ in range(4):
                t = g * 4 + dt_

                def evac(gg, ps, dt_=dt_):
                    out = _ap(self.x1T, (4 * gg) * 512 + dt_ * 128,
                              [[8 * 512, 128], [512, 4], [1, 128]])
                    if gg % 2 == 0:
                        return ("Sc", self.inc(
                            SC.activation(out, ps[:], AF.Copy), "Sc"))
                    self.wait(V, "T", self.cnt["T"])
                    return ("V", self.inc(V.tensor_copy(out, ps[:]), "V"))

                self._tr8(self.x1[:, t, :], evac,
                          banks=[self.psA, self.psB, self.psG, self.psH])
            self.wait(TE, "Sc")
            self.wait(TE, "V")
            for ti, (wsb, bias, kind) in enumerate([
                    (self.wq2, self.bq2_fm, "q"),
                    (self.wk2, self.bk2_fm, "k"),
                    (self.wv2, self.bv2_fm, "v")]):
                self.wait(TE, "V")
                sname, scnt = self._w2_done[kind]
                self.wait(TE, sname, scnt)
                evacd = {}
                for hc in range(9):
                    if hc < 8:
                        ps = [self.psA, self.psB][hc % 2]
                        if hc >= 2:
                            self.wait(TE, "V", evacd[hc - 2])
                        mm = None
                        for c in range(8):
                            mm = TE.matmul(ps[:],
                                           wsb[:, c, hc * 128:(hc + 1) * 128],
                                           self.x1T[:, c, :],
                                           start=(c == 0), stop=(c == 7))
                        it = self.inc(mm, "T")
                        self.wait(V, "T", it)
                        evacd[hc] = self.inc(
                            V.tensor_scalar(self.qkv_full[:, hc, :], ps[:],
                                            bias[:, hc:hc + 1], None, ALU.add),
                            "V")
                    cg = hc - 1
                    if cg >= 0:
                        self.wait(TE, "V", evacd[cg])
                        if cg == 0:
                            self.wait(TE, "Sc", getattr(self, "_gps_free", 0))
                        for j in range(2):
                            psj = [self.psC, self.psD][j]
                            mm = TE.matmul(psj[:],
                                           self.G[:, cg, j * 128:(j + 1) * 128],
                                           self.qkv_full[:, cg, :],
                                           start=(cg == 0), stop=(cg == 7),
                                           skip_group_check=True)
                            if cg == 7:
                                self.inc(mm, "T")
                for j in range(2):
                    ps = [self.psC, self.psD][j]
                    it = self.cnt["T"]
                    if kind == "v":
                        self.wait(SC, "T", it)
                        isc = self.inc(SC.activation(self.vsfm[:], ps[:],
                                                     AF.Copy), "Sc")
                        self.wait(TE, "Sc", isc)
                        tr = None
                        for dt_ in range(4):
                            out_bf = self.psE[:, dt_ * 64:(dt_ + 1) * 64].bitcast(
                                BF16)
                            tr = TE.transpose(out_bf,
                                              self.vsfm[:, dt_ * 128:(dt_ + 1) * 128],
                                              self.id_bf[:])
                        it2 = self.inc(tr, "T")
                        self.wait(SC, "T", it2)
                        isc = None
                        for dt_ in range(4):
                            t = g * 4 + dt_
                            dst = _ap(self.Vs, t * NA * 65 + (2 * j) * 65,
                                      [[self.NT * NA * 65, 128], [65, 2], [1, 64]])
                            srcb = self.psE[:, dt_ * 64:(dt_ + 1) * 64].bitcast(
                                BF16)
                            src2 = bass.AP(tensor=srcb.tensor, offset=srcb.offset,
                                           ap=[list(srcb.ap[0]), [64, 2], [1, 64]])
                            isc = self.inc(SC.activation(dst, src2, AF.Copy),
                                           "Sc")
                        self.wait(TE, "Sc", isc)
                        self._gps_free = self.cnt["Sc"]
                    else:
                        dsts = self.Qs if kind == "q" else self.Ks
                        scale = 0.125 if kind == "q" else 1.0
                        self.wait(SC, "T", it)
                        self.inc(SC.activation(
                            dsts[2 * j][0:64, g * 512:(g + 1) * 512],
                            ps[0:64, :], AF.Copy, scale=scale), "Sc")
                        self.wait(SC, "dW", getattr(self, "_shift_idm", 0))
                        isc = self.inc(SC.activation(
                            self.shiftb[64:128, :], ps[64:128, :], AF.Copy,
                            scale=scale), "Sc")
                        self.wait(gp, "Sc", isc)
                        idm = self.inc(gp.dma_start(
                            out=dsts[2 * j + 1][0:64, g * 512:(g + 1) * 512],
                            in_=self.shiftb[64:128, :]), "dW")
                        gp.wait_ge(self.sem["dW"], idm)
                        self._shift_idm = idm
                        self._gps_free = self.cnt["Sc"]

    def _bounds(self, b):
        nc = self.nc
        sy = nc.sync
        TE, SC, V = nc.tensor, nc.scalar, nc.vector
        S = self.S
        nch = S // 512
        V.memset(self.sqb[0][64:65, :], 1.0)
        V.memset(self.sqb[1][64:65, :], 1.0)
        units = []
        for a in range(NA):
            units.append((a, "k", self.Ks[a]))
            units.append((a, "q", self.Qs[a]))
        mm_done, ln_done, exp_done, fin_done = {}, {}, {}, {}
        for u, (a, which, ten) in enumerate(units):
            sq = self.sqb[u % 2]
            nac = self.nac2[u % 2]
            pb = [self.psA, self.psB] if u % 2 == 0 else [self.psC, self.psD]
            self.wait(V, "T", mm_done.get(u - 2, 0))
            iv = self.inc(V.tensor_tensor(sq[0:64, :], ten[0:64, :],
                                          ten[0:64, :], ALU.mult), "V")
            self.wait(TE, "V", iv)
            self.wait(TE, "Sc", ln_done.get(u - 2, 0))
            mm = None
            for ch in range(nch):
                bp = (ch // 2) * 32
                mm = TE.matmul(pb[ch % 2][bp:bp + 1, 0:512], self.ones65_bf[:],
                               sq[:, ch * 512:(ch + 1) * 512],
                               start=True, stop=True)
            mm_done[u] = self.inc(mm, "T")
            self.wait(SC, "T", mm_done[u])
            self.wait(SC, "V", fin_done.get(u - 2, 0))
            for ch in range(nch):
                bp = (ch // 2) * 32
                SC.activation(nac[0:1, ch * 512:(ch + 1) * 512],
                              pb[ch % 2][bp:bp + 1, 0:512], AF.Ln)
            SC.drain()
            exp_done[u] = self.inc(SC.activation(nac[:], nac[:], AF.Exp,
                                                 scale=0.5), "Sc")
            ln_done[u] = exp_done[u]
            self.wait(V, "Sc", exp_done[u])
            if which == "k":
                iv = self.inc(V.reduce_max(self.kmax_row[0:1, a:a + 1],
                                           nac[:], AX.X), "V")
                V.drain()
            else:
                iv = self.inc(V.tensor_scalar(
                    ten[64:65, :], nac[:],
                    self.kmax_row[0:1, a:a + 1], -1.0,
                    ALU.mult, ALU.mult), "V")
                V.drain()
                V.memset(self.Ks[a][64:65, :], 1.0)
                self.inc(V.memset(self.vmax8[0:1, 1:2], 0.0), "V")
            fin_done[u] = self.cnt["V"]
        self.wait(TE, "V")
        gp = self.nc.gpsimd
        if "d_qs" in self.dbg:
            self.wait(gp, "V")
            idm = None
            for a in range(NA):
                self.inc(gp.dma_start(out=self.dbg["d_qs"].ap()[b, a],
                                      in_=self.Qs[a][:]), "dW")
                idm = self.inc(gp.dma_start(out=self.dbg["d_ks"].ap()[b, a],
                                            in_=self.Ks[a][:]), "dW")
            gp.wait_ge(self.sem["dW"], idm)
        if "d_vs" in self.dbg:
            self.wait(gp, "V")
            idm = None
            for t in range(self.NT):
                idm = self.inc(gp.dma_start(
                    out=self.dbg["d_vs"].ap()[b, t * 128:(t + 1) * 128, :],
                    in_=self.Vs[:, t, :]), "dW")
            gp.wait_ge(self.sem["dW"], idm)

    def _phase45(self, b, last=False):
        nc = self.nc
        sy = nc.sync
        TE, SC, V = nc.tensor, nc.scalar, nc.vector
        S, NT = self.S, self.NT
        NQC = S // 1024
        dO_tile = {}
        self.wait(TE, "dW")
        for qc in range(min(NQC, self.p45_qc)):
            otn_done = {}
            for a in range(NA):
                exp_done = {}
                pv_done = {}
                psO_pair = [[self.psE, self.psF], [self.psG, self.psH]][a % 2]
                if a < 2:
                    self.wait(TE, "V")
                sbank = [self.psA, self.psB, self.psC, self.psD]
                for kt in range(NT + 1):
                    if kt < NT:
                        for hf in range(2):
                            ei = 2 * kt + hf
                            psS = sbank[ei % 4]
                            self.wait(TE, "Sc", exp_done.get(
                                ei - 4, getattr(self, "_rep_sc", 0)))
                            mm = TE.matmul(
                                psS[:],
                                self.Ks[a][:, kt * 128:(kt + 1) * 128],
                                self.Qs[a][:, qc * 1024 + hf * 512:
                                           qc * 1024 + (hf + 1) * 512],
                                start=True, stop=True)
                            it = self.inc(mm, "T")
                            self.wait(SC, "T", it)
                            self.wait(SC, "T", pv_done.get(kt - 2, 0))
                            isc = self.inc(SC.activation(
                                self.Pt[kt % 2][:, hf * 512:(hf + 1) * 512],
                                psS[:], AF.Exp), "Sc")
                            exp_done[ei] = isc
                    kp = kt - 1
                    if kp >= 0:
                        self.wait(TE, "Sc", exp_done[2 * kp + 1])
                        if kp == 0 and a >= 2:
                            self.wait(TE, "V", otn_done.get(a - 2, 0))
                        mm2 = None
                        for hf in range(2):
                            psO = psO_pair[hf]
                            mm2 = TE.matmul(
                                psO[0:65, :],
                                self.Vs[:, kp, a * 65:(a + 1) * 65],
                                self.Pt[kp % 2][:, hf * 512:(hf + 1) * 512],
                                start=(kp == 0), stop=(kp == NT - 1),
                                skip_group_check=True)
                        pv_done[kp] = self.inc(mm2, "T")
                self.wait(SC, "T")
                with nc.allow_low_precision(reason="f32r operand prep"):
                    for hf in range(2):
                        psO = psO_pair[hf]
                        SC.activation(self.nrm_f[0:1, hf * 512:(hf + 1) * 512],
                                      psO[64:65, :], AF.Ln)
                    SC.drain()
                    isc = self.inc(SC.activation(self.rec_row[:],
                                                 self.nrm_f[:], AF.Exp,
                                                 scale=-1.0), "Sc")
                self.wait(TE, "Sc", isc)
                psR_pair = [self.psC, self.psD]
                last_mm = None
                for hf in range(2):
                    last_mm = TE.matmul(psR_pair[hf][0:64, :], self.onesr64_r[:],
                                        self.rec_row[0:1, hf * 512:(hf + 1) * 512],
                                        start=True, stop=True)
                it = self.inc(last_mm, "T")
                self.wait(SC, "T", it)
                self.wait(SC, "V", otn_done.get(a - 1, 0))
                isc = None
                for hf in range(2):
                    isc = self.inc(SC.activation(
                        self.rep_sb[:, hf * 512:(hf + 1) * 512],
                        psR_pair[hf][0:64, :], AF.Copy), "Sc")
                self._rep_sc = isc
                self.wait(V, "Sc", isc)
                self.wait(V, "T", pv_done[NT - 1])
                last_v = None
                for hf in range(2):
                    psO = psO_pair[hf]
                    with nc.allow_low_precision(reason="f32r operand prep"):
                        last_v = V.tensor_tensor(
                            self.OTn[0:64, a, hf * 512:(hf + 1) * 512],
                            psO[0:64, :],
                            self.rep_sb[:, hf * 512:(hf + 1) * 512], ALU.mult)
                otn_done[a] = self.inc(last_v, "V")
                if "d_ot" in self.dbg:
                    gp = nc.gpsimd
                    self.wait(gp, "V")
                    idm = self.inc(gp.dma_start(
                        out=self.dbg["d_ot"].ap()[b, a, :,
                                                  qc * 1024:(qc + 1) * 1024],
                        in_=self.OTn[0:64, a, :]), "dW")
                    gp.wait_ge(self.sem["dW"], idm)
                    self.wait(V, "dW", idm)
            self.wait(TE, "V")

            if self.p45_level < 2:
                continue
            res_done = {}
            v_at_start = self.cnt["V"]
            for dt_ in range(8):
                t = qc * 8 + dt_
                opair = [[self.psA, self.psB], [self.psC, self.psD]][dt_ % 2]
                self.wait(TE, "V", res_done.get(dt_ - 2, v_at_start))
                for half in range(2):
                    ps = opair[half]
                    for a in range(NA):
                        TE.matmul(ps[:],
                                  self.OTn[0:64, a, dt_ * 128:(dt_ + 1) * 128],
                                  self.wo2r[:, a, half * 512:(half + 1) * 512],
                                  start=(a == 0), stop=False)
                    mm = TE.matmul(ps[:], self.ones2_bf[:],
                                   self.bo2_hl[:, half * 512:(half + 1) * 512],
                                   start=False, stop=True)
                    self.inc(mm, "T")
                so = "dO%d" % (t % 2)
                self.wait(V, so, dO_tile.get(t - 2, 0))
                self.wait(V, "T")
                for half in range(2):
                    psO = opair[half]
                    V.scalar_tensor_tensor(
                        self.x1[:, t, half * 512:(half + 1) * 512], psO[:], 0.0,
                        self.x1[:, t, half * 512:(half + 1) * 512],
                        ALU.add, ALU.add,
                        accum_out=self.sums[:, half:half + 1])
                self.inc(V.memset(self.sums[:, 8:9], 0.0), "V")
                V.drain()
                self._ln_finalize(self.x1[:, t, :], self.sums[:, 0:1],
                                  self.sums[:, 1:2], self.x1[:, t, :])
                res_done[dt_] = self.cnt["V"]
                self.wait(sy, "V")
                dO_tile[t] = self.inc(
                    sy.dma_start(out=self.out_h.ap()[b, t * 128:(t + 1) * 128, :],
                                 in_=self.x1[:, t, :]), so)
        if not last:
            self._wload_b1(first=False)


def _pack_inputs(inputs):
    g = lambda n: np.ascontiguousarray(np.asarray(inputs[n], np.float32))
    m = {}
    m["a_wq"] = g("L0_attn_wq").reshape(D, D).copy()
    m["a_wk"] = g("L0_attn_wk").reshape(D, D).copy()
    m["a_wv"] = g("L0_attn_wv").reshape(D, D).copy()
    m["a_wo"] = g("L0_attn_wo").reshape(D, D).copy()
    m["a_bq"] = g("L0_attn_bq").reshape(D).copy()
    m["a_bk"] = g("L0_attn_bk").reshape(D).copy()
    m["a_bv"] = g("L0_attn_bv").reshape(D).copy()
    m["a_bo"] = g("L0_attn_bo").reshape(D).copy()
    m["f_wq"] = g("L0_ffn_Wq")
    m["f_wk"] = g("L0_ffn_Wk")
    m["f_wv"] = g("L0_ffn_Wv")
    m["f_wo"] = g("L0_ffn_Wo")
    m["f_bq"] = g("L0_ffn_bq")
    m["f_bk"] = g("L0_ffn_bk")
    m["f_bv"] = g("L0_ffn_bv")
    m["f_bo"] = g("L0_ffn_bo")
    m["f_wr"] = g("L0_ffn_Wr")
    m["f_br"] = g("L0_ffn_br")
    return m


def kernel(**inputs):
    x = np.ascontiguousarray(np.asarray(inputs["L0_attn_x"], np.float32))
    B, S, _ = x.shape
    nb = B // N_CORES
    prog = Prog(S=S, NB=nb)
    wm = _pack_inputs(inputs)
    in_maps = []
    for core in range(N_CORES):
        im = dict(wm)
        im["x"] = np.ascontiguousarray(x[core * nb:(core + 1) * nb])
        in_maps.append(im)
    res = run_bass_kernel_spmd(prog.nc, in_maps, core_ids=list(range(N_CORES)))
    out = np.concatenate([r["out"] for r in res.results], axis=0)
    return out.astype(np.float32)


# version 14
# speedup vs baseline: 1.0543x; 1.0097x over previous
import numpy as np

import concourse.bass as bass
import concourse.mybir as mybir
from concourse.bass_utils import run_bass_kernel_spmd

F32 = mybir.dt.float32
F32R = mybir.dt.float32r
BF16 = mybir.dt.bfloat16
AF = mybir.ActivationFunctionType
ALU = mybir.AluOpType
AX = mybir.AxisListType

D = 1024
H = 16
HD = 64
NA = 4
EPS_ATTN = 1e-6
EPS_LN = 1e-5
N_CORES = 8

WNAMES = ["a_wq", "a_wk", "a_wv", "a_wo", "a_bq", "a_bk", "a_bv", "a_bo",
          "f_wq", "f_wk", "f_wv", "f_wo", "f_bq", "f_bk", "f_bv", "f_bo",
          "f_wr", "f_br"]


def _ap(t, offset, dims):
    return bass.AP(tensor=t[:].tensor, offset=offset, ap=[list(d) for d in dims])


class Prog:
    def __init__(self, S=2048, NB=2, debug=(), stop_after=None, p2_level=9,
                 p2y_level=9, p45_level=9, p45_qc=99):
        assert S % 1024 == 0
        self.stop_after = stop_after
        self.p2_level = p2_level
        self.p2y_level = p2y_level
        self.p45_level = p45_level
        self.p45_qc = p45_qc
        self.S, self.NB = S, NB
        self.NT = S // 128
        self.NG = self.NT // 4
        self.debug = set(debug)
        nc = bass.Bass("TRN2", target_bir_lowering=False)
        self.nc = nc

        dt = nc.dram_tensor
        self.x_h = dt("x", [NB, S, D], F32, kind="ExternalInput")
        self.out_h = dt("out", [NB, S, D], F32, kind="ExternalOutput")
        shp = {"a_wq": [D, D], "a_wk": [D, D], "a_wv": [D, D], "a_wo": [D, D],
               "a_bq": [D], "a_bk": [D], "a_bv": [D], "a_bo": [D],
               "f_wq": [D, D], "f_wk": [D, D], "f_wv": [D, D],
               "f_wo": [NA * HD, D], "f_bq": [D], "f_bk": [D], "f_bv": [D],
               "f_bo": [D], "f_wr": [D, H], "f_br": [H]}
        self.w = {n: dt(n, shp[n], F32, kind="ExternalInput") for n in WNAMES}
        self.wo1r_h = dt("wo1r", [D, D], F32R, kind="Internal")
        self.dbg = {}
        for name, s in [("d_qm", [NB, 128, 8, 512]), ("d_ynt", [NB, 128, 8, 512]),
                        ("d_xg", [NB, 8, D]), ("d_km", [NB, 128, D]),
                        ("d_vaug", [NB, 128, H * 65]), ("d_xsum", [NB, D]),
                        ("d_kv", [NB, 128, 16 * 65]),
                        ("d_x1", [NB, S, D]), ("d_r", [NB, H]),
                        ("d_sel", [NB, H, NA]),
                        ("d_qs", [NB, NA, 65, S]), ("d_ks", [NB, NA, 65, S]),
                        ("d_vs", [NB, S, NA * 65]), ("d_ot", [NB, NA, 64, S])]:
            if name in self.debug:
                self.dbg[name] = dt(name, s, F32, kind="ExternalOutput")

        it = nc.inline_tensor
        self.c_id = it(np.eye(128, dtype=np.float32), "c_id")
        self.c_ones2 = it(np.ones((2, 128), np.float32), "c_ones2")
        self.c_ones64 = it(np.ones((64, 1), np.float32), "c_ones64")
        o65 = np.ones((65, 1), np.float32); o65[64, 0] = 1e-30
        self.c_ones65 = it(o65, "c_ones65")
        self.c_ones128 = it(np.ones((128, 1), np.float32), "c_ones128")
        self.c_onesr64 = it(np.ones((1, 64), np.float32), "c_onesr64")
        self.c_u16 = it(np.triu(np.ones((H, H), np.float32), 1).copy(), "c_u16")
        lpar = np.zeros((H, 128), np.float32)
        for h in range(H):
            lpar[h, (h % 2) * 64:(h % 2) * 64 + 64] = 1.0
        self.c_lpar = it(lpar, "c_lpar")
        selmask = np.zeros((H, 32), np.float32)
        for h in range(H):
            selmask[h, (h // 2) * 4:(h // 2) * 4 + 4] = 1.0
        self.c_selmask = it(selmask, "c_selmask")
        self.c_i2 = it(np.vstack([np.eye(64, dtype=np.float32)] * 2).copy(), "c_i2")

        self._tr_rot = 0
        self._sbuf()
        self._sems()
        self._emit()

    def _sbuf(self):
        nc = self.nc
        S, NT = self.S, self.NT
        a = nc.alloc_sbuf_tensor

        def at(name, shape, dtp, off):
            return nc.alloc_sbuf_tensor_at(name, shape, dtp, offset=off)

        self.id_f = a("id_f", [128, 128], F32)
        self.id_bf = a("id_bf", [128, 128], BF16)
        self.ones2_bf = a("ones2_bf", [2, 128], BF16)
        self.ones64_bf = a("ones64_bf", [64, 1], BF16)
        self.ones65_bf = a("ones65_bf", [65, 1], BF16)
        self.onesr64_r = a("onesr64_r", [1, 64], F32R)
        self.ones128_f = a("ones128_f", [128, 1], F32)
        self.lpar_f = a("lpar_f", [H, 128], F32)
        self.selmask_f = a("selmask_f", [H, 32], F32)
        self.selp = a("selp", [H, 32], F32)
        self.i2_f = a("i2_f", [128, 64], F32)
        self.u16_f = a("u16_f", [H, H], F32)

        self.x1 = a("x1", [128, NT, D], F32)
        self.Vs = a("Vs", [128, NT, NA * 65], BF16)
        self.kv_z = a("kv_z", [128, H, 65], BF16)
        self.esc = a("esc", [128, 512], F32)
        self.esc2 = a("esc2", [128, 512], F32)
        self.sums = a("sums", [128, 10], F32)
        self.stat = a("stat", [128, 6], F32)
        self.wr_f = a("wr_f", [128, 8, H], F32)
        self.xsum = a("xsum", [1, D], F32)
        self.xsum_fm = a("xsum_fm", [128, 8], F32)
        self.nrm_f = a("nrm_f", [1, D], F32)

        self.bq1_fm = a("bq1_fm", [128, 8], F32)
        self.bq2_fm = a("bq2_fm", [128, 8], F32)
        self.bk2_fm = a("bk2_fm", [128, 8], F32)
        self.bv2_fm = a("bv2_fm", [128, 8], F32)
        self.bk1_hl = a("bk1_hl", [2, D], BF16)
        self.bv1_hl = a("bv1_hl", [2, D], BF16)
        self.bo1_hl = a("bo1_hl", [2, D], BF16)
        self.bo2_hl = a("bo2_hl", [2, D], BF16)
        self.br_row = a("br_row", [1, H], F32)

        self.r_row = a("r_row", [1, H], F32)
        self.z16 = a("z16", [128, H], F32)
        self.epsc = a("epsc", [128, 1], F32)
        self.vmax8 = a("vmax8", [1, 8], F32)
        self.active = a("active", [1, H], F32)
        self.activeT = a("activeT", [H, 1], F32)
        self.sel = a("sel", [H, NA], F32)
        self.kmax_row = a("kmax_row", [1, NA], F32)

        rqk = nc.bump_sbuf(NA * 2 * S * 2)[0]
        self.Qs = [at(f"Qs{i}", [65, S], BF16, rqk + i * (S * 2)) for i in range(NA)]
        self.Ks = [at(f"Ks{i}", [65, S], BF16, rqk + (NA + i) * (S * 2))
                   for i in range(NA)]

        ph = nc.bump_sbuf(78336)[0]
        self.ph = ph
        self.wk1 = at("wk1", [128, 8, D], BF16, ph + 0)
        self.wv1 = at("wv1", [128, 8, D], BF16, ph + 16384)
        self.xin = [at(f"xin{i}", [128, D], F32, ph + 32768 + i * 4096)
                    for i in range(2)]
        self.xin_bf = [at(f"xinb{i}", [128, D], BF16, ph + 32768 + i * 2048)
                       for i in range(4)]
        self.km = [at(f"km{i}", [128, D], BF16,
                      ph + (40960 if i == 0 else 65664)) for i in range(2)]
        self.vaug = [at(f"vaug{i}", [128, H * 65], BF16,
                        ph + (43008 if i == 0 else 67712)) for i in range(2)]
        self.xt1 = [at(f"xt1_{i}", [128, 8, 128], BF16, ph + 45184 + i * 2048)
                    for i in range(2)]
        self.wq1 = at("wq1", [128, 8, D], BF16, ph + 49280)
        self.xTg = at("xTg", [128, 8, 512], BF16, ph + 0)
        self.qm = at("qm", [128, 8, 512], BF16, ph + 8192)
        self.ynTg = at("ynTg", [128, 8, 512], F32R, ph + 16384)
        self.wo1_ring = [at(f"wo1g{i}", [128, 512], F32R,
                            ph + (32768 + i * 2048 if i < 4 else
                                  69824 + (i - 4) * 2048)) for i in range(8)]
        self.ynorm = [at(f"ynorm{i}", [128, D], F32,
                         ph + (40960 if i == 0 else 45056)) for i in range(2)]
        self.wq2 = at("wq2", [128, 8, D], BF16, ph + 0)
        self.wk2 = at("wk2", [128, 8, D], BF16, ph + 16384)
        self.wv2 = at("wv2", [128, 8, D], BF16, ph + 32768)
        self.x1T = at("x1T", [128, 8, 512], BF16, ph + 49152)
        self.qkv_full = at("qkv_full", [128, 8, 512], BF16, ph + 57344)
        self.shiftb = at("shiftb", [128, 512], BF16, ph + 65536)
        self.vsfm = at("vsfm", [128, 512], BF16, ph + 66560)
        self.G = at("G", [128, 8, 256], BF16, ph + 67584)
        self.sqb = [at(f"sqb{i}", [65, S], BF16, ph + i * 4096)
                    for i in range(2)]
        self.nac2 = [at(f"nac2_{i}", [1, S], F32, ph + 8192 + i * 8192)
                     for i in range(2)]
        self.OTn = at("OTn", [64, NA, D], F32R, ph + 0)
        self.Pt = [at(f"Pt{i}", [128, D], BF16, ph + 16384 + i * 2048)
                   for i in range(2)]
        self.rep_sb = at("rep_sb", [64, D], F32, ph + 20480)
        self.rec_row = at("rec_row", [1, D], F32R, ph + 24576)
        self.wo2r = at("wo2r", [64, NA, D], F32R, ph + 28672)
        self.wk1n = at("wk1n", [128, 8, D], BF16, ph + 45056)
        self.wv1n = at("wv1n", [128, 8, D], BF16, ph + 61440)
        self.brow = at("brow", [1, D], F32, ph + 0)
        self.blo_f = at("blo_f", [1, D], F32, ph + 4096)

        p = nc.alloc_psum_tensor
        (self.psA, self.psB, self.psC, self.psD, self.psE, self.psF,
         self.psG, self.psH) = [p(f"ps{i}", [128, 512], F32) for i in range(8)]

    def _sems(self):
        self.sem, self.cnt = {}, {}
        for n in ["dW", "dX", "dX0", "dX1", "dX2", "dX3",
                  "dB0", "dB1", "dB2",
                  "dG0", "dG1", "dG2", "dG3", "dO0", "dO1",
                  "dR0", "dR1", "dR2", "dR3", "dR4", "dR5", "dR6", "dR7",
                  "T", "V", "Sc"]:
            self.sem[n] = self.nc.alloc_semaphore(n)
            self.cnt[n] = 0

    def inc(self, instr, name):
        amt = 16 if name.startswith("d") else 1
        instr.then_inc(self.sem[name], amt)
        self.cnt[name] += amt
        return self.cnt[name]

    def wait(self, eng, name, val=None):
        eng.wait_ge(self.sem[name], self.cnt[name] if val is None else val)

    def _emit(self):
        nc = self.nc
        sa = self.stop_after
        with nc.Block():
            self._startup()
            nc.all_engine_barrier()
            self._wload_b1(first=True)
            self._w1_done = self.cnt["dW"]
            nc.all_engine_barrier()
            for b in range(self.NB):
                if sa == "startup":
                    break
                self._phase1(b)
                nc.all_engine_barrier()
                if sa == "p1":
                    break
                self._phase2(b)
                nc.all_engine_barrier()
                if sa == "p2":
                    break
                self._router(b)
                self._wload_b2()
                nc.all_engine_barrier()
                if sa == "router":
                    break
                self._phase3(b)
                nc.all_engine_barrier()
                if sa == "p3":
                    break
                self._wload_wo2()
                self._bounds(b)
                nc.all_engine_barrier()
                if sa == "bounds":
                    break
                self._phase45(b, last=(b == self.NB - 1))
                nc.all_engine_barrier()
                if b + 1 < self.NB:
                    gp = nc.gpsimd
                    self.inc(gp.dma_start(out=self.wk1[:], in_=self.wk1n[:]), "dW")
                    self.inc(gp.dma_start(out=self.wv1[:], in_=self.wv1n[:]), "dW")
                    gp.wait_ge(self.sem["dW"], self.cnt["dW"])
                    self._wload_wq1()
                    self._w1_done = self.cnt["dW"]
                    nc.all_engine_barrier()
            for s in ["dX", "dX0", "dX1", "dX2", "dX3", "dO0", "dO1",
                      "dR0", "dR1", "dR2",
                      "dR3", "dR4", "dR5", "dR6", "dR7"]:
                self.wait(nc.sync, s)
            for s in ["dG0", "dG1", "dG2", "dG3", "dB0", "dB1", "dB2"]:
                self.wait(nc.sync, s)
            self.wait(nc.gpsimd, "dW")

    def _startup(self):
        nc = self.nc
        sy, gp, V = nc.sync, nc.gpsimd, nc.vector
        w = self.w
        for dst, src in [(self.id_f, self.c_id), (self.ones128_f, self.c_ones128),
                         (self.lpar_f, self.c_lpar),
                         (self.selmask_f, self.c_selmask),
                         (self.i2_f, self.c_i2), (self.u16_f, self.c_u16)]:
            self.inc(sy.dma_start(out=dst[:], in_=src.ap()), "dX")
        for dst, src in [(self.id_bf, self.c_id), (self.ones2_bf, self.c_ones2),
                         (self.ones64_bf, self.c_ones64),
                         (self.ones65_bf, self.c_ones65),
                         (self.onesr64_r, self.c_onesr64)]:
            self.inc(gp.dma_start(out=dst[:], in_=src.ap()), "dW")
        self.inc(gp.dma_start(out=self.wo1r_h.ap(), in_=w["a_wo"].ap()), "dW")
        self.inc(sy.dma_start(
            out=self.wr_f[:],
            in_=w["f_wr"].ap().rearrange("(p c) h -> p c h", p=128)), "dX")
        for src, dst in [("a_bq", self.bq1_fm), ("f_bq", self.bq2_fm),
                         ("f_bk", self.bk2_fm), ("f_bv", self.bv2_fm)]:
            self.inc(sy.dma_start(
                out=dst[:], in_=w[src].ap().rearrange("(c p) -> p c", p=128),
                allow_slow_non_contiguous=True), "dX")
        self.inc(sy.dma_start(out=self.br_row[:], in_=w["f_br"].ap()[None, :]),
                 "dX")
        for src, dst in [("a_bk", self.bk1_hl), ("a_bv", self.bv1_hl),
                         ("a_bo", self.bo1_hl), ("f_bo", self.bo2_hl)]:
            self.inc(sy.dma_start(out=self.brow[:], in_=w[src].ap()[None, :]),
                     "dX")
            self.wait(V, "dX")
            self.wait(V, "dW")
            V.tensor_copy(dst[0:1, :], self.brow[:])
            V.drain()
            iv = self.inc(V.tensor_sub(self.blo_f[:], self.brow[:], dst[0:1, :]),
                          "V")
            self.wait(gp, "V", iv)
            self.inc(gp.dma_start(out=dst[1:2, :], in_=self.blo_f[:]), "dW")
            sy.wait_ge(self.sem["V"], iv)
        self.wait(V, "dW")
        V.memset(_ap(self.Vs, 64, [[self.NT * NA * 65, 128], [65, self.NT * NA]]),
                 1.0)
        V.memset(self.epsc[:], EPS_LN)
        V.memset(self.sums[:], 0.0)
        self.inc(V.memset(self.stat[:], 0.0), "V")

    def _wload_b1(self, first=False):
        gp = self.nc.gpsimd
        for src, dst in [("a_wk", self.wk1 if first else self.wk1n),
                         ("a_wv", self.wv1 if first else self.wv1n)]:
            self.inc(gp.dma_start(
                out=dst[:],
                in_=self.w[src].ap().rearrange("(c p) f -> p c f", p=128)), "dW")
        if first:
            self._wload_wq1()

    def _wload_wq1(self):
        gp = self.nc.gpsimd
        self.inc(gp.dma_start(
            out=self.wq1[:],
            in_=self.w["a_wq"].ap().rearrange("(c p) f -> p c f", p=128)), "dW")

    def _wload_b2(self):
        gp = self.nc.gpsimd
        self._w2_done = {}
        for src, dst, kind, sname in [("f_wq", self.wq2, "q", "dB0"),
                                      ("f_wk", self.wk2, "k", "dB1"),
                                      ("f_wv", self.wv2, "v", "dB2")]:
            self.wait(gp, sname)
            cnt = self.inc(gp.dma_start(
                out=dst[:],
                in_=self.w[src].ap().rearrange("(c p) f -> p c f", p=128)), sname)
            self._w2_done[kind] = (sname, cnt)

    def _wload_wo2(self):
        gp = self.nc.gpsimd
        self.inc(gp.dma_start(
            out=self.wo2r[:],
            in_=self.w["f_wo"].ap().rearrange("(a d) f -> d a f", d=HD)), "dW")

    def _tr8(self, src_f32, evac_list, banks=None):
        nc = self.nc
        TE = nc.tensor
        if banks is None:
            banks = [self.psG, self.psH]
        if not hasattr(self, "_tr_evac"):
            self._tr_evac = {}
        for g in range(2):
            bi = (self._tr_rot + g) % len(banks)
            ps = banks[bi]
            esem, ecnt = self._tr_evac.get(id(ps), ("Sc", 0))
            self.wait(TE, esem, ecnt)
            if not hasattr(self, "_ps_vfree"):
                self._ps_vfree = {}
            self.wait(TE, "V", self._ps_vfree.get(id(ps), 0))
            last = None
            for c4 in range(4):
                c = 4 * g + c4
                last = TE.transpose(ps[:, c4 * 128:(c4 + 1) * 128],
                                    src_f32[:, c * 128:(c + 1) * 128],
                                    self.id_f[:])
            it = self.inc(last, "T")
            self.nc.scalar.wait_ge(self.sem["T"], it)
            self._tr_evac[id(ps)] = evac_list(g, ps)
        self._tr_rot = (self._tr_rot + 2) % len(banks)

    def _phase1(self, b):
        nc = self.nc
        sy = nc.sync
        TE, SC, V = nc.tensor, nc.scalar, nc.vector
        NT = self.NT
        self.wait(TE, "dW", self._w1_done)
        V.memset(_ap(self.vaug[0], 64, [[H * 65, 128], [65, H]]), 1.0)
        V.memset(_ap(self.vaug[1], 64, [[H * 65, 128], [65, H]]), 1.0)
        self.inc(V.memset(self.sums[:, 7:8], 0.0), "V")
        v_done = self.cnt["V"]
        self._p1_vdone = {}
        gp = nc.gpsimd
        for t in range(NT):
            if t % 2 == 0:
                eng, sx = sy, "dX%d" % ((t // 2) % 4)
            else:
                eng, sx = gp, "dG%d" % ((t // 2) % 4)
            self.wait(eng, sx)
            self.wait(eng, "dO%d" % (t % 2))
            self.inc(eng.dma_start(
                out=self.x1[:, t, :],
                in_=self.x_h.ap()[b, t * 128:(t + 1) * 128, :]), sx)
            self.wait(TE, sx)
            self.wait(TE, "V", v_done)
            xt = self.xt1[t % 2]

            def evac(gg, ps, xt=xt):
                out = _ap(xt, 4 * gg * 128, [[1024, 128], [1, 512]])
                if gg % 2 == 0:
                    return ("Sc", self.inc(SC.activation(out, ps[:], AF.Copy),
                                           "Sc"))
                self.wait(V, "T", self.cnt["T"])
                return ("V", self.inc(V.tensor_copy(out, ps[:]), "V"))

            self._tr8(self.x1[:, t, :], evac)
            self.wait(TE, "Sc")
            self.wait(TE, "V")
            tmark = {}
            for ti, (wsb, bhl) in enumerate([(self.wk1, self.bk1_hl),
                                             (self.wv1, self.bv1_hl)]):
                for half in range(2):
                    ps = [self.psA, self.psB, self.psC, self.psD][2 * ti + half]
                    for c in range(8):
                        TE.matmul(ps[:], xt[:, c, :],
                                  wsb[:, c, half * 512:(half + 1) * 512],
                                  start=(c == 0), stop=False)
                    mm = TE.matmul(ps[:], self.ones2_bf[:],
                                   bhl[:, half * 512:(half + 1) * 512],
                                   start=False, stop=True)
                    tmark[(ti, half)] = self.inc(mm, "T")
            for half in range(2):
                ps = [self.psA, self.psB][half]
                self.wait(V, "T", tmark[(0, half)])
                V.tensor_scalar_min(self.esc[:], ps[:], 0.0)
                iv = self.inc(V.memset(self.sums[:, 7:8], 0.0), "V")
                self.wait(SC, "V", iv)
                isc = self.inc(SC.activation(self.esc2[:], self.esc[:], AF.Exp),
                               "Sc")
                self.wait(V, "Sc", isc)
                st = V.scalar_tensor_tensor(
                    self.km[t % 2][:, half * 512:(half + 1) * 512],
                    ps[:], 0.0, self.esc2[:], ALU.max, ALU.add)
                self.inc(st, "V")
            for half in range(2):
                ps = [self.psC, self.psD][half]
                self.wait(V, "T", tmark[(1, half)])
                out = _ap(self.vaug[t % 2], half * 8 * 65,
                          [[H * 65, 128], [65, 8], [1, 64]])
                self.inc(V.tensor_copy(out, ps[:]), "V")
            v_done = self.cnt["V"]
            self._p1_vdone[t] = v_done
            if t == 0 and "d_km" in self.dbg:
                gp = nc.gpsimd
                self.wait(gp, "V", v_done)
                self.inc(gp.dma_start(out=self.dbg["d_km"].ap()[b],
                                      in_=self.km[t % 2][:]), "dW")
                idm = self.inc(gp.dma_start(out=self.dbg["d_vaug"].ap()[b],
                                            in_=self.vaug[t % 2][:]), "dW")
                gp.wait_ge(self.sem["dW"], idm)
                self.wait(V, "dW", idm)
            if t > 0:
                self._p1_kv_accum(t - 1, NT)
        self._p1_kv_accum(NT - 1, NT)
        self.wait(V, "T")
        V.memset(self.kv_z[:], 0.0)
        V.drain()
        for bi, bank in enumerate([self.psE, self.psF]):
            oute = _ap(self.kv_z, (2 * 4 * bi) * 65,
                       [[H * 65, 64], [2 * 65, 4], [1, 65]])
            srce = _ap(bank, 0, [[512, 64], [128, 4], [1, 65]])
            V.tensor_copy(oute, srce)
            V.drain()
            oto = _ap(self.kv_z, (2 * 4 * bi + 1) * 65 + 64 * H * 65,
                      [[H * 65, 64], [2 * 65, 4], [1, 65]])
            srco = _ap(bank, 64 * 512, [[512, 64], [128, 4], [1, 65]])
            self.inc(V.tensor_copy(oto, srco), "V")
            V.drain()
        if "d_kv" in self.dbg:
            gp = nc.gpsimd
            self.wait(gp, "V")
            idm = self.inc(gp.dma_start(
                out=self.dbg["d_kv"].ap()[b],
                in_=self.kv_z[:].rearrange("p c k -> p (c k)")), "dW")
            gp.wait_ge(self.sem["dW"], idm)

    def _p1_kv_accum(self, t, NT):
        TE = self.nc.tensor
        self.wait(TE, "V", self._p1_vdone[t])
        last = None
        for h in range(H):
            bank = self.psE if h < 8 else self.psF
            slot = (h % 8) // 2
            bp = (h % 2) * 64
            last = TE.matmul(bank[bp:bp + 64, slot * 128:slot * 128 + 65],
                             self.km[t % 2][:, h * 64:(h + 1) * 64],
                             _ap(self.vaug[t % 2], h * 65,
                                 [[H * 65, 128], [1, 65]]),
                             start=(t == 0 and h % 8 < 2),
                             stop=(t == NT - 1), skip_group_check=True)
        self.inc(last, "T")

    def _phase2(self, b):
        nc = self.nc
        sy = nc.sync
        TE, SC, V = nc.tensor, nc.scalar, nc.vector
        NG = self.NG
        self.wait(TE, "dW", self._w1_done)
        self.qm_evac = {}
        self._ring_used = {}
        self._xsum_pending = None
        self._ynorm_v = {}
        self._ynT_done = {}
        V.memset(self.xsum[:], 0.0)
        self.inc(V.memset(self.sums[:], 0.0), "V")
        ring_cnt = 0
        for g in range(NG):
            for dt_ in range(4):
                t = g * 4 + dt_

                def evac(gg, ps, dt_=dt_):
                    out = _ap(self.xTg, (4 * gg) * 512 + dt_ * 128,
                              [[8 * 512, 128], [512, 4], [1, 128]])
                    if gg % 2 == 0:
                        return ("Sc", self.inc(
                            SC.activation(out, ps[:], AF.Copy), "Sc"))
                    self.wait(V, "T", self.cnt["T"])
                    return ("V", self.inc(V.tensor_copy(out, ps[:]), "V"))

                self._tr8(self.x1[:, t, :], evac)
            self.wait(TE, "Sc")
            self.wait(TE, "V")
            self._emit_xsum(b)
            if self.p2_level < 2:
                continue
            for hc in range(8):
                ps = [self.psA, self.psB, self.psC, self.psD][hc % 4]
                if hc >= 4:
                    self.wait(TE, "V", self.qm_evac[hc - 4])
                mm = None
                for c in range(8):
                    mm = TE.matmul(ps[:], self.wq1[:, c, hc * 128:(hc + 1) * 128],
                                   self.xTg[:, c, :], start=(c == 0), stop=(c == 7))
                it = self.inc(mm, "T")
                self.wait(SC, "T", it)
                self.wait(SC, "V", self.qm_evac.get(hc - 1, 0))
                isc = self.inc(SC.activation(self.esc2[:], ps[:], AF.Relu,
                                             bias=self.bq1_fm[:, hc:hc + 1]), "Sc")
                self.wait(V, "Sc", isc)
                V.tensor_scalar(self.esc[:], ps[:], self.bq1_fm[:, hc:hc + 1],
                                None, ALU.add)
                V.drain()
                V.tensor_sub(self.esc[:], self.esc[:], self.esc2[:])
                iv = self.inc(V.memset(self.sums[:, 7:8], 0.0), "V")
                self.wait(SC, "V", iv)
                isc = self.inc(SC.activation(self.esc[:], self.esc[:], AF.Exp),
                               "Sc")
                self.wait(V, "Sc", isc)
                self.qm_evac[hc] = self.inc(
                    V.tensor_add(self.qm[:, hc, :], self.esc[:], self.esc2[:]),
                    "V")
            self.wait(TE, "V")
            if self.p2_level < 3:
                continue
            for dt_ in range(4):
                self._p2_y_tile(dt_)
                if dt_ > 0:
                    self._p2_ynt(dt_ - 1)
            self._p2_ynt(3)
            if self.p2_level < 4:
                continue
            if g == 0 and "d_qm" in self.dbg:
                gpq = nc.gpsimd
                self.wait(gpq, "V")
                self.wait(gpq, "Sc")
                self.inc(gpq.dma_start(out=self.dbg["d_qm"].ap()[b],
                                       in_=self.qm[:]), "dW")
                idm = self.inc(gpq.dma_start(out=self.dbg["d_ynt"].ap()[b],
                                             in_=self.ynTg[:]), "dW")
                gpq.wait_ge(self.sem["dW"], idm)
                self.wait(TE, "dW", idm)
            self.wait(TE, "Sc")
            for half in range(2):
                for c in range(8):
                    slot = ring_cnt % 8
                    sr = "dR%d" % slot
                    self.wait(sy, sr)
                    self.wait(sy, "T", self._ring_used.get(slot, 0))
                    self.inc(sy.dma_start(
                        out=self.wo1_ring[slot][:],
                        in_=self.wo1r_h.ap()[c * 128:(c + 1) * 128,
                                             half * 512:(half + 1) * 512]), sr)
                    ring_cnt += 1
                    self.wait(TE, sr)
                    if c == 0:
                        self.wait(TE, "V")
                    for dt_ in range(4):
                        psO = [self.psA, self.psB, self.psC, self.psD][dt_]
                        mm = TE.matmul(
                            psO[:],
                            _ap(self.ynTg, (dt_ % 2) * 64 + c * 512,
                                [[8 * 512, 128], [1, 64]]).bitcast(F32R)
                            if False else
                            self.ynTg[:, c, dt_ * 128:(dt_ + 1) * 128],
                            self.wo1_ring[slot][:],
                            start=(c == 0), stop=False)
                        if c == 7:
                            mm = TE.matmul(psO[:], self.ones2_bf[:],
                                           self.bo1_hl[:, half * 512:(half + 1) * 512],
                                           start=False, stop=True)
                    if c == 7:
                        self.inc(mm, "T")
                    else:
                        self.inc(mm, "T")
                    self._ring_used[slot] = self.cnt["T"]
                if self.p2_level < 5:
                    continue
                self.wait(V, "T")
                for dt_ in range(4):
                    t = g * 4 + dt_
                    psO = [self.psA, self.psB, self.psC, self.psD][dt_]
                    st = V.scalar_tensor_tensor(
                        self.x1[:, t, half * 512:(half + 1) * 512], psO[:], 0.0,
                        self.x1[:, t, half * 512:(half + 1) * 512],
                        ALU.add, ALU.add,
                        accum_out=self.sums[:, 2 * dt_ + half:2 * dt_ + half + 1])
                self.inc(V.memset(self.sums[:, 8:9], 0.0), "V")
                V.drain()
                if not hasattr(self, "_ps_vfree"):
                    self._ps_vfree = {}
                for pso in [self.psA, self.psB, self.psC, self.psD]:
                    self._ps_vfree[id(pso)] = self.cnt["V"]
            if self.p2_level < 6:
                continue
            for dt_ in range(4):
                t = g * 4 + dt_
                self._ln_finalize(self.x1[:, t, :],
                                  self.sums[:, 2 * dt_:2 * dt_ + 1],
                                  self.sums[:, 2 * dt_ + 1:2 * dt_ + 2],
                                  self.x1[:, t, :])
            self._ln_done = self.cnt["V"]
            self._xsum_pending = g
            if "d_xg" in self.dbg:
                gpx = nc.gpsimd
                self.wait(gpx, "V")
                idm = self.inc(gpx.dma_start(
                    out=self.dbg["d_xg"].ap()[b, g][None, :], in_=self.xsum[:]),
                    "dW")
                gpx.wait_ge(self.sem["dW"], idm)
                self.wait(V, "dW", idm)
        self._emit_xsum(b)
        if "d_x1" in self.dbg:
            self.wait(sy, "V")
            for t in range(self.NT):
                self.inc(sy.dma_start(
                    out=self.dbg["d_x1"].ap()[b, t * 128:(t + 1) * 128, :],
                    in_=self.x1[:, t, :]), "dX")
            self.wait(sy, "dX")

    def _emit_xsum(self, b):
        if getattr(self, "_xsum_pending", None) is None:
            return
        g = self._xsum_pending
        self._xsum_pending = None
        nc = self.nc
        TE, V = nc.tensor, nc.vector
        self.wait(TE, "V", self._ln_done)
        self.wait(TE, "V", getattr(self, "_xsum_vfree", 0))
        last = None
        for dt_ in range(4):
            t = g * 4 + dt_
            for half in range(2):
                ps = [self.psE, self.psF][half]
                last = TE.matmul(ps[0:1, :], self.ones128_f[:],
                                 self.x1[:, t, half * 512:(half + 1) * 512],
                                 start=(dt_ == 0), stop=(dt_ == 3),
                                 skip_group_check=True)
        it = self.inc(last, "T")
        self.wait(V, "T", it)
        for half in range(2):
            ps = [self.psE, self.psF][half]
            V.tensor_add(self.xsum[0:1, half * 512:(half + 1) * 512],
                         self.xsum[0:1, half * 512:(half + 1) * 512],
                         ps[0:1, :])
        V.drain()
        self._xsum_vfree = self.inc(V.memset(self.sums[:, 8:9], 0.0), "V")

    def _p2_y_tile(self, dt_):
        nc = self.nc
        TE, SC, V = nc.tensor, nc.scalar, nc.vector
        lvl = getattr(self, "p2y_level", 9)
        if not hasattr(self, "_y_pair_free"):
            self._y_pair_free = {}
        pairs = [[self.psC, self.psD], [self.psA, self.psB]]
        marks = {}
        for hg in range(2):
            bpair = pairs[hg]
            self.wait(TE, "V", self._y_pair_free.get(hg, 0))
            last = None
            for hh in range(8):
                h = hg * 8 + hh
                bank = bpair[0] if hh < 4 else bpair[1]
                slot = hh % 4
                qm_lhs = self.qm[:, h // 2, dt_ * 128:(dt_ + 1) * 128]
                kv_rhs = self.kv_z[:, h, 0:65]
                last = TE.matmul(bank[:, slot * 128:slot * 128 + 65], qm_lhs,
                                 kv_rhs, start=True, stop=True)
            marks[hg] = self.inc(last, "T")
        banks4 = [(hg * 8 + bi * 4, pairs[hg][bi])
                  for hg in range(2) for bi in range(2)]
        self.wait(V, "T", marks[1])
        self.wait(V, "T", self._ynT_done.get(dt_ % 2, 0))
        if lvl >= 2:
            V.drain()
            for i, (h0, bank) in enumerate(banks4):
                V.tensor_scalar(self.z16[:, i * 4:(i + 1) * 4],
                                _ap(bank, 64, [[512, 128], [128, 4]]),
                                EPS_ATTN, None, ALU.add)
            V.drain()
            V.reciprocal(self.z16[:], self.z16[:])
            V.drain()
        iv = None
        for i, (h0, bank) in enumerate(banks4):
            dst = _ap(self.ynorm[dt_ % 2], h0 * 64,
                      [[1024, 128], [64, 4], [1, 64]])
            src = _ap(bank, 0, [[512, 128], [128, 4], [1, 64]])
            if lvl < 2:
                iv = self.inc(V.tensor_copy(dst, src), "V")
            else:
                zb = _ap(self.z16, i * 4, [[H, 128], [1, 4], [0, 64]])
                iv = self.inc(V.tensor_tensor(dst, src, zb, ALU.mult), "V")
            if i == 1:
                self._y_pair_free[0] = iv
        self._y_pair_free[1] = iv
        self._ynorm_v[dt_] = self.cnt["V"]

    def _p2_ynt(self, dt_):
        nc = self.nc
        TE, SC = nc.tensor, nc.scalar

        def evac(gg, ps, dt_=dt_):
            out = _ap(self.ynTg, (4 * gg) * 512 + dt_ * 128,
                      [[8 * 512, 128], [512, 4], [1, 128]])
            with nc.allow_low_precision(reason="f32r operand prep"):
                return ("Sc", self.inc(SC.activation(out, ps[:], AF.Copy),
                                       "Sc"))

        self.wait(TE, "V", self._ynorm_v[dt_])
        self._tr8(self.ynorm[dt_ % 2][:], evac)
        self._ynT_done[dt_ % 2] = self.cnt["T"]

    def _ln_finalize(self, xpre, s0, s1, x_out):
        nc = self.nc
        SC, V = nc.scalar, nc.vector
        for half in range(2):
            V.scalar_tensor_tensor(
                [self.esc, self.esc2][half][:],
                xpre[:, half * 512:(half + 1) * 512], 1.0,
                xpre[:, half * 512:(half + 1) * 512], ALU.mult, ALU.mult,
                accum_out=self.sums[:, 8 + half:9 + half])
        V.drain()
        V.tensor_add(self.stat[:, 0:1], s0, s1)
        V.tensor_add(self.stat[:, 1:2], self.sums[:, 8:9], self.sums[:, 9:10])
        V.drain()
        V.scalar_tensor_tensor(self.stat[:, 2:3], self.stat[:, 0:1], 1.0 / D,
                               self.stat[:, 0:1], ALU.mult, ALU.mult)
        V.drain()
        iv = self.inc(V.tensor_sub(self.stat[:, 1:2], self.stat[:, 1:2],
                                   self.stat[:, 2:3]), "V")
        self.wait(SC, "V", iv)
        SC.activation(self.stat[:, 4:5], self.stat[:, 1:2], AF.Ln,
                      scale=1.0 / D, bias=self.epsc[:, 0:1])
        SC.drain()
        isc = self.inc(SC.activation(self.stat[:, 2:3], self.stat[:, 4:5],
                                     AF.Exp, scale=-0.5), "Sc")
        self.wait(V, "Sc", isc)
        V.scalar_tensor_tensor(self.stat[:, 3:4], self.stat[:, 0:1], -1.0 / D,
                               self.stat[:, 2:3], ALU.mult, ALU.mult)
        V.drain()
        self.inc(V.tensor_scalar(x_out, xpre, self.stat[:, 2:3],
                                 self.stat[:, 3:4], ALU.mult, ALU.add), "V")

    def _router(self, b):
        nc = self.nc
        sy, gp = nc.sync, nc.gpsimd
        TE, SC, V = nc.tensor, nc.scalar, nc.vector
        self.wait(gp, "V")
        if "d_xsum" in self.dbg:
            idm = self.inc(gp.dma_start(out=self.dbg["d_xsum"].ap()[b][None, :],
                                        in_=self.xsum[:]), "dW")
            gp.wait_ge(self.sem["dW"], idm)
        self.inc(gp.dma_start(
            out=self.xsum_fm[:],
            in_=_ap(self.xsum, 0, [[1024, 1], [8, 128], [1, 8]]),
            allow_slow_non_contiguous=True), "dW")
        gp.wait_ge(self.sem["dW"], self.cnt["dW"])
        self.wait(TE, "dW")
        last = None
        for c in range(8):
            last = TE.matmul(self.psA[0:1, 0:H], self.xsum_fm[:, c:c + 1],
                             self.wr_f[:, c, :], start=(c == 0), stop=(c == 7))
        it = self.inc(last, "T")
        self.wait(V, "T", it)
        V.scalar_tensor_tensor(self.r_row[:], self.br_row[:], float(self.S),
                               self.psA[0:1, 0:H], ALU.mult, ALU.add)
        V.drain()
        V.max(out=self.vmax8[:], in_=self.r_row[:])
        V.drain()
        iv = self.inc(V.tensor_scalar(self.active[:], self.r_row[:],
                                      self.vmax8[:, NA - 1:NA], None, ALU.is_ge),
                      "V")
        V.memset(self.xsum[:], 0.0)
        self.wait(gp, "V", iv)
        self.inc(gp.dma_start(out=self.activeT[:],
                              in_=_ap(self.active, 0, [[16, 1], [1, 16]]),
                              allow_slow_non_contiguous=True), "dW")
        gp.wait_ge(self.sem["dW"], self.cnt["dW"])
        self.wait(TE, "dW")
        mm = TE.matmul(self.psC[0:H, 0:1], self.u16_f[:], self.activeT[:],
                       start=True, stop=True)
        it = self.inc(mm, "T")
        self.wait(V, "T", it)
        last = None
        for a in range(NA):
            last = V.scalar_tensor_tensor(self.sel[:, a:a + 1], self.psC[0:H, 0:1],
                                          float(a), self.activeT[:],
                                          ALU.is_equal, ALU.mult)
        iv = self.inc(last, "V")
        V.drain()
        s_b = _ap(self.sel, 0, [[NA, H], [0, 8], [1, NA]])
        m_b = _ap(self.selmask_f, 0, [[32, H], [NA, 8], [1, NA]])
        o_b = _ap(self.selp, 0, [[32, H], [NA, 8], [1, NA]])
        iv = self.inc(V.tensor_tensor(o_b, s_b, m_b, ALU.mult), "V")
        self.wait(TE, "V", iv)
        mm = TE.matmul(self.psD[:, 0:32], self.lpar_f[:], self.selp[:],
                       start=True, stop=True)
        it = self.inc(mm, "T")
        self.wait(V, "T", it)
        last = None
        for c in range(8):
            src = _ap(self.psD, c * 4, [[512, 128], [1, NA], [0, 64]])
            i2b = _ap(self.i2_f, 0, [[64, 128], [0, NA], [1, 64]])
            dst = _ap(self.G, c * 256, [[8 * 256, 128], [64, NA], [1, 64]])
            last = V.tensor_tensor(dst, src, i2b, ALU.mult)
        self.inc(last, "V")
        self.wait(TE, "V")
        if "d_r" in self.dbg:
            self.wait(sy, "V")
            self.inc(sy.dma_start(out=self.dbg["d_r"].ap()[b][None, :],
                                  in_=self.r_row[:]), "dX")
        if "d_sel" in self.dbg:
            self.wait(sy, "V")
            self.inc(sy.dma_start(out=self.dbg["d_sel"].ap()[b], in_=self.sel[:]),
                     "dX")

    def _phase3(self, b):
        nc = self.nc
        sy, gp = nc.sync, nc.gpsimd
        TE, SC, V = nc.tensor, nc.scalar, nc.vector
        for g in range(self.NG):
            for dt_ in range(4):
                t = g * 4 + dt_

                def evac(gg, ps, dt_=dt_):
                    out = _ap(self.x1T, (4 * gg) * 512 + dt_ * 128,
                              [[8 * 512, 128], [512, 4], [1, 128]])
                    if gg % 2 == 0:
                        return ("Sc", self.inc(
                            SC.activation(out, ps[:], AF.Copy), "Sc"))
                    self.wait(V, "T", self.cnt["T"])
                    return ("V", self.inc(V.tensor_copy(out, ps[:]), "V"))

                self._tr8(self.x1[:, t, :], evac,
                          banks=[self.psA, self.psB, self.psG, self.psH])
            self.wait(TE, "Sc")
            self.wait(TE, "V")
            for ti, (wsb, bias, kind) in enumerate([
                    (self.wq2, self.bq2_fm, "q"),
                    (self.wk2, self.bk2_fm, "k"),
                    (self.wv2, self.bv2_fm, "v")]):
                self.wait(TE, "V")
                sname, scnt = self._w2_done[kind]
                self.wait(TE, sname, scnt)
                evacd = {}
                for hc in range(9):
                    if hc < 8:
                        ps = [self.psA, self.psB][hc % 2]
                        if hc >= 2:
                            self.wait(TE, "V", evacd[hc - 2])
                        mm = None
                        for c in range(8):
                            mm = TE.matmul(ps[:],
                                           wsb[:, c, hc * 128:(hc + 1) * 128],
                                           self.x1T[:, c, :],
                                           start=(c == 0), stop=(c == 7))
                        it = self.inc(mm, "T")
                        self.wait(V, "T", it)
                        evacd[hc] = self.inc(
                            V.tensor_scalar(self.qkv_full[:, hc, :], ps[:],
                                            bias[:, hc:hc + 1], None, ALU.add),
                            "V")
                    cg = hc - 1
                    if cg >= 0:
                        self.wait(TE, "V", evacd[cg])
                        if cg == 0:
                            self.wait(TE, "Sc", getattr(self, "_gps_free", 0))
                        for j in range(2):
                            psj = [self.psC, self.psD][j]
                            mm = TE.matmul(psj[:],
                                           self.G[:, cg, j * 128:(j + 1) * 128],
                                           self.qkv_full[:, cg, :],
                                           start=(cg == 0), stop=(cg == 7),
                                           skip_group_check=True)
                            if cg == 7:
                                self.inc(mm, "T")
                for j in range(2):
                    ps = [self.psC, self.psD][j]
                    it = self.cnt["T"]
                    if kind == "v":
                        self.wait(SC, "T", it)
                        isc = self.inc(SC.activation(self.vsfm[:], ps[:],
                                                     AF.Copy), "Sc")
                        self.wait(TE, "Sc", isc)
                        tr = None
                        for dt_ in range(4):
                            out_bf = self.psE[:, dt_ * 64:(dt_ + 1) * 64].bitcast(
                                BF16)
                            tr = TE.transpose(out_bf,
                                              self.vsfm[:, dt_ * 128:(dt_ + 1) * 128],
                                              self.id_bf[:])
                        it2 = self.inc(tr, "T")
                        self.wait(SC, "T", it2)
                        isc = None
                        for dt_ in range(4):
                            t = g * 4 + dt_
                            dst = _ap(self.Vs, t * NA * 65 + (2 * j) * 65,
                                      [[self.NT * NA * 65, 128], [65, 2], [1, 64]])
                            srcb = self.psE[:, dt_ * 64:(dt_ + 1) * 64].bitcast(
                                BF16)
                            src2 = bass.AP(tensor=srcb.tensor, offset=srcb.offset,
                                           ap=[list(srcb.ap[0]), [64, 2], [1, 64]])
                            isc = self.inc(SC.activation(dst, src2, AF.Copy),
                                           "Sc")
                        self.wait(TE, "Sc", isc)
                        self._gps_free = self.cnt["Sc"]
                    else:
                        dsts = self.Qs if kind == "q" else self.Ks
                        scale = 0.125 if kind == "q" else 1.0
                        self.wait(SC, "T", it)
                        self.inc(SC.activation(
                            dsts[2 * j][0:64, g * 512:(g + 1) * 512],
                            ps[0:64, :], AF.Copy, scale=scale), "Sc")
                        self.wait(SC, "dW", getattr(self, "_shift_idm", 0))
                        isc = self.inc(SC.activation(
                            self.shiftb[64:128, :], ps[64:128, :], AF.Copy,
                            scale=scale), "Sc")
                        self.wait(gp, "Sc", isc)
                        idm = self.inc(gp.dma_start(
                            out=dsts[2 * j + 1][0:64, g * 512:(g + 1) * 512],
                            in_=self.shiftb[64:128, :]), "dW")
                        gp.wait_ge(self.sem["dW"], idm)
                        self._shift_idm = idm
                        self._gps_free = self.cnt["Sc"]

    def _bounds(self, b):
        nc = self.nc
        sy = nc.sync
        TE, SC, V = nc.tensor, nc.scalar, nc.vector
        S = self.S
        nch = S // 512
        V.memset(self.sqb[0][64:65, :], 1.0)
        V.memset(self.sqb[1][64:65, :], 1.0)
        units = []
        for a in range(NA):
            units.append((a, "k", self.Ks[a]))
            units.append((a, "q", self.Qs[a]))
        mm_done, ln_done, exp_done, fin_done = {}, {}, {}, {}
        for u, (a, which, ten) in enumerate(units):
            sq = self.sqb[u % 2]
            nac = self.nac2[u % 2]
            pb = [self.psA, self.psB] if u % 2 == 0 else [self.psC, self.psD]
            self.wait(V, "T", mm_done.get(u - 2, 0))
            iv = self.inc(V.tensor_tensor(sq[0:64, :], ten[0:64, :],
                                          ten[0:64, :], ALU.mult), "V")
            self.wait(TE, "V", iv)
            self.wait(TE, "Sc", ln_done.get(u - 2, 0))
            mm = None
            for ch in range(nch):
                bp = (ch // 2) * 32
                mm = TE.matmul(pb[ch % 2][bp:bp + 1, 0:512], self.ones65_bf[:],
                               sq[:, ch * 512:(ch + 1) * 512],
                               start=True, stop=True)
            mm_done[u] = self.inc(mm, "T")
            self.wait(SC, "T", mm_done[u])
            self.wait(SC, "V", fin_done.get(u - 2, 0))
            for ch in range(nch):
                bp = (ch // 2) * 32
                SC.activation(nac[0:1, ch * 512:(ch + 1) * 512],
                              pb[ch % 2][bp:bp + 1, 0:512], AF.Ln)
            SC.drain()
            exp_done[u] = self.inc(SC.activation(nac[:], nac[:], AF.Exp,
                                                 scale=0.5), "Sc")
            ln_done[u] = exp_done[u]
            self.wait(V, "Sc", exp_done[u])
            if which == "k":
                iv = self.inc(V.reduce_max(self.kmax_row[0:1, a:a + 1],
                                           nac[:], AX.X), "V")
                V.drain()
            else:
                iv = self.inc(V.tensor_scalar(
                    ten[64:65, :], nac[:],
                    self.kmax_row[0:1, a:a + 1], -1.0,
                    ALU.mult, ALU.mult), "V")
                V.drain()
                V.memset(self.Ks[a][64:65, :], 1.0)
                self.inc(V.memset(self.vmax8[0:1, 1:2], 0.0), "V")
            fin_done[u] = self.cnt["V"]
        self.wait(TE, "V")
        gp = self.nc.gpsimd
        if "d_qs" in self.dbg:
            self.wait(gp, "V")
            idm = None
            for a in range(NA):
                self.inc(gp.dma_start(out=self.dbg["d_qs"].ap()[b, a],
                                      in_=self.Qs[a][:]), "dW")
                idm = self.inc(gp.dma_start(out=self.dbg["d_ks"].ap()[b, a],
                                            in_=self.Ks[a][:]), "dW")
            gp.wait_ge(self.sem["dW"], idm)
        if "d_vs" in self.dbg:
            self.wait(gp, "V")
            idm = None
            for t in range(self.NT):
                idm = self.inc(gp.dma_start(
                    out=self.dbg["d_vs"].ap()[b, t * 128:(t + 1) * 128, :],
                    in_=self.Vs[:, t, :]), "dW")
            gp.wait_ge(self.sem["dW"], idm)

    def _phase45(self, b, last=False):
        nc = self.nc
        sy = nc.sync
        TE, SC, V = nc.tensor, nc.scalar, nc.vector
        S, NT = self.S, self.NT
        NQC = S // 1024
        dO_tile = {}
        self.wait(TE, "dW")
        for qc in range(min(NQC, self.p45_qc)):
            otn_done = {}
            for a in range(NA):
                exp_done = {}
                pv_done = {}
                psO_pair = [[self.psE, self.psF], [self.psG, self.psH]][a % 2]
                if a < 2:
                    self.wait(TE, "V")
                sbank = [self.psA, self.psB, self.psC, self.psD]
                for kt in range(NT + 1):
                    if kt < NT:
                        for hf in range(2):
                            ei = 2 * kt + hf
                            psS = sbank[ei % 4]
                            self.wait(TE, "Sc", exp_done.get(
                                ei - 4, getattr(self, "_rep_sc", 0)))
                            mm = TE.matmul(
                                psS[:],
                                self.Ks[a][:, kt * 128:(kt + 1) * 128],
                                self.Qs[a][:, qc * 1024 + hf * 512:
                                           qc * 1024 + (hf + 1) * 512],
                                start=True, stop=True)
                            it = self.inc(mm, "T")
                            self.wait(SC, "T", it)
                            self.wait(SC, "T", pv_done.get(kt - 2, 0))
                            isc = self.inc(SC.activation(
                                self.Pt[kt % 2][:, hf * 512:(hf + 1) * 512],
                                psS[:], AF.Exp), "Sc")
                            exp_done[ei] = isc
                    kp = kt - 1
                    if kp >= 0:
                        self.wait(TE, "Sc", exp_done[2 * kp + 1])
                        if kp == 0 and a >= 2:
                            self.wait(TE, "V", otn_done.get(a - 2, 0))
                        mm2 = None
                        for hf in range(2):
                            psO = psO_pair[hf]
                            mm2 = TE.matmul(
                                psO[0:65, :],
                                self.Vs[:, kp, a * 65:(a + 1) * 65],
                                self.Pt[kp % 2][:, hf * 512:(hf + 1) * 512],
                                start=(kp == 0), stop=(kp == NT - 1),
                                skip_group_check=True)
                        pv_done[kp] = self.inc(mm2, "T")
                self.wait(SC, "T")
                with nc.allow_low_precision(reason="f32r operand prep"):
                    for hf in range(2):
                        psO = psO_pair[hf]
                        SC.activation(self.nrm_f[0:1, hf * 512:(hf + 1) * 512],
                                      psO[64:65, :], AF.Ln)
                    SC.drain()
                    isc = self.inc(SC.activation(self.rec_row[:],
                                                 self.nrm_f[:], AF.Exp,
                                                 scale=-1.0), "Sc")
                self.wait(TE, "Sc", isc)
                psR_pair = [self.psC, self.psD]
                last_mm = None
                for hf in range(2):
                    last_mm = TE.matmul(psR_pair[hf][0:64, :], self.onesr64_r[:],
                                        self.rec_row[0:1, hf * 512:(hf + 1) * 512],
                                        start=True, stop=True)
                it = self.inc(last_mm, "T")
                self.wait(SC, "T", it)
                self.wait(SC, "V", otn_done.get(a - 1, 0))
                isc = None
                for hf in range(2):
                    isc = self.inc(SC.activation(
                        self.rep_sb[:, hf * 512:(hf + 1) * 512],
                        psR_pair[hf][0:64, :], AF.Copy), "Sc")
                self._rep_sc = isc
                self.wait(V, "Sc", isc)
                self.wait(V, "T", pv_done[NT - 1])
                last_v = None
                for hf in range(2):
                    psO = psO_pair[hf]
                    with nc.allow_low_precision(reason="f32r operand prep"):
                        last_v = V.tensor_tensor(
                            self.OTn[0:64, a, hf * 512:(hf + 1) * 512],
                            psO[0:64, :],
                            self.rep_sb[:, hf * 512:(hf + 1) * 512], ALU.mult)
                otn_done[a] = self.inc(last_v, "V")
                if "d_ot" in self.dbg:
                    gp = nc.gpsimd
                    self.wait(gp, "V")
                    idm = self.inc(gp.dma_start(
                        out=self.dbg["d_ot"].ap()[b, a, :,
                                                  qc * 1024:(qc + 1) * 1024],
                        in_=self.OTn[0:64, a, :]), "dW")
                    gp.wait_ge(self.sem["dW"], idm)
                    self.wait(V, "dW", idm)
            self.wait(TE, "V")

            if self.p45_level < 2:
                continue
            res_done = {}
            v_at_start = self.cnt["V"]
            for dt_ in range(8):
                t = qc * 8 + dt_
                opair = [[self.psA, self.psB], [self.psC, self.psD]][dt_ % 2]
                self.wait(TE, "V", res_done.get(dt_ - 2, v_at_start))
                for half in range(2):
                    ps = opair[half]
                    for a in range(NA):
                        TE.matmul(ps[:],
                                  self.OTn[0:64, a, dt_ * 128:(dt_ + 1) * 128],
                                  self.wo2r[:, a, half * 512:(half + 1) * 512],
                                  start=(a == 0), stop=False)
                    mm = TE.matmul(ps[:], self.ones2_bf[:],
                                   self.bo2_hl[:, half * 512:(half + 1) * 512],
                                   start=False, stop=True)
                    self.inc(mm, "T")
                so = "dO%d" % (t % 2)
                self.wait(V, so, dO_tile.get(t - 2, 0))
                self.wait(V, "T")
                for half in range(2):
                    psO = opair[half]
                    V.scalar_tensor_tensor(
                        self.x1[:, t, half * 512:(half + 1) * 512], psO[:], 0.0,
                        self.x1[:, t, half * 512:(half + 1) * 512],
                        ALU.add, ALU.add,
                        accum_out=self.sums[:, half:half + 1])
                self.inc(V.memset(self.sums[:, 8:9], 0.0), "V")
                V.drain()
                self._ln_finalize(self.x1[:, t, :], self.sums[:, 0:1],
                                  self.sums[:, 1:2], self.x1[:, t, :])
                res_done[dt_] = self.cnt["V"]
                self.wait(sy, "V")
                dO_tile[t] = self.inc(
                    sy.dma_start(out=self.out_h.ap()[b, t * 128:(t + 1) * 128, :],
                                 in_=self.x1[:, t, :]), so)
        if not last:
            self._wload_b1(first=False)


def _pack_inputs(inputs):
    g = lambda n: np.ascontiguousarray(np.asarray(inputs[n], np.float32))
    m = {}
    m["a_wq"] = g("L0_attn_wq").reshape(D, D).copy()
    m["a_wk"] = g("L0_attn_wk").reshape(D, D).copy()
    m["a_wv"] = g("L0_attn_wv").reshape(D, D).copy()
    m["a_wo"] = g("L0_attn_wo").reshape(D, D).copy()
    m["a_bq"] = g("L0_attn_bq").reshape(D).copy()
    m["a_bk"] = g("L0_attn_bk").reshape(D).copy()
    m["a_bv"] = g("L0_attn_bv").reshape(D).copy()
    m["a_bo"] = g("L0_attn_bo").reshape(D).copy()
    m["f_wq"] = g("L0_ffn_Wq")
    m["f_wk"] = g("L0_ffn_Wk")
    m["f_wv"] = g("L0_ffn_Wv")
    m["f_wo"] = g("L0_ffn_Wo")
    m["f_bq"] = g("L0_ffn_bq")
    m["f_bk"] = g("L0_ffn_bk")
    m["f_bv"] = g("L0_ffn_bv")
    m["f_bo"] = g("L0_ffn_bo")
    m["f_wr"] = g("L0_ffn_Wr")
    m["f_br"] = g("L0_ffn_br")
    return m


def kernel(**inputs):
    x = np.ascontiguousarray(np.asarray(inputs["L0_attn_x"], np.float32))
    B, S, _ = x.shape
    nb = B // N_CORES
    prog = Prog(S=S, NB=nb)
    wm = _pack_inputs(inputs)
    in_maps = []
    for core in range(N_CORES):
        im = dict(wm)
        im["x"] = np.ascontiguousarray(x[core * nb:(core + 1) * nb])
        in_maps.append(im)
    res = run_bass_kernel_spmd(prog.nc, in_maps, core_ids=list(range(N_CORES)))
    out = np.concatenate([r["out"] for r in res.results], axis=0)
    return out.astype(np.float32)


# version 15
# speedup vs baseline: 1.0632x; 1.0084x over previous
import numpy as np

import concourse.bass as bass
import concourse.mybir as mybir
from concourse.bass_utils import run_bass_kernel_spmd

F32 = mybir.dt.float32
F32R = mybir.dt.float32r
BF16 = mybir.dt.bfloat16
AF = mybir.ActivationFunctionType
ALU = mybir.AluOpType
AX = mybir.AxisListType

D = 1024
H = 16
HD = 64
NA = 4
EPS_ATTN = 1e-6
EPS_LN = 1e-5
N_CORES = 8

WNAMES = ["a_wq", "a_wk", "a_wv", "a_wo", "a_bq", "a_bk", "a_bv", "a_bo",
          "f_wq", "f_wk", "f_wv", "f_wo", "f_bq", "f_bk", "f_bv", "f_bo",
          "f_wr", "f_br"]


def _ap(t, offset, dims):
    return bass.AP(tensor=t[:].tensor, offset=offset, ap=[list(d) for d in dims])


class Prog:
    def __init__(self, S=2048, NB=2, debug=(), stop_after=None, p2_level=9,
                 p2y_level=9, p45_level=9, p45_qc=99):
        assert S % 1024 == 0
        self.stop_after = stop_after
        self.p2_level = p2_level
        self.p2y_level = p2y_level
        self.p45_level = p45_level
        self.p45_qc = p45_qc
        self.S, self.NB = S, NB
        self.NT = S // 128
        self.NG = self.NT // 4
        self.debug = set(debug)
        nc = bass.Bass("TRN2", target_bir_lowering=False)
        self.nc = nc

        dt = nc.dram_tensor
        self.x_h = dt("x", [NB, S, D], F32, kind="ExternalInput")
        self.out_h = dt("out", [NB, S, D], F32, kind="ExternalOutput")
        shp = {"a_wq": [D, D], "a_wk": [D, D], "a_wv": [D, D], "a_wo": [D, D],
               "a_bq": [D], "a_bk": [D], "a_bv": [D], "a_bo": [D],
               "f_wq": [D, D], "f_wk": [D, D], "f_wv": [D, D],
               "f_wo": [NA * HD, D], "f_bq": [D], "f_bk": [D], "f_bv": [D],
               "f_bo": [D], "f_wr": [D, H], "f_br": [H]}
        self.w = {n: dt(n, shp[n], F32, kind="ExternalInput") for n in WNAMES}
        self.wo1r_h = dt("wo1r", [D, D], F32R, kind="Internal")
        self.dbg = {}
        for name, s in [("d_qm", [NB, 128, 8, 512]), ("d_ynt", [NB, 128, 8, 512]),
                        ("d_xg", [NB, 8, D]), ("d_km", [NB, 128, D]),
                        ("d_vaug", [NB, 128, H * 65]), ("d_xsum", [NB, D]),
                        ("d_kv", [NB, 128, 16 * 65]),
                        ("d_x1", [NB, S, D]), ("d_r", [NB, H]),
                        ("d_sel", [NB, H, NA]),
                        ("d_qs", [NB, NA, 65, S]), ("d_ks", [NB, NA, 65, S]),
                        ("d_vs", [NB, S, NA * 65]), ("d_ot", [NB, NA, 64, S])]:
            if name in self.debug:
                self.dbg[name] = dt(name, s, F32, kind="ExternalOutput")

        it = nc.inline_tensor
        self.c_id = it(np.eye(128, dtype=np.float32), "c_id")
        self.c_ones2 = it(np.ones((2, 128), np.float32), "c_ones2")
        self.c_ones64 = it(np.ones((64, 1), np.float32), "c_ones64")
        o65 = np.ones((65, 1), np.float32); o65[64, 0] = 1e-30
        self.c_ones65 = it(o65, "c_ones65")
        self.c_ones128 = it(np.ones((128, 1), np.float32), "c_ones128")
        self.c_onesr64 = it(np.ones((1, 64), np.float32), "c_onesr64")
        self.c_u16 = it(np.triu(np.ones((H, H), np.float32), 1).copy(), "c_u16")
        lpar = np.zeros((H, 128), np.float32)
        for h in range(H):
            lpar[h, (h % 2) * 64:(h % 2) * 64 + 64] = 1.0
        self.c_lpar = it(lpar, "c_lpar")
        selmask = np.zeros((H, 32), np.float32)
        for h in range(H):
            selmask[h, (h // 2) * 4:(h // 2) * 4 + 4] = 1.0
        self.c_selmask = it(selmask, "c_selmask")
        self.c_i2 = it(np.vstack([np.eye(64, dtype=np.float32)] * 2).copy(), "c_i2")

        self._tr_rot = 0
        self._sbuf()
        self._sems()
        self._emit()

    def _sbuf(self):
        nc = self.nc
        S, NT = self.S, self.NT
        a = nc.alloc_sbuf_tensor

        def at(name, shape, dtp, off):
            return nc.alloc_sbuf_tensor_at(name, shape, dtp, offset=off)

        self.id_f = a("id_f", [128, 128], F32)
        self.id_bf = a("id_bf", [128, 128], BF16)
        self.ones2_bf = a("ones2_bf", [2, 128], BF16)
        self.ones64_bf = a("ones64_bf", [64, 1], BF16)
        self.ones65_bf = a("ones65_bf", [65, 1], BF16)
        self.onesr64_r = a("onesr64_r", [1, 64], F32R)
        self.ones128_f = a("ones128_f", [128, 1], F32)
        self.lpar_f = a("lpar_f", [H, 128], F32)
        self.selmask_f = a("selmask_f", [H, 32], F32)
        self.selp = a("selp", [H, 32], F32)
        self.i2_f = a("i2_f", [128, 64], F32)
        self.u16_f = a("u16_f", [H, H], F32)

        self.x1 = a("x1", [128, NT, D], F32)
        self.Vs = a("Vs", [128, NT, NA * 65], BF16)
        self.kv_z = a("kv_z", [128, H, 65], BF16)
        self.esc = a("esc", [128, 512], F32)
        self.esc2 = a("esc2", [128, 512], F32)
        self.sums = a("sums", [128, 10], F32)
        self.stat = a("stat", [128, 6], F32)
        self.wr_f = a("wr_f", [128, 8, H], F32)
        self.xsum = a("xsum", [1, D], F32)
        self.xsum_fm = a("xsum_fm", [128, 8], F32)
        self.nrm_f = a("nrm_f", [1, D], F32)

        self.bq1_fm = a("bq1_fm", [128, 8], F32)
        self.bq2_fm = a("bq2_fm", [128, 8], F32)
        self.bk2_fm = a("bk2_fm", [128, 8], F32)
        self.bv2_fm = a("bv2_fm", [128, 8], F32)
        self.bk1_hl = a("bk1_hl", [2, D], BF16)
        self.bv1_hl = a("bv1_hl", [2, D], BF16)
        self.bo1_hl = a("bo1_hl", [2, D], BF16)
        self.bo2_hl = a("bo2_hl", [2, D], BF16)
        self.br_row = a("br_row", [1, H], F32)

        self.r_row = a("r_row", [1, H], F32)
        self.z16 = a("z16", [128, H], F32)
        self.epsc = a("epsc", [128, 1], F32)
        self.vmax8 = a("vmax8", [1, 8], F32)
        self.active = a("active", [1, H], F32)
        self.activeT = a("activeT", [H, 1], F32)
        self.sel = a("sel", [H, NA], F32)
        self.kmax_row = a("kmax_row", [1, NA], F32)

        rqk = nc.bump_sbuf(NA * 2 * S * 2)[0]
        self.Qs = [at(f"Qs{i}", [65, S], BF16, rqk + i * (S * 2)) for i in range(NA)]
        self.Ks = [at(f"Ks{i}", [65, S], BF16, rqk + (NA + i) * (S * 2))
                   for i in range(NA)]

        ph = nc.bump_sbuf(78336)[0]
        self.ph = ph
        self.wk1 = at("wk1", [128, 8, D], BF16, ph + 0)
        self.wv1 = at("wv1", [128, 8, D], BF16, ph + 16384)
        self.xin = [at(f"xin{i}", [128, D], F32, ph + 32768 + i * 4096)
                    for i in range(2)]
        self.xin_bf = [at(f"xinb{i}", [128, D], BF16, ph + 32768 + i * 2048)
                       for i in range(4)]
        self.km = [at(f"km{i}", [128, D], BF16,
                      ph + (40960 if i == 0 else 65664)) for i in range(2)]
        self.vaug = [at(f"vaug{i}", [128, H * 65], BF16,
                        ph + (43008 if i == 0 else 67712)) for i in range(2)]
        self.xt1 = [at(f"xt1_{i}", [128, 8, 128], BF16, ph + 45184 + i * 2048)
                    for i in range(2)]
        self.wq1 = at("wq1", [128, 8, D], BF16, ph + 49280)
        self.xTg = at("xTg", [128, 8, 512], BF16, ph + 0)
        self.qm = at("qm", [128, 8, 512], BF16, ph + 8192)
        self.ynTg = at("ynTg", [128, 8, 512], F32R, ph + 16384)
        self.wo1_ring = [at(f"wo1g{i}", [128, 512], F32R,
                            ph + (32768 + i * 2048 if i < 4 else
                                  69824 + (i - 4) * 2048)) for i in range(8)]
        self.ynorm = [at(f"ynorm{i}", [128, D], F32,
                         ph + (40960 if i == 0 else 45056)) for i in range(2)]
        self.wq2 = at("wq2", [128, 8, D], BF16, ph + 0)
        self.wk2 = at("wk2", [128, 8, D], BF16, ph + 16384)
        self.wv2 = at("wv2", [128, 8, D], BF16, ph + 32768)
        self.x1T = at("x1T", [128, 8, 512], BF16, ph + 49152)
        self.qkv_full = at("qkv_full", [128, 8, 512], BF16, ph + 57344)
        self.shiftb = at("shiftb", [128, 512], BF16, ph + 65536)
        self.vsfm = at("vsfm", [128, 512], BF16, ph + 66560)
        self.G = at("G", [128, 8, 256], BF16, ph + 67584)
        self.sqb = [at(f"sqb{i}", [65, S], BF16, ph + i * 4096)
                    for i in range(2)]
        self.nac2 = [at(f"nac2_{i}", [1, S], F32, ph + 8192 + i * 8192)
                     for i in range(2)]
        self.OTn = at("OTn", [64, NA, D], F32R, ph + 0)
        self.Pt = [at(f"Pt{i}", [128, D], BF16, ph + 16384 + i * 2048)
                   for i in range(2)]
        self.rep_sb = at("rep_sb", [64, D], F32, ph + 20480)
        self.rec_row = at("rec_row", [1, D], F32R, ph + 24576)
        self.wo2r = at("wo2r", [64, NA, D], F32R, ph + 28672)
        self.wk1n = at("wk1n", [128, 8, D], BF16, ph + 45056)
        self.wv1n = at("wv1n", [128, 8, D], BF16, ph + 61440)
        self.brow = at("brow", [1, D], F32, ph + 0)
        self.blo_f = at("blo_f", [1, D], F32, ph + 4096)

        p = nc.alloc_psum_tensor
        (self.psA, self.psB, self.psC, self.psD, self.psE, self.psF,
         self.psG, self.psH) = [p(f"ps{i}", [128, 512], F32) for i in range(8)]

    def _sems(self):
        self.sem, self.cnt = {}, {}
        for n in ["dW", "dX", "dX0", "dX1", "dX2", "dX3",
                  "dB0", "dB1", "dB2",
                  "dG0", "dG1", "dG2", "dG3", "dO0", "dO1",
                  "dR0", "dR1", "dR2", "dR3", "dR4", "dR5", "dR6", "dR7",
                  "T", "V", "Sc"]:
            self.sem[n] = self.nc.alloc_semaphore(n)
            self.cnt[n] = 0

    def inc(self, instr, name):
        amt = 16 if name.startswith("d") else 1
        instr.then_inc(self.sem[name], amt)
        self.cnt[name] += amt
        return self.cnt[name]

    def wait(self, eng, name, val=None):
        eng.wait_ge(self.sem[name], self.cnt[name] if val is None else val)

    def _emit(self):
        nc = self.nc
        sa = self.stop_after
        with nc.Block():
            self._startup()
            nc.all_engine_barrier()
            self._wload_b1(first=True)
            self._w1_done = self.cnt["dW"]
            nc.all_engine_barrier()
            for b in range(self.NB):
                if sa == "startup":
                    break
                self._phase1(b)
                nc.all_engine_barrier()
                if sa == "p1":
                    break
                self._phase2(b)
                nc.all_engine_barrier()
                if sa == "p2":
                    break
                self._router(b)
                self._wload_b2()
                nc.all_engine_barrier()
                if sa == "router":
                    break
                self._phase3(b)
                nc.all_engine_barrier()
                if sa == "p3":
                    break
                self._wload_wo2()
                self._bounds(b)
                nc.all_engine_barrier()
                if sa == "bounds":
                    break
                self._phase45(b, last=(b == self.NB - 1))
                nc.all_engine_barrier()
                if b + 1 < self.NB:
                    gp = nc.gpsimd
                    self.inc(gp.dma_start(out=self.wk1[:], in_=self.wk1n[:]), "dW")
                    self.inc(gp.dma_start(out=self.wv1[:], in_=self.wv1n[:]), "dW")
                    gp.wait_ge(self.sem["dW"], self.cnt["dW"])
                    self._wload_wq1()
                    self._w1_done = self.cnt["dW"]
                    nc.all_engine_barrier()
            for s in ["dX", "dX0", "dX1", "dX2", "dX3", "dO0", "dO1",
                      "dR0", "dR1", "dR2",
                      "dR3", "dR4", "dR5", "dR6", "dR7"]:
                self.wait(nc.sync, s)
            for s in ["dG0", "dG1", "dG2", "dG3", "dB0", "dB1", "dB2"]:
                self.wait(nc.sync, s)
            self.wait(nc.gpsimd, "dW")

    def _startup(self):
        nc = self.nc
        sy, gp, V = nc.sync, nc.gpsimd, nc.vector
        w = self.w
        for dst, src in [(self.id_f, self.c_id), (self.ones128_f, self.c_ones128),
                         (self.lpar_f, self.c_lpar),
                         (self.selmask_f, self.c_selmask),
                         (self.i2_f, self.c_i2), (self.u16_f, self.c_u16)]:
            self.inc(sy.dma_start(out=dst[:], in_=src.ap()), "dX")
        for dst, src in [(self.id_bf, self.c_id), (self.ones2_bf, self.c_ones2),
                         (self.ones64_bf, self.c_ones64),
                         (self.ones65_bf, self.c_ones65),
                         (self.onesr64_r, self.c_onesr64)]:
            self.inc(gp.dma_start(out=dst[:], in_=src.ap()), "dW")
        self.inc(gp.dma_start(out=self.wo1r_h.ap(), in_=w["a_wo"].ap()), "dW")
        self.inc(sy.dma_start(
            out=self.wr_f[:],
            in_=w["f_wr"].ap().rearrange("(p c) h -> p c h", p=128)), "dX")
        for src, dst in [("a_bq", self.bq1_fm), ("f_bq", self.bq2_fm),
                         ("f_bk", self.bk2_fm), ("f_bv", self.bv2_fm)]:
            self.inc(sy.dma_start(
                out=dst[:], in_=w[src].ap().rearrange("(c p) -> p c", p=128),
                allow_slow_non_contiguous=True), "dX")
        self.inc(sy.dma_start(out=self.br_row[:], in_=w["f_br"].ap()[None, :]),
                 "dX")
        for src, dst in [("a_bk", self.bk1_hl), ("a_bv", self.bv1_hl),
                         ("a_bo", self.bo1_hl), ("f_bo", self.bo2_hl)]:
            self.inc(sy.dma_start(out=self.brow[:], in_=w[src].ap()[None, :]),
                     "dX")
            self.wait(V, "dX")
            self.wait(V, "dW")
            V.tensor_copy(dst[0:1, :], self.brow[:])
            V.drain()
            iv = self.inc(V.tensor_sub(self.blo_f[:], self.brow[:], dst[0:1, :]),
                          "V")
            self.wait(gp, "V", iv)
            self.inc(gp.dma_start(out=dst[1:2, :], in_=self.blo_f[:]), "dW")
            sy.wait_ge(self.sem["V"], iv)
        self.wait(V, "dW")
        V.memset(_ap(self.Vs, 64, [[self.NT * NA * 65, 128], [65, self.NT * NA]]),
                 1.0)
        V.memset(self.epsc[:], EPS_LN)
        V.memset(self.sums[:], 0.0)
        self.inc(V.memset(self.stat[:], 0.0), "V")

    def _wload_b1(self, first=False):
        gp = self.nc.gpsimd
        for src, dst in [("a_wk", self.wk1 if first else self.wk1n),
                         ("a_wv", self.wv1 if first else self.wv1n)]:
            self.inc(gp.dma_start(
                out=dst[:],
                in_=self.w[src].ap().rearrange("(c p) f -> p c f", p=128)), "dW")
        if first:
            self._wload_wq1()

    def _wload_wq1(self):
        gp = self.nc.gpsimd
        self.inc(gp.dma_start(
            out=self.wq1[:],
            in_=self.w["a_wq"].ap().rearrange("(c p) f -> p c f", p=128)), "dW")

    def _wload_b2(self):
        gp = self.nc.gpsimd
        self._w2_done = {}
        for src, dst, kind, sname in [("f_wq", self.wq2, "q", "dB0"),
                                      ("f_wk", self.wk2, "k", "dB1"),
                                      ("f_wv", self.wv2, "v", "dB2")]:
            self.wait(gp, sname)
            cnt = self.inc(gp.dma_start(
                out=dst[:],
                in_=self.w[src].ap().rearrange("(c p) f -> p c f", p=128)), sname)
            self._w2_done[kind] = (sname, cnt)

    def _wload_wo2(self):
        gp = self.nc.gpsimd
        self.inc(gp.dma_start(
            out=self.wo2r[:],
            in_=self.w["f_wo"].ap().rearrange("(a d) f -> d a f", d=HD)), "dW")

    def _tr8(self, src_f32, evac_list, banks=None):
        nc = self.nc
        TE = nc.tensor
        if banks is None:
            banks = [self.psG, self.psH]
        if not hasattr(self, "_tr_evac"):
            self._tr_evac = {}
        for g in range(2):
            bi = (self._tr_rot + g) % len(banks)
            ps = banks[bi]
            esem, ecnt = self._tr_evac.get(id(ps), ("Sc", 0))
            self.wait(TE, esem, ecnt)
            if not hasattr(self, "_ps_vfree"):
                self._ps_vfree = {}
            self.wait(TE, "V", self._ps_vfree.get(id(ps), 0))
            last = None
            for c4 in range(4):
                c = 4 * g + c4
                last = TE.transpose(ps[:, c4 * 128:(c4 + 1) * 128],
                                    src_f32[:, c * 128:(c + 1) * 128],
                                    self.id_f[:])
            it = self.inc(last, "T")
            self.nc.scalar.wait_ge(self.sem["T"], it)
            self._tr_evac[id(ps)] = evac_list(g, ps)
        self._tr_rot = (self._tr_rot + 2) % len(banks)

    def _phase1(self, b):
        nc = self.nc
        sy = nc.sync
        TE, SC, V = nc.tensor, nc.scalar, nc.vector
        NT = self.NT
        self.wait(TE, "dW", self._w1_done)
        V.memset(_ap(self.vaug[0], 64, [[H * 65, 128], [65, H]]), 1.0)
        V.memset(_ap(self.vaug[1], 64, [[H * 65, 128], [65, H]]), 1.0)
        self.inc(V.memset(self.sums[:, 7:8], 0.0), "V")
        v_done = self.cnt["V"]
        self._p1_vdone = {}
        gp = nc.gpsimd
        for t in range(NT):
            if t % 2 == 0:
                eng, sx = sy, "dX%d" % ((t // 2) % 4)
            else:
                eng, sx = gp, "dG%d" % ((t // 2) % 4)
            self.wait(eng, sx)
            self.wait(eng, "dO%d" % (t % 2))
            self.inc(eng.dma_start(
                out=self.x1[:, t, :],
                in_=self.x_h.ap()[b, t * 128:(t + 1) * 128, :]), sx)
            self.wait(TE, sx)
            self.wait(TE, "V", v_done)
            xt = self.xt1[t % 2]

            def evac(gg, ps, xt=xt):
                out = _ap(xt, 4 * gg * 128, [[1024, 128], [1, 512]])
                if gg % 2 == 0:
                    return ("Sc", self.inc(SC.activation(out, ps[:], AF.Copy),
                                           "Sc"))
                self.wait(V, "T", self.cnt["T"])
                return ("V", self.inc(V.tensor_copy(out, ps[:]), "V"))

            self._tr8(self.x1[:, t, :], evac)
            self.wait(TE, "Sc")
            self.wait(TE, "V")
            tmark = {}
            for ti, (wsb, bhl) in enumerate([(self.wk1, self.bk1_hl),
                                             (self.wv1, self.bv1_hl)]):
                for half in range(2):
                    ps = [self.psA, self.psB, self.psC, self.psD][2 * ti + half]
                    for c in range(8):
                        TE.matmul(ps[:], xt[:, c, :],
                                  wsb[:, c, half * 512:(half + 1) * 512],
                                  start=(c == 0), stop=False)
                    mm = TE.matmul(ps[:], self.ones2_bf[:],
                                   bhl[:, half * 512:(half + 1) * 512],
                                   start=False, stop=True)
                    tmark[(ti, half)] = self.inc(mm, "T")
            for half in range(2):
                ps = [self.psA, self.psB][half]
                self.wait(V, "T", tmark[(0, half)])
                V.tensor_scalar_min(self.esc[:], ps[:], 0.0)
                iv = self.inc(V.memset(self.sums[:, 7:8], 0.0), "V")
                self.wait(SC, "V", iv)
                isc = self.inc(SC.activation(self.esc2[:], self.esc[:], AF.Exp),
                               "Sc")
                self.wait(V, "Sc", isc)
                st = V.scalar_tensor_tensor(
                    self.km[t % 2][:, half * 512:(half + 1) * 512],
                    ps[:], 0.0, self.esc2[:], ALU.max, ALU.add)
                self.inc(st, "V")
            for half in range(2):
                ps = [self.psC, self.psD][half]
                self.wait(V, "T", tmark[(1, half)])
                out = _ap(self.vaug[t % 2], half * 8 * 65,
                          [[H * 65, 128], [65, 8], [1, 64]])
                self.inc(V.tensor_copy(out, ps[:]), "V")
            v_done = self.cnt["V"]
            self._p1_vdone[t] = v_done
            if t == 0 and "d_km" in self.dbg:
                gp = nc.gpsimd
                self.wait(gp, "V", v_done)
                self.inc(gp.dma_start(out=self.dbg["d_km"].ap()[b],
                                      in_=self.km[t % 2][:]), "dW")
                idm = self.inc(gp.dma_start(out=self.dbg["d_vaug"].ap()[b],
                                            in_=self.vaug[t % 2][:]), "dW")
                gp.wait_ge(self.sem["dW"], idm)
                self.wait(V, "dW", idm)
            if t > 0:
                self._p1_kv_accum(t - 1, NT)
        self._p1_kv_accum(NT - 1, NT)
        self.wait(V, "T")
        V.memset(self.kv_z[:], 0.0)
        V.drain()
        for bi, bank in enumerate([self.psE, self.psF]):
            oute = _ap(self.kv_z, (2 * 4 * bi) * 65,
                       [[H * 65, 64], [2 * 65, 4], [1, 65]])
            srce = _ap(bank, 0, [[512, 64], [128, 4], [1, 65]])
            V.tensor_copy(oute, srce)
            V.drain()
            oto = _ap(self.kv_z, (2 * 4 * bi + 1) * 65 + 64 * H * 65,
                      [[H * 65, 64], [2 * 65, 4], [1, 65]])
            srco = _ap(bank, 64 * 512, [[512, 64], [128, 4], [1, 65]])
            self.inc(V.tensor_copy(oto, srco), "V")
            V.drain()
        if "d_kv" in self.dbg:
            gp = nc.gpsimd
            self.wait(gp, "V")
            idm = self.inc(gp.dma_start(
                out=self.dbg["d_kv"].ap()[b],
                in_=self.kv_z[:].rearrange("p c k -> p (c k)")), "dW")
            gp.wait_ge(self.sem["dW"], idm)

    def _p1_kv_accum(self, t, NT):
        TE = self.nc.tensor
        self.wait(TE, "V", self._p1_vdone[t])
        last = None
        for h in range(H):
            bank = self.psE if h < 8 else self.psF
            slot = (h % 8) // 2
            bp = (h % 2) * 64
            last = TE.matmul(bank[bp:bp + 64, slot * 128:slot * 128 + 65],
                             self.km[t % 2][:, h * 64:(h + 1) * 64],
                             _ap(self.vaug[t % 2], h * 65,
                                 [[H * 65, 128], [1, 65]]),
                             start=(t == 0 and h % 8 < 2),
                             stop=(t == NT - 1), skip_group_check=True)
        self.inc(last, "T")

    def _phase2(self, b):
        nc = self.nc
        sy = nc.sync
        TE, SC, V = nc.tensor, nc.scalar, nc.vector
        NG = self.NG
        self.wait(TE, "dW", self._w1_done)
        self.qm_evac = {}
        self._ring_used = {}
        self._xsum_pending = None
        self._ynorm_v = {}
        self._ynT_done = {}
        V.memset(self.xsum[:], 0.0)
        self.inc(V.memset(self.sums[:], 0.0), "V")
        ring_cnt = 0
        for g in range(NG):
            for dt_ in range(4):
                t = g * 4 + dt_

                def evac(gg, ps, dt_=dt_):
                    out = _ap(self.xTg, (4 * gg) * 512 + dt_ * 128,
                              [[8 * 512, 128], [512, 4], [1, 128]])
                    if gg % 2 == 0:
                        return ("Sc", self.inc(
                            SC.activation(out, ps[:], AF.Copy), "Sc"))
                    self.wait(V, "T", self.cnt["T"])
                    return ("V", self.inc(V.tensor_copy(out, ps[:]), "V"))

                self._tr8(self.x1[:, t, :], evac)
            self.wait(TE, "Sc")
            self.wait(TE, "V")
            self._emit_xsum(b)
            if self.p2_level < 2:
                continue
            for hc in range(8):
                ps = [self.psA, self.psB, self.psC, self.psD][hc % 4]
                if hc >= 4:
                    self.wait(TE, "V", self.qm_evac[hc - 4])
                mm = None
                for c in range(8):
                    mm = TE.matmul(ps[:], self.wq1[:, c, hc * 128:(hc + 1) * 128],
                                   self.xTg[:, c, :], start=(c == 0), stop=(c == 7))
                it = self.inc(mm, "T")
                self.wait(SC, "T", it)
                self.wait(SC, "V", self.qm_evac.get(hc - 1, 0))
                isc = self.inc(SC.activation(self.esc2[:], ps[:], AF.Relu,
                                             bias=self.bq1_fm[:, hc:hc + 1]), "Sc")
                self.wait(V, "Sc", isc)
                V.tensor_scalar(self.esc[:], ps[:], self.bq1_fm[:, hc:hc + 1],
                                None, ALU.add)
                V.drain()
                V.tensor_sub(self.esc[:], self.esc[:], self.esc2[:])
                iv = self.inc(V.memset(self.sums[:, 7:8], 0.0), "V")
                self.wait(SC, "V", iv)
                isc = self.inc(SC.activation(self.esc[:], self.esc[:], AF.Exp),
                               "Sc")
                self.wait(V, "Sc", isc)
                self.qm_evac[hc] = self.inc(
                    V.tensor_add(self.qm[:, hc, :], self.esc[:], self.esc2[:]),
                    "V")
            self.wait(TE, "V")
            if self.p2_level < 3:
                continue
            for dt_ in range(4):
                self._p2_y_tile(dt_)
                if dt_ > 0:
                    self._p2_ynt(dt_ - 1)
            self._p2_ynt(3)
            if self.p2_level < 4:
                continue
            if g == 0 and "d_qm" in self.dbg:
                gpq = nc.gpsimd
                self.wait(gpq, "V")
                self.wait(gpq, "Sc")
                self.inc(gpq.dma_start(out=self.dbg["d_qm"].ap()[b],
                                       in_=self.qm[:]), "dW")
                idm = self.inc(gpq.dma_start(out=self.dbg["d_ynt"].ap()[b],
                                             in_=self.ynTg[:]), "dW")
                gpq.wait_ge(self.sem["dW"], idm)
                self.wait(TE, "dW", idm)
            self.wait(TE, "Sc")
            for half in range(2):
                for c in range(8):
                    slot = ring_cnt % 8
                    sr = "dR%d" % slot
                    self.wait(sy, sr)
                    self.wait(sy, "T", self._ring_used.get(slot, 0))
                    self.inc(sy.dma_start(
                        out=self.wo1_ring[slot][:],
                        in_=self.wo1r_h.ap()[c * 128:(c + 1) * 128,
                                             half * 512:(half + 1) * 512]), sr)
                    ring_cnt += 1
                    self.wait(TE, sr)
                    if c == 0:
                        self.wait(TE, "V")
                    for dt_ in range(4):
                        psO = [self.psA, self.psB, self.psC, self.psD][dt_]
                        mm = TE.matmul(
                            psO[:],
                            _ap(self.ynTg, (dt_ % 2) * 64 + c * 512,
                                [[8 * 512, 128], [1, 64]]).bitcast(F32R)
                            if False else
                            self.ynTg[:, c, dt_ * 128:(dt_ + 1) * 128],
                            self.wo1_ring[slot][:],
                            start=(c == 0), stop=False)
                        if c == 7:
                            mm = TE.matmul(psO[:], self.ones2_bf[:],
                                           self.bo1_hl[:, half * 512:(half + 1) * 512],
                                           start=False, stop=True)
                    if c == 7:
                        self.inc(mm, "T")
                    else:
                        self.inc(mm, "T")
                    self._ring_used[slot] = self.cnt["T"]
                if self.p2_level < 5:
                    continue
                self.wait(V, "T")
                for dt_ in range(4):
                    t = g * 4 + dt_
                    psO = [self.psA, self.psB, self.psC, self.psD][dt_]
                    st = V.scalar_tensor_tensor(
                        self.x1[:, t, half * 512:(half + 1) * 512], psO[:], 0.0,
                        self.x1[:, t, half * 512:(half + 1) * 512],
                        ALU.add, ALU.add,
                        accum_out=self.sums[:, 2 * dt_ + half:2 * dt_ + half + 1])
                self.inc(V.memset(self.sums[:, 8:9], 0.0), "V")
                V.drain()
                if not hasattr(self, "_ps_vfree"):
                    self._ps_vfree = {}
                for pso in [self.psA, self.psB, self.psC, self.psD]:
                    self._ps_vfree[id(pso)] = self.cnt["V"]
            if self.p2_level < 6:
                continue
            for dt_ in range(4):
                t = g * 4 + dt_
                self._ln_finalize(self.x1[:, t, :],
                                  self.sums[:, 2 * dt_:2 * dt_ + 1],
                                  self.sums[:, 2 * dt_ + 1:2 * dt_ + 2],
                                  self.x1[:, t, :])
            self._ln_done = self.cnt["V"]
            self._xsum_pending = g
            if "d_xg" in self.dbg:
                gpx = nc.gpsimd
                self.wait(gpx, "V")
                idm = self.inc(gpx.dma_start(
                    out=self.dbg["d_xg"].ap()[b, g][None, :], in_=self.xsum[:]),
                    "dW")
                gpx.wait_ge(self.sem["dW"], idm)
                self.wait(V, "dW", idm)
        self._emit_xsum(b)
        if "d_x1" in self.dbg:
            self.wait(sy, "V")
            for t in range(self.NT):
                self.inc(sy.dma_start(
                    out=self.dbg["d_x1"].ap()[b, t * 128:(t + 1) * 128, :],
                    in_=self.x1[:, t, :]), "dX")
            self.wait(sy, "dX")

    def _emit_xsum(self, b):
        if getattr(self, "_xsum_pending", None) is None:
            return
        g = self._xsum_pending
        self._xsum_pending = None
        nc = self.nc
        TE, V = nc.tensor, nc.vector
        self.wait(TE, "V", self._ln_done)
        self.wait(TE, "V", getattr(self, "_xsum_vfree", 0))
        last = None
        for dt_ in range(4):
            t = g * 4 + dt_
            for half in range(2):
                ps = [self.psE, self.psF][half]
                last = TE.matmul(ps[0:1, :], self.ones128_f[:],
                                 self.x1[:, t, half * 512:(half + 1) * 512],
                                 start=(dt_ == 0), stop=(dt_ == 3),
                                 skip_group_check=True)
        it = self.inc(last, "T")
        self.wait(V, "T", it)
        for half in range(2):
            ps = [self.psE, self.psF][half]
            V.tensor_add(self.xsum[0:1, half * 512:(half + 1) * 512],
                         self.xsum[0:1, half * 512:(half + 1) * 512],
                         ps[0:1, :])
        V.drain()
        self._xsum_vfree = self.inc(V.memset(self.sums[:, 8:9], 0.0), "V")

    def _p2_y_tile(self, dt_):
        nc = self.nc
        TE, SC, V = nc.tensor, nc.scalar, nc.vector
        lvl = getattr(self, "p2y_level", 9)
        if not hasattr(self, "_y_pair_free"):
            self._y_pair_free = {}
        pairs = [[self.psC, self.psD], [self.psA, self.psB]]
        marks = {}
        for hg in range(2):
            bpair = pairs[hg]
            self.wait(TE, "V", self._y_pair_free.get(hg, 0))
            last = None
            for hh in range(8):
                h = hg * 8 + hh
                bank = bpair[0] if hh < 4 else bpair[1]
                slot = hh % 4
                qm_lhs = self.qm[:, h // 2, dt_ * 128:(dt_ + 1) * 128]
                kv_rhs = self.kv_z[:, h, 0:65]
                last = TE.matmul(bank[:, slot * 128:slot * 128 + 65], qm_lhs,
                                 kv_rhs, start=True, stop=True)
            marks[hg] = self.inc(last, "T")
        banks4 = [(hg * 8 + bi * 4, pairs[hg][bi])
                  for hg in range(2) for bi in range(2)]
        self.wait(V, "T", marks[1])
        self.wait(V, "T", self._ynT_done.get(dt_ % 2, 0))
        if lvl >= 2:
            V.drain()
            for i, (h0, bank) in enumerate(banks4):
                V.tensor_scalar(self.z16[:, i * 4:(i + 1) * 4],
                                _ap(bank, 64, [[512, 128], [128, 4]]),
                                EPS_ATTN, None, ALU.add)
            V.drain()
            V.reciprocal(self.z16[:], self.z16[:])
            V.drain()
        iv = None
        for i, (h0, bank) in enumerate(banks4):
            dst = _ap(self.ynorm[dt_ % 2], h0 * 64,
                      [[1024, 128], [64, 4], [1, 64]])
            src = _ap(bank, 0, [[512, 128], [128, 4], [1, 64]])
            if lvl < 2:
                iv = self.inc(V.tensor_copy(dst, src), "V")
            else:
                zb = _ap(self.z16, i * 4, [[H, 128], [1, 4], [0, 64]])
                iv = self.inc(V.tensor_tensor(dst, src, zb, ALU.mult), "V")
            if i == 1:
                self._y_pair_free[0] = iv
        self._y_pair_free[1] = iv
        self._ynorm_v[dt_] = self.cnt["V"]

    def _p2_ynt(self, dt_):
        nc = self.nc
        TE, SC = nc.tensor, nc.scalar

        def evac(gg, ps, dt_=dt_):
            out = _ap(self.ynTg, (4 * gg) * 512 + dt_ * 128,
                      [[8 * 512, 128], [512, 4], [1, 128]])
            with nc.allow_low_precision(reason="f32r operand prep"):
                return ("Sc", self.inc(SC.activation(out, ps[:], AF.Copy),
                                       "Sc"))

        self.wait(TE, "V", self._ynorm_v[dt_])
        self._tr8(self.ynorm[dt_ % 2][:], evac)
        self._ynT_done[dt_ % 2] = self.cnt["T"]

    def _ln_finalize(self, xpre, s0, s1, x_out):
        nc = self.nc
        SC, V = nc.scalar, nc.vector
        for half in range(2):
            V.scalar_tensor_tensor(
                [self.esc, self.esc2][half][:],
                xpre[:, half * 512:(half + 1) * 512], 1.0,
                xpre[:, half * 512:(half + 1) * 512], ALU.mult, ALU.mult,
                accum_out=self.sums[:, 8 + half:9 + half])
        V.drain()
        V.tensor_add(self.stat[:, 0:1], s0, s1)
        V.tensor_add(self.stat[:, 1:2], self.sums[:, 8:9], self.sums[:, 9:10])
        V.drain()
        V.scalar_tensor_tensor(self.stat[:, 2:3], self.stat[:, 0:1], 1.0 / D,
                               self.stat[:, 0:1], ALU.mult, ALU.mult)
        V.drain()
        iv = self.inc(V.tensor_sub(self.stat[:, 1:2], self.stat[:, 1:2],
                                   self.stat[:, 2:3]), "V")
        self.wait(SC, "V", iv)
        SC.activation(self.stat[:, 4:5], self.stat[:, 1:2], AF.Ln,
                      scale=1.0 / D, bias=self.epsc[:, 0:1])
        SC.drain()
        isc = self.inc(SC.activation(self.stat[:, 2:3], self.stat[:, 4:5],
                                     AF.Exp, scale=-0.5), "Sc")
        self.wait(V, "Sc", isc)
        V.scalar_tensor_tensor(self.stat[:, 3:4], self.stat[:, 0:1], -1.0 / D,
                               self.stat[:, 2:3], ALU.mult, ALU.mult)
        V.drain()
        self.inc(V.tensor_scalar(x_out, xpre, self.stat[:, 2:3],
                                 self.stat[:, 3:4], ALU.mult, ALU.add), "V")

    def _router(self, b):
        nc = self.nc
        sy, gp = nc.sync, nc.gpsimd
        TE, SC, V = nc.tensor, nc.scalar, nc.vector
        self.wait(gp, "V")
        if "d_xsum" in self.dbg:
            idm = self.inc(gp.dma_start(out=self.dbg["d_xsum"].ap()[b][None, :],
                                        in_=self.xsum[:]), "dW")
            gp.wait_ge(self.sem["dW"], idm)
        self.inc(gp.dma_start(
            out=self.xsum_fm[:],
            in_=_ap(self.xsum, 0, [[1024, 1], [8, 128], [1, 8]]),
            allow_slow_non_contiguous=True), "dW")
        gp.wait_ge(self.sem["dW"], self.cnt["dW"])
        self.wait(TE, "dW")
        last = None
        for c in range(8):
            last = TE.matmul(self.psA[0:1, 0:H], self.xsum_fm[:, c:c + 1],
                             self.wr_f[:, c, :], start=(c == 0), stop=(c == 7))
        it = self.inc(last, "T")
        self.wait(V, "T", it)
        V.scalar_tensor_tensor(self.r_row[:], self.br_row[:], float(self.S),
                               self.psA[0:1, 0:H], ALU.mult, ALU.add)
        V.drain()
        V.max(out=self.vmax8[:], in_=self.r_row[:])
        V.drain()
        iv = self.inc(V.tensor_scalar(self.active[:], self.r_row[:],
                                      self.vmax8[:, NA - 1:NA], None, ALU.is_ge),
                      "V")
        V.memset(self.xsum[:], 0.0)
        self.wait(gp, "V", iv)
        self.inc(gp.dma_start(out=self.activeT[:],
                              in_=_ap(self.active, 0, [[16, 1], [1, 16]]),
                              allow_slow_non_contiguous=True), "dW")
        gp.wait_ge(self.sem["dW"], self.cnt["dW"])
        self.wait(TE, "dW")
        mm = TE.matmul(self.psC[0:H, 0:1], self.u16_f[:], self.activeT[:],
                       start=True, stop=True)
        it = self.inc(mm, "T")
        self.wait(V, "T", it)
        last = None
        for a in range(NA):
            last = V.scalar_tensor_tensor(self.sel[:, a:a + 1], self.psC[0:H, 0:1],
                                          float(a), self.activeT[:],
                                          ALU.is_equal, ALU.mult)
        iv = self.inc(last, "V")
        V.drain()
        s_b = _ap(self.sel, 0, [[NA, H], [0, 8], [1, NA]])
        m_b = _ap(self.selmask_f, 0, [[32, H], [NA, 8], [1, NA]])
        o_b = _ap(self.selp, 0, [[32, H], [NA, 8], [1, NA]])
        iv = self.inc(V.tensor_tensor(o_b, s_b, m_b, ALU.mult), "V")
        self.wait(TE, "V", iv)
        mm = TE.matmul(self.psD[:, 0:32], self.lpar_f[:], self.selp[:],
                       start=True, stop=True)
        it = self.inc(mm, "T")
        self.wait(V, "T", it)
        last = None
        for c in range(8):
            src = _ap(self.psD, c * 4, [[512, 128], [1, NA], [0, 64]])
            i2b = _ap(self.i2_f, 0, [[64, 128], [0, NA], [1, 64]])
            dst = _ap(self.G, c * 256, [[8 * 256, 128], [64, NA], [1, 64]])
            last = V.tensor_tensor(dst, src, i2b, ALU.mult)
        self.inc(last, "V")
        self.wait(TE, "V")
        if "d_r" in self.dbg:
            self.wait(sy, "V")
            self.inc(sy.dma_start(out=self.dbg["d_r"].ap()[b][None, :],
                                  in_=self.r_row[:]), "dX")
        if "d_sel" in self.dbg:
            self.wait(sy, "V")
            self.inc(sy.dma_start(out=self.dbg["d_sel"].ap()[b], in_=self.sel[:]),
                     "dX")

    def _phase3(self, b):
        nc = self.nc
        sy, gp = nc.sync, nc.gpsimd
        TE, SC, V = nc.tensor, nc.scalar, nc.vector
        for g in range(self.NG):
            for dt_ in range(4):
                t = g * 4 + dt_

                def evac(gg, ps, dt_=dt_):
                    out = _ap(self.x1T, (4 * gg) * 512 + dt_ * 128,
                              [[8 * 512, 128], [512, 4], [1, 128]])
                    if gg % 2 == 0:
                        return ("Sc", self.inc(
                            SC.activation(out, ps[:], AF.Copy), "Sc"))
                    self.wait(V, "T", self.cnt["T"])
                    return ("V", self.inc(V.tensor_copy(out, ps[:]), "V"))

                self._tr8(self.x1[:, t, :], evac,
                          banks=[self.psA, self.psB, self.psG, self.psH])
            self.wait(TE, "Sc")
            self.wait(TE, "V")
            for ti, (wsb, bias, kind) in enumerate([
                    (self.wq2, self.bq2_fm, "q"),
                    (self.wk2, self.bk2_fm, "k"),
                    (self.wv2, self.bv2_fm, "v")]):
                self.wait(TE, "V")
                sname, scnt = self._w2_done[kind]
                self.wait(TE, sname, scnt)
                evacd = {}
                for hc in range(9):
                    if hc < 8:
                        ps = [self.psA, self.psB][hc % 2]
                        if hc >= 2:
                            self.wait(TE, "V", evacd[hc - 2])
                        mm = None
                        for c in range(8):
                            mm = TE.matmul(ps[:],
                                           wsb[:, c, hc * 128:(hc + 1) * 128],
                                           self.x1T[:, c, :],
                                           start=(c == 0), stop=(c == 7))
                        it = self.inc(mm, "T")
                        self.wait(V, "T", it)
                        evacd[hc] = self.inc(
                            V.tensor_scalar(self.qkv_full[:, hc, :], ps[:],
                                            bias[:, hc:hc + 1], None, ALU.add),
                            "V")
                    cg = hc - 1
                    if cg >= 0:
                        self.wait(TE, "V", evacd[cg])
                        if cg == 0:
                            self.wait(TE, "Sc", getattr(self, "_gps_free", 0))
                        for j in range(2):
                            psj = [self.psC, self.psD][j]
                            mm = TE.matmul(psj[:],
                                           self.G[:, cg, j * 128:(j + 1) * 128],
                                           self.qkv_full[:, cg, :],
                                           start=(cg == 0), stop=(cg == 7),
                                           skip_group_check=True)
                            if cg == 7:
                                self.inc(mm, "T")
                for j in range(2):
                    ps = [self.psC, self.psD][j]
                    it = self.cnt["T"]
                    if kind == "v":
                        self.wait(SC, "T", it)
                        isc = self.inc(SC.activation(self.vsfm[:], ps[:],
                                                     AF.Copy), "Sc")
                        self.wait(TE, "Sc", isc)
                        tr = None
                        for dt_ in range(4):
                            out_bf = self.psE[:, dt_ * 64:(dt_ + 1) * 64].bitcast(
                                BF16)
                            tr = TE.transpose(out_bf,
                                              self.vsfm[:, dt_ * 128:(dt_ + 1) * 128],
                                              self.id_bf[:])
                        it2 = self.inc(tr, "T")
                        self.wait(SC, "T", it2)
                        isc = None
                        for dt_ in range(4):
                            t = g * 4 + dt_
                            dst = _ap(self.Vs, t * NA * 65 + (2 * j) * 65,
                                      [[self.NT * NA * 65, 128], [65, 2], [1, 64]])
                            srcb = self.psE[:, dt_ * 64:(dt_ + 1) * 64].bitcast(
                                BF16)
                            src2 = bass.AP(tensor=srcb.tensor, offset=srcb.offset,
                                           ap=[list(srcb.ap[0]), [64, 2], [1, 64]])
                            isc = self.inc(SC.activation(dst, src2, AF.Copy),
                                           "Sc")
                        self.wait(TE, "Sc", isc)
                        self._gps_free = self.cnt["Sc"]
                    else:
                        dsts = self.Qs if kind == "q" else self.Ks
                        scale = 0.125 if kind == "q" else 1.0
                        self.wait(SC, "T", it)
                        self.inc(SC.activation(
                            dsts[2 * j][0:64, g * 512:(g + 1) * 512],
                            ps[0:64, :], AF.Copy, scale=scale), "Sc")
                        self.wait(SC, "dW", getattr(self, "_shift_idm", 0))
                        isc = self.inc(SC.activation(
                            self.shiftb[64:128, :], ps[64:128, :], AF.Copy,
                            scale=scale), "Sc")
                        self.wait(gp, "Sc", isc)
                        idm = self.inc(gp.dma_start(
                            out=dsts[2 * j + 1][0:64, g * 512:(g + 1) * 512],
                            in_=self.shiftb[64:128, :]), "dW")
                        gp.wait_ge(self.sem["dW"], idm)
                        self._shift_idm = idm
                        self._gps_free = self.cnt["Sc"]

    def _bounds(self, b):
        nc = self.nc
        sy = nc.sync
        TE, SC, V = nc.tensor, nc.scalar, nc.vector
        S = self.S
        nch = S // 512
        V.memset(self.sqb[0][64:65, :], 1.0)
        V.memset(self.sqb[1][64:65, :], 1.0)
        units = []
        for a in range(NA):
            units.append((a, "k", self.Ks[a]))
            units.append((a, "q", self.Qs[a]))
        mm_done, ln_done, fin_done, lkm_done = {}, {}, {}, {}
        for u, (a, which, ten) in enumerate(units):
            sq = self.sqb[u % 2]
            nac = self.nac2[u % 2]
            pb = [self.psA, self.psB] if u % 2 == 0 else [self.psC, self.psD]
            self.wait(V, "T", mm_done.get(u - 2, 0))
            iv = self.inc(V.tensor_tensor(sq[0:64, :], ten[0:64, :],
                                          ten[0:64, :], ALU.mult), "V")
            self.wait(TE, "V", iv)
            self.wait(TE, "Sc", ln_done.get(u - 2, 0))
            mm = None
            for ch in range(nch):
                bp = (ch // 2) * 32
                mm = TE.matmul(pb[ch % 2][bp:bp + 1, 0:512], self.ones65_bf[:],
                               sq[:, ch * 512:(ch + 1) * 512],
                               start=True, stop=True)
            mm_done[u] = self.inc(mm, "T")
            self.wait(SC, "T", mm_done[u])
            self.wait(SC, "V", fin_done.get(u - 2, 0))
            last = None
            for ch in range(nch):
                bp = (ch // 2) * 32
                last = SC.activation(nac[0:1, ch * 512:(ch + 1) * 512],
                                     pb[ch % 2][bp:bp + 1, 0:512], AF.Ln)
            if which == "k":
                ln_done[u] = self.inc(last, "Sc")
                self.wait(V, "Sc", ln_done[u])
                V.reduce_max(self.kmax_row[0:1, a:a + 1], nac[:], AX.X)
                V.drain()
                lkm_done[a] = self.inc(V.tensor_scalar_mul(
                    self.kmax_row[0:1, a:a + 1],
                    self.kmax_row[0:1, a:a + 1], 0.5), "V")
            else:
                SC.drain()
                self.wait(SC, "V", lkm_done[a])
                ln_done[u] = self.inc(SC.activation(
                    ten[64:65, :], nac[:], AF.Exp, scale=0.5,
                    bias=self.kmax_row[0:1, a:a + 1]), "Sc")
                self.wait(V, "Sc", ln_done[u])
                V.memset(self.Ks[a][64:65, :], -1.0)
                self.inc(V.memset(self.vmax8[0:1, 1:2], 0.0), "V")
            fin_done[u] = self.cnt["V"]
        self.wait(TE, "V")
        self.wait(TE, "Sc")
        gp = self.nc.gpsimd
        if "d_qs" in self.dbg:
            self.wait(gp, "V")
            idm = None
            for a in range(NA):
                self.inc(gp.dma_start(out=self.dbg["d_qs"].ap()[b, a],
                                      in_=self.Qs[a][:]), "dW")
                idm = self.inc(gp.dma_start(out=self.dbg["d_ks"].ap()[b, a],
                                            in_=self.Ks[a][:]), "dW")
            gp.wait_ge(self.sem["dW"], idm)
        if "d_vs" in self.dbg:
            self.wait(gp, "V")
            idm = None
            for t in range(self.NT):
                idm = self.inc(gp.dma_start(
                    out=self.dbg["d_vs"].ap()[b, t * 128:(t + 1) * 128, :],
                    in_=self.Vs[:, t, :]), "dW")
            gp.wait_ge(self.sem["dW"], idm)

    def _phase45(self, b, last=False):
        nc = self.nc
        sy = nc.sync
        TE, SC, V = nc.tensor, nc.scalar, nc.vector
        S, NT = self.S, self.NT
        NQC = S // 1024
        dO_tile = {}
        self.wait(TE, "dW")
        for qc in range(min(NQC, self.p45_qc)):
            otn_done = {}
            for a in range(NA):
                exp_done = {}
                pv_done = {}
                psO_pair = [[self.psE, self.psF], [self.psG, self.psH]][a % 2]
                if a < 2:
                    self.wait(TE, "V")
                sbank = [self.psA, self.psB, self.psC, self.psD]
                for kt in range(NT + 1):
                    if kt < NT:
                        for hf in range(2):
                            ei = 2 * kt + hf
                            psS = sbank[ei % 4]
                            self.wait(TE, "Sc", exp_done.get(
                                ei - 4, getattr(self, "_rep_sc", 0)))
                            mm = TE.matmul(
                                psS[:],
                                self.Ks[a][:, kt * 128:(kt + 1) * 128],
                                self.Qs[a][:, qc * 1024 + hf * 512:
                                           qc * 1024 + (hf + 1) * 512],
                                start=True, stop=True)
                            it = self.inc(mm, "T")
                            self.wait(SC, "T", it)
                            self.wait(SC, "T", pv_done.get(kt - 2, 0))
                            isc = self.inc(SC.activation(
                                self.Pt[kt % 2][:, hf * 512:(hf + 1) * 512],
                                psS[:], AF.Exp), "Sc")
                            exp_done[ei] = isc
                    kp = kt - 1
                    if kp >= 0:
                        self.wait(TE, "Sc", exp_done[2 * kp + 1])
                        if kp == 0 and a >= 2:
                            self.wait(TE, "V", otn_done.get(a - 2, 0))
                        mm2 = None
                        for hf in range(2):
                            psO = psO_pair[hf]
                            mm2 = TE.matmul(
                                psO[0:65, :],
                                self.Vs[:, kp, a * 65:(a + 1) * 65],
                                self.Pt[kp % 2][:, hf * 512:(hf + 1) * 512],
                                start=(kp == 0), stop=(kp == NT - 1),
                                skip_group_check=True)
                        pv_done[kp] = self.inc(mm2, "T")
                self.wait(SC, "T")
                with nc.allow_low_precision(reason="f32r operand prep"):
                    for hf in range(2):
                        psO = psO_pair[hf]
                        SC.activation(self.nrm_f[0:1, hf * 512:(hf + 1) * 512],
                                      psO[64:65, :], AF.Ln)
                    SC.drain()
                    isc = self.inc(SC.activation(self.rec_row[:],
                                                 self.nrm_f[:], AF.Exp,
                                                 scale=-1.0), "Sc")
                self.wait(TE, "Sc", isc)
                psR_pair = [self.psC, self.psD]
                last_mm = None
                for hf in range(2):
                    last_mm = TE.matmul(psR_pair[hf][0:64, :], self.onesr64_r[:],
                                        self.rec_row[0:1, hf * 512:(hf + 1) * 512],
                                        start=True, stop=True)
                it = self.inc(last_mm, "T")
                self.wait(SC, "T", it)
                self.wait(SC, "V", otn_done.get(a - 1, 0))
                isc = None
                for hf in range(2):
                    isc = self.inc(SC.activation(
                        self.rep_sb[:, hf * 512:(hf + 1) * 512],
                        psR_pair[hf][0:64, :], AF.Copy), "Sc")
                self._rep_sc = isc
                self.wait(V, "Sc", isc)
                self.wait(V, "T", pv_done[NT - 1])
                last_v = None
                for hf in range(2):
                    psO = psO_pair[hf]
                    with nc.allow_low_precision(reason="f32r operand prep"):
                        last_v = V.tensor_tensor(
                            self.OTn[0:64, a, hf * 512:(hf + 1) * 512],
                            psO[0:64, :],
                            self.rep_sb[:, hf * 512:(hf + 1) * 512], ALU.mult)
                otn_done[a] = self.inc(last_v, "V")
                if "d_ot" in self.dbg:
                    gp = nc.gpsimd
                    self.wait(gp, "V")
                    idm = self.inc(gp.dma_start(
                        out=self.dbg["d_ot"].ap()[b, a, :,
                                                  qc * 1024:(qc + 1) * 1024],
                        in_=self.OTn[0:64, a, :]), "dW")
                    gp.wait_ge(self.sem["dW"], idm)
                    self.wait(V, "dW", idm)
            self.wait(TE, "V")

            if self.p45_level < 2:
                continue
            res_done = {}
            v_at_start = self.cnt["V"]
            for dt_ in range(8):
                t = qc * 8 + dt_
                opair = [[self.psA, self.psB], [self.psC, self.psD]][dt_ % 2]
                self.wait(TE, "V", res_done.get(dt_ - 2, v_at_start))
                for half in range(2):
                    ps = opair[half]
                    for a in range(NA):
                        TE.matmul(ps[:],
                                  self.OTn[0:64, a, dt_ * 128:(dt_ + 1) * 128],
                                  self.wo2r[:, a, half * 512:(half + 1) * 512],
                                  start=(a == 0), stop=False)
                    mm = TE.matmul(ps[:], self.ones2_bf[:],
                                   self.bo2_hl[:, half * 512:(half + 1) * 512],
                                   start=False, stop=True)
                    self.inc(mm, "T")
                so = "dO%d" % (t % 2)
                self.wait(V, so, dO_tile.get(t - 2, 0))
                self.wait(V, "T")
                for half in range(2):
                    psO = opair[half]
                    V.scalar_tensor_tensor(
                        self.x1[:, t, half * 512:(half + 1) * 512], psO[:], 0.0,
                        self.x1[:, t, half * 512:(half + 1) * 512],
                        ALU.add, ALU.add,
                        accum_out=self.sums[:, half:half + 1])
                self.inc(V.memset(self.sums[:, 8:9], 0.0), "V")
                V.drain()
                self._ln_finalize(self.x1[:, t, :], self.sums[:, 0:1],
                                  self.sums[:, 1:2], self.x1[:, t, :])
                res_done[dt_] = self.cnt["V"]
                self.wait(sy, "V")
                dO_tile[t] = self.inc(
                    sy.dma_start(out=self.out_h.ap()[b, t * 128:(t + 1) * 128, :],
                                 in_=self.x1[:, t, :]), so)
        if not last:
            self._wload_b1(first=False)


def _pack_inputs(inputs):
    g = lambda n: np.ascontiguousarray(np.asarray(inputs[n], np.float32))
    m = {}
    m["a_wq"] = g("L0_attn_wq").reshape(D, D).copy()
    m["a_wk"] = g("L0_attn_wk").reshape(D, D).copy()
    m["a_wv"] = g("L0_attn_wv").reshape(D, D).copy()
    m["a_wo"] = g("L0_attn_wo").reshape(D, D).copy()
    m["a_bq"] = g("L0_attn_bq").reshape(D).copy()
    m["a_bk"] = g("L0_attn_bk").reshape(D).copy()
    m["a_bv"] = g("L0_attn_bv").reshape(D).copy()
    m["a_bo"] = g("L0_attn_bo").reshape(D).copy()
    m["f_wq"] = g("L0_ffn_Wq")
    m["f_wk"] = g("L0_ffn_Wk")
    m["f_wv"] = g("L0_ffn_Wv")
    m["f_wo"] = g("L0_ffn_Wo")
    m["f_bq"] = g("L0_ffn_bq")
    m["f_bk"] = g("L0_ffn_bk")
    m["f_bv"] = g("L0_ffn_bv")
    m["f_bo"] = g("L0_ffn_bo")
    m["f_wr"] = g("L0_ffn_Wr")
    m["f_br"] = g("L0_ffn_br")
    return m


def kernel(**inputs):
    x = np.ascontiguousarray(np.asarray(inputs["L0_attn_x"], np.float32))
    B, S, _ = x.shape
    nb = B // N_CORES
    prog = Prog(S=S, NB=nb)
    wm = _pack_inputs(inputs)
    in_maps = []
    for core in range(N_CORES):
        im = dict(wm)
        im["x"] = np.ascontiguousarray(x[core * nb:(core + 1) * nb])
        in_maps.append(im)
    res = run_bass_kernel_spmd(prog.nc, in_maps, core_ids=list(range(N_CORES)))
    out = np.concatenate([r["out"] for r in res.results], axis=0)
    return out.astype(np.float32)


# version 17 (final)
# speedup vs baseline: 1.0955x; 1.0304x over previous
import numpy as np

import concourse.bass as bass
import concourse.mybir as mybir
from concourse.bass_utils import run_bass_kernel_spmd

F32 = mybir.dt.float32
F32R = mybir.dt.float32r
BF16 = mybir.dt.bfloat16
AF = mybir.ActivationFunctionType
ALU = mybir.AluOpType
AX = mybir.AxisListType

D = 1024
H = 16
HD = 64
NA = 4
EPS_ATTN = 1e-6
EPS_LN = 1e-5
N_CORES = 8

WNAMES = ["a_wq", "a_wk", "a_wv", "a_wo", "a_bq", "a_bk", "a_bv", "a_bo",
          "f_wq", "f_wk", "f_wv", "f_wo", "f_bq", "f_bk", "f_bv", "f_bo",
          "f_wr", "f_br"]


def _ap(t, offset, dims):
    return bass.AP(tensor=t[:].tensor, offset=offset, ap=[list(d) for d in dims])


class Prog:
    def __init__(self, S=2048, NB=2, debug=(), stop_after=None, p2_level=9,
                 p2y_level=9, p45_level=9, p45_qc=99):
        assert S % 1024 == 0
        self.stop_after = stop_after
        self.p2_level = p2_level
        self.p2y_level = p2y_level
        self.p45_level = p45_level
        self.p45_qc = p45_qc
        self.S, self.NB = S, NB
        self.NT = S // 128
        self.NG = self.NT // 4
        self.debug = set(debug)
        nc = bass.Bass("TRN2", target_bir_lowering=False)
        self.nc = nc

        dt = nc.dram_tensor
        self.x_h = dt("x", [NB, S, D], F32, kind="ExternalInput")
        self.out_h = dt("out", [NB, S, D], F32, kind="ExternalOutput")
        shp = {"a_wq": [D, D], "a_wk": [D, D], "a_wv": [D, D], "a_wo": [D, D],
               "a_bq": [D], "a_bk": [D], "a_bv": [D], "a_bo": [D],
               "f_wq": [D, D], "f_wk": [D, D], "f_wv": [D, D],
               "f_wo": [NA * HD, D], "f_bq": [D], "f_bk": [D], "f_bv": [D],
               "f_bo": [D], "f_wr": [D, H], "f_br": [H]}
        self.w = {n: dt(n, shp[n], F32, kind="ExternalInput") for n in WNAMES}
        self.wo1r_h = dt("wo1r", [D, D], F32R, kind="Internal")
        self.dbg = {}
        for name, s in [("d_qm", [NB, 128, 8, 512]), ("d_ynt", [NB, 128, 8, 512]),
                        ("d_xg", [NB, 8, D]), ("d_km", [NB, 128, D]),
                        ("d_vaug", [NB, 128, H * 65]), ("d_xsum", [NB, D]),
                        ("d_kv", [NB, 128, 16 * 65]),
                        ("d_x1", [NB, S, D]), ("d_r", [NB, H]),
                        ("d_sel", [NB, H, NA]),
                        ("d_qs", [NB, NA, 65, S]), ("d_ks", [NB, NA, 65, S]),
                        ("d_vs", [NB, S, NA * 65]), ("d_ot", [NB, NA, 64, S])]:
            if name in self.debug:
                self.dbg[name] = dt(name, s, F32, kind="ExternalOutput")

        it = nc.inline_tensor
        self.c_id = it(np.eye(128, dtype=np.float32), "c_id")
        self.c_ones2 = it(np.ones((2, 128), np.float32), "c_ones2")
        self.c_ones64 = it(np.ones((64, 1), np.float32), "c_ones64")
        o65 = np.ones((65, 1), np.float32); o65[64, 0] = 1e-30
        self.c_ones65 = it(o65, "c_ones65")
        self.c_ones128 = it(np.ones((128, 1), np.float32), "c_ones128")
        self.c_onesr64 = it(np.ones((1, 64), np.float32), "c_onesr64")
        self.c_u16 = it(np.triu(np.ones((H, H), np.float32), 1).copy(), "c_u16")
        lpar = np.zeros((H, 128), np.float32)
        for h in range(H):
            lpar[h, (h % 2) * 64:(h % 2) * 64 + 64] = 1.0
        self.c_lpar = it(lpar, "c_lpar")
        selmask = np.zeros((H, 32), np.float32)
        for h in range(H):
            selmask[h, (h // 2) * 4:(h // 2) * 4 + 4] = 1.0
        self.c_selmask = it(selmask, "c_selmask")
        self.c_i2 = it(np.vstack([np.eye(64, dtype=np.float32)] * 2).copy(), "c_i2")

        self._tr_rot = 0
        self._sbuf()
        self._sems()
        self._emit()

    def _sbuf(self):
        nc = self.nc
        S, NT = self.S, self.NT
        a = nc.alloc_sbuf_tensor

        def at(name, shape, dtp, off):
            return nc.alloc_sbuf_tensor_at(name, shape, dtp, offset=off)

        self.id_f = a("id_f", [128, 128], F32)
        self.id_bf = a("id_bf", [128, 128], BF16)
        self.ones2_bf = a("ones2_bf", [2, 128], BF16)
        self.ones64_bf = a("ones64_bf", [64, 1], BF16)
        self.ones65_bf = a("ones65_bf", [65, 1], BF16)
        self.onesr64_r = a("onesr64_r", [1, 64], F32R)
        self.ones128_f = a("ones128_f", [128, 1], F32)
        self.lpar_f = a("lpar_f", [H, 128], F32)
        self.selmask_f = a("selmask_f", [H, 32], F32)
        self.selp = a("selp", [H, 32], F32)
        self.i2_f = a("i2_f", [128, 64], F32)
        self.u16_f = a("u16_f", [H, H], F32)

        self.x1 = a("x1", [128, NT, D], F32)
        self.Vs = a("Vs", [128, NT, NA * 65], BF16)
        self.kv_z = a("kv_z", [128, H, 65], BF16)
        self.esc = a("esc", [128, 512], F32)
        self.esc2 = a("esc2", [128, 512], F32)
        self.sums = a("sums", [128, 10], F32)
        self.stat = a("stat", [128, 6], F32)
        self.wr_f = a("wr_f", [128, 8, H], F32)
        self.xsum = a("xsum", [1, D], F32)
        self.xsum_fm = a("xsum_fm", [128, 8], F32)
        self.nrm_f = a("nrm_f", [1, D], F32)

        self.bq1_fm = a("bq1_fm", [128, 8], F32)
        self.bq2_fm = a("bq2_fm", [128, 8], F32)
        self.bk2_fm = a("bk2_fm", [128, 8], F32)
        self.bv2_fm = a("bv2_fm", [128, 8], F32)
        self.bk1_hl = a("bk1_hl", [2, D], BF16)
        self.bv1_hl = a("bv1_hl", [2, D], BF16)
        self.bo1_hl = a("bo1_hl", [2, D], BF16)
        self.bo2_hl = a("bo2_hl", [2, D], BF16)
        self.br_row = a("br_row", [1, H], F32)

        self.r_row = a("r_row", [1, H], F32)
        self.z16 = a("z16", [128, H], F32)
        self.epsc = a("epsc", [128, 1], F32)
        self.vmax8 = a("vmax8", [1, 8], F32)
        self.active = a("active", [1, H], F32)
        self.activeT = a("activeT", [H, 1], F32)
        self.sel = a("sel", [H, NA], F32)
        self.kmax_row = a("kmax_row", [1, NA], F32)

        rqk = nc.bump_sbuf(NA * 2 * S * 2)[0]
        self.Qs = [at(f"Qs{i}", [65, S], BF16, rqk + i * (S * 2)) for i in range(NA)]
        self.Ks = [at(f"Ks{i}", [65, S], BF16, rqk + (NA + i) * (S * 2))
                   for i in range(NA)]

        ph = nc.bump_sbuf(78336)[0]
        self.ph = ph
        self.wk1 = at("wk1", [128, 8, D], BF16, ph + 0)
        self.wv1 = at("wv1", [128, 8, D], BF16, ph + 16384)
        self.xin = [at(f"xin{i}", [128, D], F32, ph + 32768 + i * 4096)
                    for i in range(2)]
        self.xin_bf = [at(f"xinb{i}", [128, D], BF16, ph + 32768 + i * 2048)
                       for i in range(4)]
        self.km = [at(f"km{i}", [128, D], BF16,
                      ph + (40960 if i == 0 else 65664)) for i in range(2)]
        self.vaug = [at(f"vaug{i}", [128, H * 65], BF16,
                        ph + (43008 if i == 0 else 67712)) for i in range(2)]
        self.xt1 = [at(f"xt1_{i}", [128, 8, 128], BF16, ph + 45184 + i * 2048)
                    for i in range(2)]
        self.wq1 = at("wq1", [128, 8, D], BF16, ph + 49280)
        self.xTg = at("xTg", [128, 8, 512], BF16, ph + 0)
        self.qm = at("qm", [128, 8, 512], BF16, ph + 8192)
        self.ynTg = at("ynTg", [128, 8, 512], F32R, ph + 16384)
        self.wo1_ring = [at(f"wo1g{i}", [128, 512], F32R,
                            ph + (32768 + i * 2048 if i < 4 else
                                  69824 + (i - 4) * 2048)) for i in range(8)]
        self.ynorm = [at(f"ynorm{i}", [128, D], F32,
                         ph + (40960 if i == 0 else 45056)) for i in range(2)]
        self.wq2 = at("wq2", [128, 8, D], BF16, ph + 0)
        self.wk2 = at("wk2", [128, 8, D], BF16, ph + 16384)
        self.wv2 = at("wv2", [128, 8, D], BF16, ph + 32768)
        self.x1T = at("x1T", [128, 8, 512], BF16, ph + 49152)
        self.qkv_full = at("qkv_full", [128, 8, 512], BF16, ph + 57344)
        self.shiftb = at("shiftb", [128, 512], BF16, ph + 65536)
        self.vsfm = at("vsfm", [128, 512], BF16, ph + 66560)
        self.G = at("G", [128, 8, 256], BF16, ph + 67584)
        self.sqb = [at(f"sqb{i}", [65, S], BF16, ph + i * 4096)
                    for i in range(2)]
        self.nac2 = [at(f"nac2_{i}", [1, S], F32, ph + 8192 + i * 8192)
                     for i in range(2)]
        self.OTn = at("OTn", [64, NA, D], F32R, ph + 0)
        self.Pt = [at(f"Pt{i}", [128, D], BF16, ph + 16384 + i * 2048)
                   for i in range(2)]
        self.rep_sb = at("rep_sb", [64, D], F32, ph + 20480)
        self.rec_row = at("rec_row", [1, D], F32R, ph + 24576)
        self.wo2r = at("wo2r", [64, NA, D], F32R, ph + 28672)
        self.wk1n = at("wk1n", [128, 8, D], BF16, ph + 45056)
        self.wv1n = at("wv1n", [128, 8, D], BF16, ph + 61440)
        self.brow = at("brow", [1, D], F32, ph + 0)
        self.blo_f = at("blo_f", [1, D], F32, ph + 4096)

        p = nc.alloc_psum_tensor
        (self.psA, self.psB, self.psC, self.psD, self.psE, self.psF,
         self.psG, self.psH) = [p(f"ps{i}", [128, 512], F32) for i in range(8)]

    def _sems(self):
        self.sem, self.cnt = {}, {}
        for n in ["dW", "dX", "dX0", "dX1", "dX2", "dX3",
                  "dB0", "dB1", "dB2",
                  "dG0", "dG1", "dG2", "dG3", "dO0", "dO1",
                  "dR0", "dR1", "dR2", "dR3", "dR4", "dR5", "dR6", "dR7",
                  "T", "V", "Sc"]:
            self.sem[n] = self.nc.alloc_semaphore(n)
            self.cnt[n] = 0

    def inc(self, instr, name):
        amt = 16 if name.startswith("d") else 1
        instr.then_inc(self.sem[name], amt)
        self.cnt[name] += amt
        return self.cnt[name]

    def wait(self, eng, name, val=None):
        eng.wait_ge(self.sem[name], self.cnt[name] if val is None else val)

    def _emit(self):
        nc = self.nc
        sa = self.stop_after
        with nc.Block():
            self._startup()
            nc.all_engine_barrier()
            self._wload_b1(first=True)
            self._w1_done = self.cnt["dW"]
            nc.all_engine_barrier()
            for b in range(self.NB):
                if sa == "startup":
                    break
                self._phase1(b)
                nc.all_engine_barrier()
                if sa == "p1":
                    break
                self._phase2(b)
                nc.all_engine_barrier()
                if sa == "p2":
                    break
                self._router(b)
                self._wload_b2()
                nc.all_engine_barrier()
                if sa == "router":
                    break
                self._phase3(b)
                nc.all_engine_barrier()
                if sa == "p3":
                    break
                self._wload_wo2()
                self._bounds(b)
                nc.all_engine_barrier()
                if sa == "bounds":
                    break
                self._phase45(b, last=(b == self.NB - 1))
                nc.all_engine_barrier()
                if b + 1 < self.NB:
                    gp = nc.gpsimd
                    self.inc(gp.dma_start(out=self.wk1[:], in_=self.wk1n[:]), "dW")
                    self.inc(gp.dma_start(out=self.wv1[:], in_=self.wv1n[:]), "dW")
                    gp.wait_ge(self.sem["dW"], self.cnt["dW"])
                    self._wload_wq1()
                    self._w1_done = self.cnt["dW"]
                    nc.all_engine_barrier()
            for s in ["dX", "dX0", "dX1", "dX2", "dX3", "dO0", "dO1",
                      "dR0", "dR1", "dR2",
                      "dR3", "dR4", "dR5", "dR6", "dR7"]:
                self.wait(nc.sync, s)
            for s in ["dG0", "dG1", "dG2", "dG3", "dB0", "dB1", "dB2"]:
                self.wait(nc.sync, s)
            self.wait(nc.gpsimd, "dW")

    def _startup(self):
        nc = self.nc
        sy, gp, V = nc.sync, nc.gpsimd, nc.vector
        w = self.w
        for dst, src in [(self.id_f, self.c_id), (self.ones128_f, self.c_ones128),
                         (self.lpar_f, self.c_lpar),
                         (self.selmask_f, self.c_selmask),
                         (self.i2_f, self.c_i2), (self.u16_f, self.c_u16)]:
            self.inc(sy.dma_start(out=dst[:], in_=src.ap()), "dX")
        for dst, src in [(self.id_bf, self.c_id), (self.ones2_bf, self.c_ones2),
                         (self.ones64_bf, self.c_ones64),
                         (self.ones65_bf, self.c_ones65),
                         (self.onesr64_r, self.c_onesr64)]:
            self.inc(gp.dma_start(out=dst[:], in_=src.ap()), "dW")
        self.inc(gp.dma_start(out=self.wo1r_h.ap(), in_=w["a_wo"].ap()), "dW")
        self.inc(sy.dma_start(
            out=self.wr_f[:],
            in_=w["f_wr"].ap().rearrange("(p c) h -> p c h", p=128)), "dX")
        for src, dst in [("a_bq", self.bq1_fm), ("f_bq", self.bq2_fm),
                         ("f_bk", self.bk2_fm), ("f_bv", self.bv2_fm)]:
            self.inc(sy.dma_start(
                out=dst[:], in_=w[src].ap().rearrange("(c p) -> p c", p=128),
                allow_slow_non_contiguous=True), "dX")
        self.inc(sy.dma_start(out=self.br_row[:], in_=w["f_br"].ap()[None, :]),
                 "dX")
        for src, dst in [("a_bk", self.bk1_hl), ("a_bv", self.bv1_hl),
                         ("a_bo", self.bo1_hl), ("f_bo", self.bo2_hl)]:
            self.inc(sy.dma_start(out=self.brow[:], in_=w[src].ap()[None, :]),
                     "dX")
            self.wait(V, "dX")
            self.wait(V, "dW")
            V.tensor_copy(dst[0:1, :], self.brow[:])
            V.drain()
            iv = self.inc(V.tensor_sub(self.blo_f[:], self.brow[:], dst[0:1, :]),
                          "V")
            self.wait(gp, "V", iv)
            self.inc(gp.dma_start(out=dst[1:2, :], in_=self.blo_f[:]), "dW")
            sy.wait_ge(self.sem["V"], iv)
        self.wait(V, "dW")
        V.memset(_ap(self.Vs, 64, [[self.NT * NA * 65, 128], [65, self.NT * NA]]),
                 1.0)
        V.memset(self.epsc[:], EPS_LN)
        V.memset(self.sums[:], 0.0)
        self.inc(V.memset(self.stat[:], 0.0), "V")

    def _wload_b1(self, first=False):
        gp = self.nc.gpsimd
        for src, dst in [("a_wk", self.wk1 if first else self.wk1n),
                         ("a_wv", self.wv1 if first else self.wv1n)]:
            self.inc(gp.dma_start(
                out=dst[:],
                in_=self.w[src].ap().rearrange("(c p) f -> p c f", p=128)), "dW")
        if first:
            self._wload_wq1()

    def _wload_wq1(self):
        gp = self.nc.gpsimd
        self.inc(gp.dma_start(
            out=self.wq1[:],
            in_=self.w["a_wq"].ap().rearrange("(c p) f -> p c f", p=128)), "dW")

    def _wload_b2(self):
        gp = self.nc.gpsimd
        self._w2_done = {}
        for src, dst, kind, sname in [("f_wq", self.wq2, "q", "dB0"),
                                      ("f_wk", self.wk2, "k", "dB1"),
                                      ("f_wv", self.wv2, "v", "dB2")]:
            self.wait(gp, sname)
            cnt = self.inc(gp.dma_start(
                out=dst[:],
                in_=self.w[src].ap().rearrange("(c p) f -> p c f", p=128)), sname)
            self._w2_done[kind] = (sname, cnt)

    def _wload_wo2(self):
        gp = self.nc.gpsimd
        self.inc(gp.dma_start(
            out=self.wo2r[:],
            in_=self.w["f_wo"].ap().rearrange("(a d) f -> d a f", d=HD)), "dW")

    def _tr8(self, src_f32, evac_list, banks=None):
        nc = self.nc
        TE = nc.tensor
        if banks is None:
            banks = [self.psG, self.psH]
        if not hasattr(self, "_tr_evac"):
            self._tr_evac = {}
        for g in range(2):
            bi = (self._tr_rot + g) % len(banks)
            ps = banks[bi]
            esem, ecnt = self._tr_evac.get(id(ps), ("Sc", 0))
            self.wait(TE, esem, ecnt)
            if not hasattr(self, "_ps_vfree"):
                self._ps_vfree = {}
            self.wait(TE, "V", self._ps_vfree.get(id(ps), 0))
            last = None
            for c4 in range(4):
                c = 4 * g + c4
                last = TE.transpose(ps[:, c4 * 128:(c4 + 1) * 128],
                                    src_f32[:, c * 128:(c + 1) * 128],
                                    self.id_f[:])
            it = self.inc(last, "T")
            self.nc.scalar.wait_ge(self.sem["T"], it)
            self._tr_evac[id(ps)] = evac_list(g, ps)
        self._tr_rot = (self._tr_rot + 2) % len(banks)

    def _phase1(self, b):
        nc = self.nc
        sy = nc.sync
        TE, SC, V = nc.tensor, nc.scalar, nc.vector
        NT = self.NT
        self.wait(TE, "dW", self._w1_done)
        V.memset(_ap(self.vaug[0], 64, [[H * 65, 128], [65, H]]), 1.0)
        V.memset(_ap(self.vaug[1], 64, [[H * 65, 128], [65, H]]), 1.0)
        self.inc(V.memset(self.sums[:, 7:8], 0.0), "V")
        v_done = self.cnt["V"]
        self._p1_vdone = {}
        gp = nc.gpsimd
        for t in range(NT):
            if t % 2 == 0:
                eng, sx = sy, "dX%d" % ((t // 2) % 4)
            else:
                eng, sx = gp, "dG%d" % ((t // 2) % 4)
            self.wait(eng, sx)
            self.wait(eng, "dO%d" % (t % 2))
            self.inc(eng.dma_start(
                out=self.x1[:, t, :],
                in_=self.x_h.ap()[b, t * 128:(t + 1) * 128, :]), sx)
            self.wait(TE, sx)
            self.wait(TE, "V", v_done)
            xt = self.xt1[t % 2]

            def evac(gg, ps, xt=xt):
                out = _ap(xt, 4 * gg * 128, [[1024, 128], [1, 512]])
                if gg % 2 == 0:
                    return ("Sc", self.inc(SC.activation(out, ps[:], AF.Copy),
                                           "Sc"))
                self.wait(V, "T", self.cnt["T"])
                return ("V", self.inc(V.tensor_copy(out, ps[:]), "V"))

            self._tr8(self.x1[:, t, :], evac)
            self.wait(TE, "Sc")
            self.wait(TE, "V")
            tmark = {}
            for ti, (wsb, bhl) in enumerate([(self.wk1, self.bk1_hl),
                                             (self.wv1, self.bv1_hl)]):
                for half in range(2):
                    ps = [self.psA, self.psB, self.psC, self.psD][2 * ti + half]
                    for c in range(8):
                        TE.matmul(ps[:], xt[:, c, :],
                                  wsb[:, c, half * 512:(half + 1) * 512],
                                  start=(c == 0), stop=False)
                    mm = TE.matmul(ps[:], self.ones2_bf[:],
                                   bhl[:, half * 512:(half + 1) * 512],
                                   start=False, stop=True)
                    tmark[(ti, half)] = self.inc(mm, "T")
            for half in range(2):
                ps = [self.psA, self.psB][half]
                self.wait(V, "T", tmark[(0, half)])
                iv = self.inc(V.tensor_scalar_min(self.esc[:], ps[:], 0.0),
                              "V")
                self.wait(SC, "V", iv)
                isc = self.inc(SC.activation(self.esc2[:], self.esc[:], AF.Exp),
                               "Sc")
                self.wait(V, "Sc", isc)
                st = V.scalar_tensor_tensor(
                    self.km[t % 2][:, half * 512:(half + 1) * 512],
                    ps[:], 0.0, self.esc2[:], ALU.max, ALU.add)
                self.inc(st, "V")
            for half in range(2):
                ps = [self.psC, self.psD][half]
                self.wait(V, "T", tmark[(1, half)])
                out = _ap(self.vaug[t % 2], half * 8 * 65,
                          [[H * 65, 128], [65, 8], [1, 64]])
                self.inc(V.tensor_copy(out, ps[:]), "V")
            v_done = self.cnt["V"]
            self._p1_vdone[t] = v_done
            if t == 0 and "d_km" in self.dbg:
                gp = nc.gpsimd
                self.wait(gp, "V", v_done)
                self.inc(gp.dma_start(out=self.dbg["d_km"].ap()[b],
                                      in_=self.km[t % 2][:]), "dW")
                idm = self.inc(gp.dma_start(out=self.dbg["d_vaug"].ap()[b],
                                            in_=self.vaug[t % 2][:]), "dW")
                gp.wait_ge(self.sem["dW"], idm)
                self.wait(V, "dW", idm)
            if t > 0:
                self._p1_kv_accum(t - 1, NT)
        self._p1_kv_accum(NT - 1, NT)
        self.wait(V, "T")
        V.memset(self.kv_z[:], 0.0)
        V.drain()
        for bi, bank in enumerate([self.psE, self.psF]):
            oute = _ap(self.kv_z, (2 * 4 * bi) * 65,
                       [[H * 65, 64], [2 * 65, 4], [1, 65]])
            srce = _ap(bank, 0, [[512, 64], [128, 4], [1, 65]])
            V.tensor_copy(oute, srce)
            V.drain()
            oto = _ap(self.kv_z, (2 * 4 * bi + 1) * 65 + 64 * H * 65,
                      [[H * 65, 64], [2 * 65, 4], [1, 65]])
            srco = _ap(bank, 64 * 512, [[512, 64], [128, 4], [1, 65]])
            self.inc(V.tensor_copy(oto, srco), "V")
            V.drain()
        if "d_kv" in self.dbg:
            gp = nc.gpsimd
            self.wait(gp, "V")
            idm = self.inc(gp.dma_start(
                out=self.dbg["d_kv"].ap()[b],
                in_=self.kv_z[:].rearrange("p c k -> p (c k)")), "dW")
            gp.wait_ge(self.sem["dW"], idm)

    def _p1_kv_accum(self, t, NT):
        TE = self.nc.tensor
        self.wait(TE, "V", self._p1_vdone[t])
        last = None
        for h in range(H):
            bank = self.psE if h < 8 else self.psF
            slot = (h % 8) // 2
            bp = (h % 2) * 64
            last = TE.matmul(bank[bp:bp + 64, slot * 128:slot * 128 + 65],
                             self.km[t % 2][:, h * 64:(h + 1) * 64],
                             _ap(self.vaug[t % 2], h * 65,
                                 [[H * 65, 128], [1, 65]]),
                             start=(t == 0 and h % 8 < 2),
                             stop=(t == NT - 1), skip_group_check=True)
        self.inc(last, "T")

    def _phase2(self, b):
        nc = self.nc
        sy = nc.sync
        TE, SC, V = nc.tensor, nc.scalar, nc.vector
        NG = self.NG
        self.wait(TE, "dW", self._w1_done)
        self.qm_evac = {}
        self._ring_used = {}
        self._xsum_pending = None
        self._ynorm_v = {}
        self._ynT_done = {}
        V.memset(self.xsum[:], 0.0)
        self.inc(V.memset(self.sums[:], 0.0), "V")
        ring_cnt = 0
        for g in range(NG):
            for dt_ in range(4):
                t = g * 4 + dt_

                def evac(gg, ps, dt_=dt_):
                    out = _ap(self.xTg, (4 * gg) * 512 + dt_ * 128,
                              [[8 * 512, 128], [512, 4], [1, 128]])
                    if gg % 2 == 0:
                        return ("Sc", self.inc(
                            SC.activation(out, ps[:], AF.Copy), "Sc"))
                    self.wait(V, "T", self.cnt["T"])
                    return ("V", self.inc(V.tensor_copy(out, ps[:]), "V"))

                self._tr8(self.x1[:, t, :], evac)
            self.wait(TE, "Sc")
            self.wait(TE, "V")
            self._emit_xsum(b)
            if self.p2_level < 2:
                continue
            for hc in range(8):
                ps = [self.psA, self.psB, self.psC, self.psD][hc % 4]
                if hc >= 4:
                    self.wait(TE, "V", self.qm_evac[hc - 4])
                mm = None
                for c in range(8):
                    mm = TE.matmul(ps[:], self.wq1[:, c, hc * 128:(hc + 1) * 128],
                                   self.xTg[:, c, :], start=(c == 0), stop=(c == 7))
                it = self.inc(mm, "T")
                self.wait(SC, "T", it)
                self.wait(SC, "V", self.qm_evac.get(hc - 1, 0))
                isc = self.inc(SC.activation(self.esc2[:], ps[:], AF.Relu,
                                             bias=self.bq1_fm[:, hc:hc + 1]), "Sc")
                self.wait(V, "Sc", isc)
                iv = self.inc(V.tensor_scalar(
                    self.esc[:], ps[:], self.bq1_fm[:, hc:hc + 1], 0.0,
                    ALU.add, ALU.min), "V")
                self.wait(SC, "V", iv)
                isc = self.inc(SC.activation(self.esc[:], self.esc[:], AF.Exp),
                               "Sc")
                self.wait(V, "Sc", isc)
                self.qm_evac[hc] = self.inc(
                    V.tensor_add(self.qm[:, hc, :], self.esc[:], self.esc2[:]),
                    "V")
            self.wait(TE, "V")
            if self.p2_level < 3:
                continue
            for dt_ in range(4):
                self._p2_y_tile(dt_)
                if dt_ > 0:
                    self._p2_ynt(dt_ - 1)
            self._p2_ynt(3)
            if self.p2_level < 4:
                continue
            if g == 0 and "d_qm" in self.dbg:
                gpq = nc.gpsimd
                self.wait(gpq, "V")
                self.wait(gpq, "Sc")
                self.inc(gpq.dma_start(out=self.dbg["d_qm"].ap()[b],
                                       in_=self.qm[:]), "dW")
                idm = self.inc(gpq.dma_start(out=self.dbg["d_ynt"].ap()[b],
                                             in_=self.ynTg[:]), "dW")
                gpq.wait_ge(self.sem["dW"], idm)
                self.wait(TE, "dW", idm)
            self.wait(TE, "Sc")
            for half in range(2):
                for c in range(8):
                    slot = ring_cnt % 8
                    sr = "dR%d" % slot
                    self.wait(sy, sr)
                    self.wait(sy, "T", self._ring_used.get(slot, 0))
                    self.inc(sy.dma_start(
                        out=self.wo1_ring[slot][:],
                        in_=self.wo1r_h.ap()[c * 128:(c + 1) * 128,
                                             half * 512:(half + 1) * 512]), sr)
                    ring_cnt += 1
                    self.wait(TE, sr)
                    if c == 0:
                        self.wait(TE, "V")
                    for dt_ in range(4):
                        psO = [self.psA, self.psB, self.psC, self.psD][dt_]
                        mm = TE.matmul(
                            psO[:],
                            _ap(self.ynTg, (dt_ % 2) * 64 + c * 512,
                                [[8 * 512, 128], [1, 64]]).bitcast(F32R)
                            if False else
                            self.ynTg[:, c, dt_ * 128:(dt_ + 1) * 128],
                            self.wo1_ring[slot][:],
                            start=(c == 0), stop=False)
                        if c == 7:
                            mm = TE.matmul(psO[:], self.ones2_bf[:],
                                           self.bo1_hl[:, half * 512:(half + 1) * 512],
                                           start=False, stop=True)
                    if c == 7:
                        self.inc(mm, "T")
                    else:
                        self.inc(mm, "T")
                    self._ring_used[slot] = self.cnt["T"]
                if self.p2_level < 5:
                    continue
                self.wait(V, "T")
                for dt_ in range(4):
                    t = g * 4 + dt_
                    psO = [self.psA, self.psB, self.psC, self.psD][dt_]
                    st = V.scalar_tensor_tensor(
                        self.x1[:, t, half * 512:(half + 1) * 512], psO[:], 0.0,
                        self.x1[:, t, half * 512:(half + 1) * 512],
                        ALU.add, ALU.add,
                        accum_out=self.sums[:, 2 * dt_ + half:2 * dt_ + half + 1])
                self.inc(V.memset(self.sums[:, 8:9], 0.0), "V")
                V.drain()
                if not hasattr(self, "_ps_vfree"):
                    self._ps_vfree = {}
                for pso in [self.psA, self.psB, self.psC, self.psD]:
                    self._ps_vfree[id(pso)] = self.cnt["V"]
            if self.p2_level < 6:
                continue
            if not hasattr(self, "_ln_v"):
                self._ln_v = {}
            for dt_ in range(4):
                t = g * 4 + dt_
                self._ln_finalize(self.x1[:, t, :],
                                  self.sums[:, 2 * dt_:2 * dt_ + 1],
                                  self.sums[:, 2 * dt_ + 1:2 * dt_ + 2],
                                  self.x1[:, t, :])
                self._ln_v[t] = self.cnt["V"]
            self._ln_done = self.cnt["V"]
            self._xsum_pending = g
            if "d_xg" in self.dbg:
                gpx = nc.gpsimd
                self.wait(gpx, "V")
                idm = self.inc(gpx.dma_start(
                    out=self.dbg["d_xg"].ap()[b, g][None, :], in_=self.xsum[:]),
                    "dW")
                gpx.wait_ge(self.sem["dW"], idm)
                self.wait(V, "dW", idm)
        self._emit_xsum(b)
        if "d_x1" in self.dbg:
            self.wait(sy, "V")
            for t in range(self.NT):
                self.inc(sy.dma_start(
                    out=self.dbg["d_x1"].ap()[b, t * 128:(t + 1) * 128, :],
                    in_=self.x1[:, t, :]), "dX")
            self.wait(sy, "dX")

    def _emit_xsum(self, b):
        if getattr(self, "_xsum_pending", None) is None:
            return
        g = self._xsum_pending
        self._xsum_pending = None
        nc = self.nc
        TE, V = nc.tensor, nc.vector
        self.wait(TE, "V", getattr(self, "_xsum_vfree", 0))
        last = None
        for dt_ in range(4):
            t = g * 4 + dt_
            self.wait(TE, "V", self._ln_v[t])
            for half in range(2):
                ps = [self.psE, self.psF][half]
                last = TE.matmul(ps[0:1, :], self.ones128_f[:],
                                 self.x1[:, t, half * 512:(half + 1) * 512],
                                 start=(dt_ == 0), stop=(dt_ == 3),
                                 skip_group_check=True)
        it = self.inc(last, "T")
        self.wait(V, "T", it)
        for half in range(2):
            ps = [self.psE, self.psF][half]
            V.tensor_add(self.xsum[0:1, half * 512:(half + 1) * 512],
                         self.xsum[0:1, half * 512:(half + 1) * 512],
                         ps[0:1, :])
        V.drain()
        self._xsum_vfree = self.inc(V.memset(self.sums[:, 8:9], 0.0), "V")

    def _p2_y_tile(self, dt_):
        nc = self.nc
        TE, SC, V = nc.tensor, nc.scalar, nc.vector
        lvl = getattr(self, "p2y_level", 9)
        if not hasattr(self, "_y_pair_free"):
            self._y_pair_free = {}
        pairs = [[self.psC, self.psD], [self.psA, self.psB]]
        marks = {}
        for hg in range(2):
            bpair = pairs[hg]
            self.wait(TE, "V", self._y_pair_free.get(hg, 0))
            last = None
            for hh in range(8):
                h = hg * 8 + hh
                bank = bpair[0] if hh < 4 else bpair[1]
                slot = hh % 4
                qm_lhs = self.qm[:, h // 2, dt_ * 128:(dt_ + 1) * 128]
                kv_rhs = self.kv_z[:, h, 0:65]
                last = TE.matmul(bank[:, slot * 128:slot * 128 + 65], qm_lhs,
                                 kv_rhs, start=True, stop=True)
            marks[hg] = self.inc(last, "T")
        banks4 = [(hg * 8 + bi * 4, pairs[hg][bi])
                  for hg in range(2) for bi in range(2)]
        self.wait(V, "T", marks[0])
        self.wait(V, "T", self._ynT_done.get(dt_ % 2, 0))
        if lvl >= 2:
            V.drain()
            for i in (0, 1):
                bank = banks4[i][1]
                V.tensor_scalar(self.z16[:, i * 4:(i + 1) * 4],
                                _ap(bank, 64, [[512, 128], [128, 4]]),
                                EPS_ATTN, None, ALU.add)
            self.wait(V, "T", marks[1])
            for i in (2, 3):
                bank = banks4[i][1]
                V.tensor_scalar(self.z16[:, i * 4:(i + 1) * 4],
                                _ap(bank, 64, [[512, 128], [128, 4]]),
                                EPS_ATTN, None, ALU.add)
            V.drain()
            V.reciprocal(self.z16[:], self.z16[:])
            V.drain()
        else:
            self.wait(V, "T", marks[1])
        iv = None
        for i, (h0, bank) in enumerate(banks4):
            dst = _ap(self.ynorm[dt_ % 2], h0 * 64,
                      [[1024, 128], [64, 4], [1, 64]])
            src = _ap(bank, 0, [[512, 128], [128, 4], [1, 64]])
            if lvl < 2:
                iv = self.inc(V.tensor_copy(dst, src), "V")
            else:
                zb = _ap(self.z16, i * 4, [[H, 128], [1, 4], [0, 64]])
                iv = self.inc(V.tensor_tensor(dst, src, zb, ALU.mult), "V")
            if i == 1:
                self._y_pair_free[0] = iv
        self._y_pair_free[1] = iv
        self._ynorm_v[dt_] = self.cnt["V"]

    def _p2_ynt(self, dt_):
        nc = self.nc
        TE, SC = nc.tensor, nc.scalar

        def evac(gg, ps, dt_=dt_):
            out = _ap(self.ynTg, (4 * gg) * 512 + dt_ * 128,
                      [[8 * 512, 128], [512, 4], [1, 128]])
            with nc.allow_low_precision(reason="f32r operand prep"):
                return ("Sc", self.inc(SC.activation(out, ps[:], AF.Copy),
                                       "Sc"))

        self.wait(TE, "V", self._ynorm_v[dt_])
        self._tr8(self.ynorm[dt_ % 2][:], evac)
        self._ynT_done[dt_ % 2] = self.cnt["T"]

    def _ln_finalize(self, xpre, s0, s1, x_out):
        nc = self.nc
        SC, V = nc.scalar, nc.vector
        for half in range(2):
            V.scalar_tensor_tensor(
                [self.esc, self.esc2][half][:],
                xpre[:, half * 512:(half + 1) * 512], 1.0,
                xpre[:, half * 512:(half + 1) * 512], ALU.mult, ALU.mult,
                accum_out=self.sums[:, 8 + half:9 + half])
        V.drain()
        V.tensor_add(self.stat[:, 0:1], s0, s1)
        V.tensor_add(self.stat[:, 1:2], self.sums[:, 8:9], self.sums[:, 9:10])
        V.drain()
        V.scalar_tensor_tensor(self.stat[:, 2:3], self.stat[:, 0:1], 1.0 / D,
                               self.stat[:, 0:1], ALU.mult, ALU.mult)
        V.drain()
        iv = self.inc(V.tensor_sub(self.stat[:, 1:2], self.stat[:, 1:2],
                                   self.stat[:, 2:3]), "V")
        self.wait(SC, "V", iv)
        SC.activation(self.stat[:, 4:5], self.stat[:, 1:2], AF.Ln,
                      scale=1.0 / D, bias=self.epsc[:, 0:1])
        SC.drain()
        isc = self.inc(SC.activation(self.stat[:, 2:3], self.stat[:, 4:5],
                                     AF.Exp, scale=-0.5), "Sc")
        self.wait(V, "Sc", isc)
        V.scalar_tensor_tensor(self.stat[:, 3:4], self.stat[:, 0:1], -1.0 / D,
                               self.stat[:, 2:3], ALU.mult, ALU.mult)
        V.drain()
        self.inc(V.tensor_scalar(x_out, xpre, self.stat[:, 2:3],
                                 self.stat[:, 3:4], ALU.mult, ALU.add), "V")

    def _router(self, b):
        nc = self.nc
        sy, gp = nc.sync, nc.gpsimd
        TE, SC, V = nc.tensor, nc.scalar, nc.vector
        self.wait(gp, "V")
        if "d_xsum" in self.dbg:
            idm = self.inc(gp.dma_start(out=self.dbg["d_xsum"].ap()[b][None, :],
                                        in_=self.xsum[:]), "dW")
            gp.wait_ge(self.sem["dW"], idm)
        self.inc(gp.dma_start(
            out=self.xsum_fm[:],
            in_=_ap(self.xsum, 0, [[1024, 1], [8, 128], [1, 8]]),
            allow_slow_non_contiguous=True), "dW")
        gp.wait_ge(self.sem["dW"], self.cnt["dW"])
        self.wait(TE, "dW")
        last = None
        for c in range(8):
            last = TE.matmul(self.psA[0:1, 0:H], self.xsum_fm[:, c:c + 1],
                             self.wr_f[:, c, :], start=(c == 0), stop=(c == 7))
        it = self.inc(last, "T")
        self.wait(V, "T", it)
        V.scalar_tensor_tensor(self.r_row[:], self.br_row[:], float(self.S),
                               self.psA[0:1, 0:H], ALU.mult, ALU.add)
        V.drain()
        V.max(out=self.vmax8[:], in_=self.r_row[:])
        V.drain()
        iv = self.inc(V.tensor_scalar(self.active[:], self.r_row[:],
                                      self.vmax8[:, NA - 1:NA], None, ALU.is_ge),
                      "V")
        V.memset(self.xsum[:], 0.0)
        self.wait(gp, "V", iv)
        self.inc(gp.dma_start(out=self.activeT[:],
                              in_=_ap(self.active, 0, [[16, 1], [1, 16]]),
                              allow_slow_non_contiguous=True), "dW")
        gp.wait_ge(self.sem["dW"], self.cnt["dW"])
        self.wait(TE, "dW")
        mm = TE.matmul(self.psC[0:H, 0:1], self.u16_f[:], self.activeT[:],
                       start=True, stop=True)
        it = self.inc(mm, "T")
        self.wait(V, "T", it)
        last = None
        for a in range(NA):
            last = V.scalar_tensor_tensor(self.sel[:, a:a + 1], self.psC[0:H, 0:1],
                                          float(a), self.activeT[:],
                                          ALU.is_equal, ALU.mult)
        iv = self.inc(last, "V")
        V.drain()
        s_b = _ap(self.sel, 0, [[NA, H], [0, 8], [1, NA]])
        m_b = _ap(self.selmask_f, 0, [[32, H], [NA, 8], [1, NA]])
        o_b = _ap(self.selp, 0, [[32, H], [NA, 8], [1, NA]])
        iv = self.inc(V.tensor_tensor(o_b, s_b, m_b, ALU.mult), "V")
        self.wait(TE, "V", iv)
        mm = TE.matmul(self.psD[:, 0:32], self.lpar_f[:], self.selp[:],
                       start=True, stop=True)
        it = self.inc(mm, "T")
        self.wait(V, "T", it)
        last = None
        for c in range(8):
            src = _ap(self.psD, c * 4, [[512, 128], [1, NA], [0, 64]])
            i2b = _ap(self.i2_f, 0, [[64, 128], [0, NA], [1, 64]])
            dst = _ap(self.G, c * 256, [[8 * 256, 128], [64, NA], [1, 64]])
            last = V.tensor_tensor(dst, src, i2b, ALU.mult)
        self.inc(last, "V")
        self.wait(TE, "V")
        if "d_r" in self.dbg:
            self.wait(sy, "V")
            self.inc(sy.dma_start(out=self.dbg["d_r"].ap()[b][None, :],
                                  in_=self.r_row[:]), "dX")
        if "d_sel" in self.dbg:
            self.wait(sy, "V")
            self.inc(sy.dma_start(out=self.dbg["d_sel"].ap()[b], in_=self.sel[:]),
                     "dX")

    def _phase3(self, b):
        nc = self.nc
        sy, gp = nc.sync, nc.gpsimd
        TE, SC, V = nc.tensor, nc.scalar, nc.vector
        for g in range(self.NG):
            for dt_ in range(4):
                t = g * 4 + dt_

                def evac(gg, ps, dt_=dt_):
                    out = _ap(self.x1T, (4 * gg) * 512 + dt_ * 128,
                              [[8 * 512, 128], [512, 4], [1, 128]])
                    if gg % 2 == 0:
                        return ("Sc", self.inc(
                            SC.activation(out, ps[:], AF.Copy), "Sc"))
                    self.wait(V, "T", self.cnt["T"])
                    return ("V", self.inc(V.tensor_copy(out, ps[:]), "V"))

                self._tr8(self.x1[:, t, :], evac,
                          banks=[self.psA, self.psB, self.psG, self.psH])
            self.wait(TE, "Sc")
            self.wait(TE, "V")
            for ti, (wsb, bias, kind) in enumerate([
                    (self.wq2, self.bq2_fm, "q"),
                    (self.wk2, self.bk2_fm, "k"),
                    (self.wv2, self.bv2_fm, "v")]):
                self.wait(TE, "V")
                sname, scnt = self._w2_done[kind]
                self.wait(TE, sname, scnt)
                evacd = {}
                for hc in range(9):
                    if hc < 8:
                        ps = [self.psA, self.psB][hc % 2]
                        if hc >= 2:
                            self.wait(TE, "V", evacd[hc - 2])
                        mm = None
                        for c in range(8):
                            mm = TE.matmul(ps[:],
                                           wsb[:, c, hc * 128:(hc + 1) * 128],
                                           self.x1T[:, c, :],
                                           start=(c == 0), stop=(c == 7))
                        it = self.inc(mm, "T")
                        self.wait(V, "T", it)
                        evacd[hc] = self.inc(
                            V.tensor_scalar(self.qkv_full[:, hc, :], ps[:],
                                            bias[:, hc:hc + 1], None, ALU.add),
                            "V")
                    cg = hc - 1
                    if cg >= 0:
                        self.wait(TE, "V", evacd[cg])
                        if cg == 0:
                            self.wait(TE, "Sc", getattr(self, "_gps_free", 0))
                        for j in range(2):
                            psj = [self.psC, self.psD][j]
                            mm = TE.matmul(psj[:],
                                           self.G[:, cg, j * 128:(j + 1) * 128],
                                           self.qkv_full[:, cg, :],
                                           start=(cg == 0), stop=(cg == 7),
                                           skip_group_check=True)
                            if cg == 7:
                                self.inc(mm, "T")
                for j in range(2):
                    ps = [self.psC, self.psD][j]
                    it = self.cnt["T"]
                    if kind == "v":
                        self.wait(SC, "T", it)
                        isc = self.inc(SC.activation(self.vsfm[:], ps[:],
                                                     AF.Copy), "Sc")
                        self.wait(TE, "Sc", isc)
                        tr = None
                        for dt_ in range(4):
                            out_bf = self.psE[:, dt_ * 64:(dt_ + 1) * 64].bitcast(
                                BF16)
                            tr = TE.transpose(out_bf,
                                              self.vsfm[:, dt_ * 128:(dt_ + 1) * 128],
                                              self.id_bf[:])
                        it2 = self.inc(tr, "T")
                        self.wait(SC, "T", it2)
                        isc = None
                        for dt_ in range(4):
                            t = g * 4 + dt_
                            dst = _ap(self.Vs, t * NA * 65 + (2 * j) * 65,
                                      [[self.NT * NA * 65, 128], [65, 2], [1, 64]])
                            srcb = self.psE[:, dt_ * 64:(dt_ + 1) * 64].bitcast(
                                BF16)
                            src2 = bass.AP(tensor=srcb.tensor, offset=srcb.offset,
                                           ap=[list(srcb.ap[0]), [64, 2], [1, 64]])
                            isc = self.inc(SC.activation(dst, src2, AF.Copy),
                                           "Sc")
                        self.wait(TE, "Sc", isc)
                        self._gps_free = self.cnt["Sc"]
                    else:
                        dsts = self.Qs if kind == "q" else self.Ks
                        scale = 0.125 if kind == "q" else 1.0
                        self.wait(SC, "T", it)
                        self.inc(SC.activation(
                            dsts[2 * j][0:64, g * 512:(g + 1) * 512],
                            ps[0:64, :], AF.Copy, scale=scale), "Sc")
                        self.wait(SC, "dW", getattr(self, "_shift_idm", 0))
                        isc = self.inc(SC.activation(
                            self.shiftb[64:128, :], ps[64:128, :], AF.Copy,
                            scale=scale), "Sc")
                        self.wait(gp, "Sc", isc)
                        idm = self.inc(gp.dma_start(
                            out=dsts[2 * j + 1][0:64, g * 512:(g + 1) * 512],
                            in_=self.shiftb[64:128, :]), "dW")
                        gp.wait_ge(self.sem["dW"], idm)
                        self._shift_idm = idm
                        self._gps_free = self.cnt["Sc"]

    def _bounds(self, b):
        nc = self.nc
        sy = nc.sync
        TE, SC, V = nc.tensor, nc.scalar, nc.vector
        S = self.S
        nch = S // 512
        V.memset(self.sqb[0][64:65, :], 1.0)
        V.memset(self.sqb[1][64:65, :], 1.0)
        units = []
        for a in range(NA):
            units.append((a, "k", self.Ks[a]))
            units.append((a, "q", self.Qs[a]))
        mm_done, ln_done, fin_done, lkm_done = {}, {}, {}, {}
        for u, (a, which, ten) in enumerate(units):
            sq = self.sqb[u % 2]
            nac = self.nac2[u % 2]
            pb = [self.psA, self.psB] if u % 2 == 0 else [self.psC, self.psD]
            self.wait(V, "T", mm_done.get(u - 2, 0))
            iv = self.inc(V.tensor_tensor(sq[0:64, :], ten[0:64, :],
                                          ten[0:64, :], ALU.mult), "V")
            self.wait(TE, "V", iv)
            self.wait(TE, "Sc", ln_done.get(u - 2, 0))
            mm = None
            for ch in range(nch):
                bp = (ch // 2) * 32
                mm = TE.matmul(pb[ch % 2][bp:bp + 1, 0:512], self.ones65_bf[:],
                               sq[:, ch * 512:(ch + 1) * 512],
                               start=True, stop=True)
            mm_done[u] = self.inc(mm, "T")
            self.wait(SC, "T", mm_done[u])
            self.wait(SC, "V", fin_done.get(u - 2, 0))
            last = None
            for ch in range(nch):
                bp = (ch // 2) * 32
                last = SC.activation(nac[0:1, ch * 512:(ch + 1) * 512],
                                     pb[ch % 2][bp:bp + 1, 0:512], AF.Ln)
            if which == "k":
                ln_done[u] = self.inc(last, "Sc")
                self.wait(V, "Sc", ln_done[u])
                V.reduce_max(self.kmax_row[0:1, a:a + 1], nac[:], AX.X)
                V.drain()
                lkm_done[a] = self.inc(V.tensor_scalar_mul(
                    self.kmax_row[0:1, a:a + 1],
                    self.kmax_row[0:1, a:a + 1], 0.5), "V")
            else:
                SC.drain()
                self.wait(SC, "V", lkm_done[a])
                ln_done[u] = self.inc(SC.activation(
                    ten[64:65, :], nac[:], AF.Exp, scale=0.5,
                    bias=self.kmax_row[0:1, a:a + 1]), "Sc")
                self.wait(V, "Sc", ln_done[u])
                V.memset(self.Ks[a][64:65, :], -1.0)
                self.inc(V.memset(self.vmax8[0:1, 1:2], 0.0), "V")
            fin_done[u] = self.cnt["V"]
        self.wait(TE, "V")
        self.wait(TE, "Sc")
        gp = self.nc.gpsimd
        if "d_qs" in self.dbg:
            self.wait(gp, "V")
            idm = None
            for a in range(NA):
                self.inc(gp.dma_start(out=self.dbg["d_qs"].ap()[b, a],
                                      in_=self.Qs[a][:]), "dW")
                idm = self.inc(gp.dma_start(out=self.dbg["d_ks"].ap()[b, a],
                                            in_=self.Ks[a][:]), "dW")
            gp.wait_ge(self.sem["dW"], idm)
        if "d_vs" in self.dbg:
            self.wait(gp, "V")
            idm = None
            for t in range(self.NT):
                idm = self.inc(gp.dma_start(
                    out=self.dbg["d_vs"].ap()[b, t * 128:(t + 1) * 128, :],
                    in_=self.Vs[:, t, :]), "dW")
            gp.wait_ge(self.sem["dW"], idm)

    def _phase45(self, b, last=False):
        nc = self.nc
        sy = nc.sync
        TE, SC, V = nc.tensor, nc.scalar, nc.vector
        S, NT = self.S, self.NT
        NQC = S // 1024
        dO_tile = {}
        self.wait(TE, "dW")
        for qc in range(min(NQC, self.p45_qc)):
            otn_done = {}
            for a in range(NA):
                exp_done = {}
                pv_done = {}
                psO_pair = [[self.psE, self.psF], [self.psG, self.psH]][a % 2]
                if a < 2:
                    self.wait(TE, "V")
                sbank = [self.psA, self.psB, self.psC, self.psD]
                for kt in range(NT + 1):
                    if kt < NT:
                        for hf in range(2):
                            ei = 2 * kt + hf
                            psS = sbank[ei % 4]
                            self.wait(TE, "Sc", exp_done.get(
                                ei - 4, getattr(self, "_rep_sc", 0)))
                            mm = TE.matmul(
                                psS[:],
                                self.Ks[a][:, kt * 128:(kt + 1) * 128],
                                self.Qs[a][:, qc * 1024 + hf * 512:
                                           qc * 1024 + (hf + 1) * 512],
                                start=True, stop=True)
                            it = self.inc(mm, "T")
                            self.wait(SC, "T", it)
                            self.wait(SC, "T", pv_done.get(kt - 2, 0))
                            isc = self.inc(SC.activation(
                                self.Pt[kt % 2][:, hf * 512:(hf + 1) * 512],
                                psS[:], AF.Exp), "Sc")
                            exp_done[ei] = isc
                    kp = kt - 1
                    if kp >= 0:
                        self.wait(TE, "Sc", exp_done[2 * kp + 1])
                        if kp == 0 and a >= 2:
                            self.wait(TE, "V", otn_done.get(a - 2, 0))
                        mm2 = None
                        for hf in range(2):
                            psO = psO_pair[hf]
                            mm2 = TE.matmul(
                                psO[0:65, :],
                                self.Vs[:, kp, a * 65:(a + 1) * 65],
                                self.Pt[kp % 2][:, hf * 512:(hf + 1) * 512],
                                start=(kp == 0), stop=(kp == NT - 1),
                                skip_group_check=True)
                        pv_done[kp] = self.inc(mm2, "T")
                self.wait(SC, "T")
                with nc.allow_low_precision(reason="f32r operand prep"):
                    for hf in range(2):
                        psO = psO_pair[hf]
                        SC.activation(self.nrm_f[0:1, hf * 512:(hf + 1) * 512],
                                      psO[64:65, :], AF.Ln)
                    SC.drain()
                    isc = self.inc(SC.activation(self.rec_row[:],
                                                 self.nrm_f[:], AF.Exp,
                                                 scale=-1.0), "Sc")
                self.wait(TE, "Sc", isc)
                psR_pair = [self.psC, self.psD]
                last_mm = None
                for hf in range(2):
                    last_mm = TE.matmul(psR_pair[hf][0:64, :], self.onesr64_r[:],
                                        self.rec_row[0:1, hf * 512:(hf + 1) * 512],
                                        start=True, stop=True)
                it = self.inc(last_mm, "T")
                self.wait(SC, "T", it)
                self.wait(SC, "V", otn_done.get(a - 1, 0))
                isc = None
                for hf in range(2):
                    isc = self.inc(SC.activation(
                        self.rep_sb[:, hf * 512:(hf + 1) * 512],
                        psR_pair[hf][0:64, :], AF.Copy), "Sc")
                self._rep_sc = isc
                self.wait(V, "Sc", isc)
                self.wait(V, "T", pv_done[NT - 1])
                last_v = None
                for hf in range(2):
                    psO = psO_pair[hf]
                    with nc.allow_low_precision(reason="f32r operand prep"):
                        last_v = V.tensor_tensor(
                            self.OTn[0:64, a, hf * 512:(hf + 1) * 512],
                            psO[0:64, :],
                            self.rep_sb[:, hf * 512:(hf + 1) * 512], ALU.mult)
                otn_done[a] = self.inc(last_v, "V")
                if "d_ot" in self.dbg:
                    gp = nc.gpsimd
                    self.wait(gp, "V")
                    idm = self.inc(gp.dma_start(
                        out=self.dbg["d_ot"].ap()[b, a, :,
                                                  qc * 1024:(qc + 1) * 1024],
                        in_=self.OTn[0:64, a, :]), "dW")
                    gp.wait_ge(self.sem["dW"], idm)
                    self.wait(V, "dW", idm)
            self.wait(TE, "V")

            if self.p45_level < 2:
                continue
            res_done = {}
            v_at_start = self.cnt["V"]
            for dt_ in range(8):
                t = qc * 8 + dt_
                opair = [[self.psA, self.psB], [self.psC, self.psD]][dt_ % 2]
                self.wait(TE, "V", res_done.get(dt_ - 2, v_at_start))
                for half in range(2):
                    ps = opair[half]
                    for a in range(NA):
                        TE.matmul(ps[:],
                                  self.OTn[0:64, a, dt_ * 128:(dt_ + 1) * 128],
                                  self.wo2r[:, a, half * 512:(half + 1) * 512],
                                  start=(a == 0), stop=False)
                    mm = TE.matmul(ps[:], self.ones2_bf[:],
                                   self.bo2_hl[:, half * 512:(half + 1) * 512],
                                   start=False, stop=True)
                    self.inc(mm, "T")
                so = "dO%d" % (t % 2)
                self.wait(V, so, dO_tile.get(t - 2, 0))
                self.wait(V, "T")
                for half in range(2):
                    psO = opair[half]
                    V.scalar_tensor_tensor(
                        self.x1[:, t, half * 512:(half + 1) * 512], psO[:], 0.0,
                        self.x1[:, t, half * 512:(half + 1) * 512],
                        ALU.add, ALU.add,
                        accum_out=self.sums[:, half:half + 1])
                self.inc(V.memset(self.sums[:, 8:9], 0.0), "V")
                V.drain()
                self._ln_finalize(self.x1[:, t, :], self.sums[:, 0:1],
                                  self.sums[:, 1:2], self.x1[:, t, :])
                res_done[dt_] = self.cnt["V"]
                self.wait(sy, "V")
                dO_tile[t] = self.inc(
                    sy.dma_start(out=self.out_h.ap()[b, t * 128:(t + 1) * 128, :],
                                 in_=self.x1[:, t, :]), so)
        if not last:
            self._wload_b1(first=False)


def _pack_inputs(inputs):
    g = lambda n: np.ascontiguousarray(np.asarray(inputs[n], np.float32))
    m = {}
    m["a_wq"] = g("L0_attn_wq").reshape(D, D).copy()
    m["a_wk"] = g("L0_attn_wk").reshape(D, D).copy()
    m["a_wv"] = g("L0_attn_wv").reshape(D, D).copy()
    m["a_wo"] = g("L0_attn_wo").reshape(D, D).copy()
    m["a_bq"] = g("L0_attn_bq").reshape(D).copy()
    m["a_bk"] = g("L0_attn_bk").reshape(D).copy()
    m["a_bv"] = g("L0_attn_bv").reshape(D).copy()
    m["a_bo"] = g("L0_attn_bo").reshape(D).copy()
    m["f_wq"] = g("L0_ffn_Wq")
    m["f_wk"] = g("L0_ffn_Wk")
    m["f_wv"] = g("L0_ffn_Wv")
    m["f_wo"] = g("L0_ffn_Wo")
    m["f_bq"] = g("L0_ffn_bq")
    m["f_bk"] = g("L0_ffn_bk")
    m["f_bv"] = g("L0_ffn_bv")
    m["f_bo"] = g("L0_ffn_bo")
    m["f_wr"] = g("L0_ffn_Wr")
    m["f_br"] = g("L0_ffn_br")
    return m


def kernel(**inputs):
    x = np.ascontiguousarray(np.asarray(inputs["L0_attn_x"], np.float32))
    B, S, _ = x.shape
    nb = B // N_CORES
    prog = Prog(S=S, NB=nb)
    wm = _pack_inputs(inputs)
    in_maps = []
    for core in range(N_CORES):
        im = dict(wm)
        im["x"] = np.ascontiguousarray(x[core * nb:(core + 1) * nb])
        in_maps.append(im)
    res = run_bass_kernel_spmd(prog.nc, in_maps, core_ids=list(range(N_CORES)))
    out = np.concatenate([r["out"] for r in res.results], axis=0)
    return out.astype(np.float32)
